# version 27
# baseline (speedup 1.0000x reference)
import numpy as np
from contextlib import ExitStack

import concourse.bass as bass
import concourse.mybir as mybir
from concourse.bass_utils import run_bass_kernel_spmd

F32 = mybir.dt.float32
BF16 = mybir.dt.bfloat16
I32 = mybir.dt.int32
AF = mybir.ActivationFunctionType
ALU = mybir.AluOpType
AX = mybir.AxisListType

ENGINES = ("tensor", "vector", "scalar", "gpsimd", "sync")

D = 1024
NCH = 8
H0 = 4
DQH = 128
DVH = 256
CH = 128
H1 = 16
HD = 64
I_FFN = 2816
NKI = 22
LN_EPS = 1e-5
NORM_EPS = 1e-5


class Op:
    __slots__ = ("idx", "engine", "fn", "reads", "writes", "dma_buf", "tag",
                 "deps", "needs_inc", "tick", "waits", "snap", "est", "lat")


REORDER = True


class Sched:
    def __init__(self):
        self.ops = []
        self.dma_bufs = []

    def op(self, engine, fn, reads=(), writes=(), dma=None, tag="", est=400.0,
           lat=0.0):
        o = Op()
        o.idx = len(self.ops)
        o.engine = engine
        o.fn = fn
        o.reads = tuple(reads)
        o.writes = tuple(writes)
        o.dma_buf = dma
        o.tag = tag
        o.est = est
        o.lat = lat
        self.ops.append(o)
        if dma is not None and dma not in self.dma_bufs:
            self.dma_bufs.append(dma)
        return o

    def _reorder(self):
        import heapq
        n = len(self.ops)
        succs = [[] for _ in range(n)]
        npred = [0] * n
        for o in self.ops:
            for d2 in o.deps:
                succs[d2].append(o.idx)
            npred[o.idx] = len(o.deps)
        cp = [0.0] * n
        for o in reversed(self.ops):
            i = o.idx
            m = 0.0
            for s2 in succs[i]:
                if cp[s2] > m:
                    m = cp[s2]
            cp[i] = m + o.est + o.lat
        avail = [0.0] * n
        done = [0.0] * n
        ready = {e: [] for e in ENGINES}
        for o in self.ops:
            if npred[o.idx] == 0:
                heapq.heappush(ready[o.engine], (-cp[o.idx], o.idx))
        free = {e: 0.0 for e in ENGINES}
        order = []
        remaining = n
        while remaining:
            best = None
            for e in ENGINES:
                h = ready[e]
                if not h:
                    continue
                tmp = []
                cand = None
                for _ in range(8):
                    if not h:
                        break
                    negcp, i = heapq.heappop(h)
                    st = free[e] if free[e] > avail[i] else avail[i]
                    k = (st, negcp, i)
                    if cand is None or k < cand:
                        cand = k
                    tmp.append((negcp, i))
                for it in tmp:
                    heapq.heappush(h, it)
                if best is None or cand[:2] < best[0][:2]:
                    best = (cand, e)
            (st, negcp, i), e = best
            ready[e].remove((negcp, i))
            heapq.heapify(ready[e])
            o = self.ops[i]
            fin = st + o.est
            free[e] = fin
            done[i] = fin + o.lat
            order.append(o)
            remaining -= 1
            for s2 in succs[i]:
                npred[s2] -= 1
                if avail[s2] < done[i]:
                    avail[s2] = done[i]
                if npred[s2] == 0:
                    heapq.heappush(ready[self.ops[s2].engine], (-cp[s2], s2))
        self.makespan = max(free.values())
        return order

    def solve(self):
        last_writer = {}
        readers = {}
        for o in self.ops:
            deps = set()
            for b in o.reads:
                w = last_writer.get(b)
                if w is not None:
                    deps.add(w)
            for b in o.writes:
                w = last_writer.get(b)
                if w is not None:
                    deps.add(w)
                deps.update(readers.get(b, ()))
            deps.discard(o.idx)
            o.deps = deps
            for b in o.reads:
                readers.setdefault(b, []).append(o.idx)
            for b in o.writes:
                last_writer[b] = o.idx
                readers[b] = []
        for o in self.ops:
            o.needs_inc = False
        for o in self.ops:
            for d in o.deps:
                self.ops[d].needs_inc = True
        self.makespan = 0.0
        self.order = self._reorder() if REORDER else list(self.ops)
        etick = {e: 0 for e in ENGINES}
        dcum = {}
        for o in self.order:
            if o.dma_buf is not None:
                o.needs_inc = True
                dcum[o.dma_buf] = dcum.get(o.dma_buf, 0) + 16
                o.tick = ("D", o.dma_buf, dcum[o.dma_buf])
            elif o.needs_inc:
                etick[o.engine] += 1
                o.tick = ("E", o.engine, etick[o.engine])
            else:
                o.tick = None
        known = {e: {} for e in ENGINES}
        for o in self.order:
            need = {}
            for d in o.deps:
                t = self.ops[d].tick
                key = (t[0], t[1])
                if need.get(key, (-1, None))[0] < t[2]:
                    need[key] = (t[2], d)
            k = known[o.engine]
            waits = []
            for key in sorted(need):
                v, d = need[key]
                if k.get(key, -1) >= v:
                    continue
                waits.append((key, v))
                for sk, sv in self.ops[d].snap.items():
                    if k.get(sk, -1) < sv:
                        k[sk] = sv
                if k.get(key, -1) < v:
                    k[key] = v
            o.waits = waits
            if o.tick is not None:
                snap = dict(k)
                snap[(o.tick[0], o.tick[1])] = o.tick[2]
                o.snap = snap
            else:
                o.snap = None
        n_waits = sum(len(o.waits) for o in self.ops)
        n_incs = sum(1 for o in self.ops if o.tick is not None)
        return {"ops": len(self.ops), "waits": n_waits, "incs": n_incs,
                "dma_bufs": len(self.dma_bufs),
                "makespan_us": round(self.makespan / 1000.0, 1)}

    def emit(self, nc, ctx):
        sems = {}
        for e in ENGINES:
            sems[("E", e)] = ctx.enter_context(nc.semaphore(f"se_{e}"))
        for b in self.dma_bufs:
            sems[("D", b)] = ctx.enter_context(nc.semaphore(f"sd_{b}"[:28]))
        block = ctx.enter_context(nc.Block())
        by_engine = {e: [] for e in ENGINES}
        for o in self.order:
            by_engine[o.engine].append(o)

        def run_engine(eng_ops):
            def body(eng):
                for o in eng_ops:
                    for (key, v) in o.waits:
                        eng.wait_ge(sems[key], v)
                    if o.fn is None:
                        continue
                    inst = o.fn(eng)
                    if o.tick is not None:
                        assert inst is not None, f"op {o.tag} needs inc but fn returned None"
                        if o.tick[0] == "D":
                            inst.then_inc(sems[("D", o.tick[1])], 16)
                        else:
                            inst.then_inc(sems[("E", o.engine)], 1)
            return body

        block.tensor(run_engine(by_engine["tensor"]))
        block.vector(run_engine(by_engine["vector"]))
        block.scalar(run_engine(by_engine["scalar"]))
        block.gpsimd(run_engine(by_engine["gpsimd"]))
        block.sync(run_engine(by_engine["sync"]))


def _fsz(ap, default=512):
    try:
        sh = ap.shape
        n = 1
        for d2 in sh[1:]:
            n *= int(d2)
        return n
    except Exception:
        return default


class Bld:

    def __init__(self, nc, S, T):
        self.nc = nc
        self.S = S
        self.T = T
        self.t = {}
        self.uid = 0

    def u(self):
        self.uid += 1
        return self.uid

    def mm(self, out, pairs, rd, wr, tag="mm", skip_group_check=False,
           accumulate=False, stop=True):
        def f(e, out=out, pairs=pairs, acc=accumulate, sgc=skip_group_check,
              stop=stop):
            inst = None
            n = len(pairs)
            for i, (l, r) in enumerate(pairs):
                inst = e.matmul(out, l, r, start=(i == 0 and not acc),
                                stop=(stop and i == n - 1),
                                skip_group_check=sgc)
            return inst
        est = 0.0
        for (l, r) in pairs:
            fs = _fsz(r)
            try:
                narrow = l.shape[0] < 100 or _fsz(l) < 100
            except Exception:
                narrow = False
            est += fs * (0.72 if narrow else 0.44) + 80.0
        return self.S.op("tensor", f, reads=rd, writes=wr, tag=tag, est=est)

    def tp(self, outs_ins, ident, rd, wr, tag="tp"):
        def f(e, oi=outs_ins, ident=ident):
            inst = None
            for o, i in oi:
                inst = e.transpose(o, i, ident)
            return inst
        est = sum(_fsz(i) * 0.42 + 80.0 for (_, i) in outs_ins)
        return self.S.op("tensor", f, reads=rd, writes=wr, tag=tag, est=est)

    def vop(self, name, args, rd, wr, tag="v", eng="vector", kw=None):
        def f(e, name=name, args=args, kw=kw or {}):
            return getattr(e, name)(*args, **kw)
        fs = 1
        for a in args:
            s = _fsz(a, 0) if not isinstance(a, (int, float, type(None),
                                                 mybir.AluOpType)) else 0
            if s > fs:
                fs = s
        est = 170.0 + fs * (0.85 if eng == "vector" else 1.3)
        return self.S.op(eng, f, reads=rd, writes=wr, tag=tag, est=est)

    def act(self, out, in_, func, rd, wr, bias=0.0, scale=1.0, tag="act"):
        def f(e, out=out, in_=in_, func=func, bias=bias, scale=scale):
            return e.activation(out=out, in_=in_, func=func, bias=bias,
                                scale=scale)
        est = 170.0 + _fsz(in_) * 1.0
        return self.S.op("scalar", f, reads=rd, writes=wr, tag=tag, est=est)

    def dma(self, eng, out, in_, dmabuf, rd, wr, tag="dma", transpose=False):
        def f(e, out=out, in_=in_):
            if transpose:
                return e.dma_start_transpose(out, in_)
            return e.dma_start(out=out, in_=in_)
        lat = 1300.0 + _fsz(out) * 128 * 2 * 0.01
        return self.S.op(eng, f, reads=rd, writes=wr, dma=dmabuf, tag=tag,
                         est=150.0, lat=lat)


def build(T=2048, phases=("l0a", "l0f", "l1a", "l1f"), nb=2):
    assert T % 512 == 0
    nc = bass.Bass("TRN2", target_bir_lowering=False, debug=False)
    S = Sched()
    bld = Bld(nc, S, T)
    TW = 512
    NTT = T // TW
    NT128 = T // 128
    NCHK = T // CH

    def din(name, shape, dtype=F32):
        return nc.dram_tensor(name, list(shape), dtype, kind="ExternalInput").ap()

    xin = din("xin", (nb, D, T))
    w_cw = din("cw", (128, 32))
    w_q0 = din("wq0", (D, 512))
    w_k0 = din("wk0", (D, 512))
    w_v0 = din("wv0", (D, D))
    w_b0 = din("wb0", (D, 4))
    w_g0 = din("wg0", (D, D))
    w_o0 = din("wo0", (D, D))
    w_nw = din("nw_t", (128, DVH))
    pk0 = din("pk0", (256, 1024), I32)
    wsc0 = din("wsc0", (1, 1))
    bias0 = din("bias0", (128, NCH))
    w_q1 = din("wq1", (D, D)); b_q1 = din("bq1", (128, NCH))
    w_k1 = din("wk1", (D, D)); b_k1 = din("bk1", (128, NCH))
    w_v1 = din("wv1", (D, D)); b_v1 = din("bv1", (128, NCH))
    w_o1 = din("wo1", (D, D)); b_o1 = din("bo1", (128, NCH))
    w_gt = din("wgt", (I_FFN, D))
    w_up = din("wup", (I_FFN, D))
    w_dn = din("wdn", (D, I_FFN))
    c_id = din("c_id", (128, 128), BF16)
    c_mneg_lo = din("c_mneg_lo", (128, 128), BF16)
    c_mneg_up = din("c_mneg_up", (128, 128), BF16)
    c_m_upincl = din("c_m_upincl", (128, 128), BF16)
    c_ones = din("c_ones", (128, 1), BF16)
    c_ones_row = din("c_ones_row", (1, 128), BF16)
    c_ones_mat = din("c_ones_mat", (128, 128), BF16)
    c_wsc_col = din("c_wsc_col", (128, 1))

    tok_major_out = "l1f" in phases
    out_shape = [nb, T, D] if tok_major_out else [nb, D, T]
    out = nc.dram_tensor("out", out_shape, F32, kind="ExternalOutput").ap()

    ds_tern = nc.dram_tensor("ds_tern", [D, D], BF16, kind="Internal").ap()
    ds_wg = nc.dram_tensor("ds_wg", [I_FFN, D], BF16, kind="Internal").ap()
    ds_wu = nc.dram_tensor("ds_wu", [I_FFN, D], BF16, kind="Internal").ap()
    ds_wd = nc.dram_tensor("ds_wd", [D, I_FFN], BF16, kind="Internal").ap()
    ds_x = nc.dram_tensor("ds_x", [D, T], BF16, kind="Internal").ap()

    ctx = ExitStack()
    E = ctx.enter_context

    class Buf:
        def __init__(self, name, cols, dtype=BF16, unit=128, parts=128):
            self.h = E(nc.sbuf_tensor(name, [parts, cols], dtype))
            self.name = name
            self.unit = unit
            self.cols = cols

        def __call__(self, lo, hi, p0=0, p1=None):
            if p1 is None:
                p1 = self.h[:].shape[0]
            ap = self.h[:][p0:p1, lo:hi]
            keys = [f"{self.name}{i}" for i in range(lo // self.unit,
                                                     (hi - 1) // self.unit + 1)]
            return ap, keys

        def all(self):
            return self(0, self.cols)

    A4C = max(NCH * T, NT128 * 8 * 130)
    XB = Buf("XB", NCH * T, unit=TW)
    A1 = Buf("A1", NCH * T, unit=TW)
    A2 = Buf("A2", NCH * T, unit=TW)
    A3 = Buf("A3", NCH * T, unit=TW)
    A4 = Buf("A4", A4C, unit=T)
    WS = Buf("WS", 8192, unit=1024)
    SB = Buf("SB", 3072)
    ABT = Buf("AB", 2048)
    SCR = Buf("SR", 1664)
    S0B = Buf("S0", 4 * DVH, unit=DVH)
    CST = Buf("CS", 512, unit=512)
    NWT = Buf("NW", DVH, unit=DVH)
    CWT = Buf("CW", 32, F32, unit=32)
    ON2 = Buf("O2", 128, unit=128)
    SRA = Buf("RA", 512, unit=512)
    SRB = Buf("RB", 512, unit=512)
    SRC = Buf("RC", 512)
    SRF = Buf("RF", 512, F32, unit=512)
    OST = Buf("OS", 512, F32, unit=256)
    BCL = Buf("BC", 66, F32, unit=1)
    QSC = Buf("QC", 64, F32, unit=32)
    ATB = Buf("AT", 2048, unit=1024)
    BIA = Buf("BI", 48, F32, unit=48)
    ECL = Buf("EC", 16, F32, unit=16)
    PKF = Buf("PK", 512, F32, unit=512)

    PSQ = E(nc.psum_tensor("PSQ", [128, 2048], F32))
    PS4 = E(nc.psum_tensor("PS4", [128, 512], F32))
    PS5 = E(nc.psum_tensor("PS5", [128, 512], F32))
    PSB = [PSQ[:][:, i * 512:(i + 1) * 512] for i in range(4)] + [PS4[:], PS5[:]]
    PT = E(nc.psum_tensor("PT", [128, 1024], BF16))
    PR = E(nc.psum_tensor("PR", [128, 512], F32))
    SPS = [PSB[2], PSB[3], PSB[4], PSB[5]]

    ident, k_id = CST(0, 128)
    m_neg_lo, k_mlo = CST(128, 256)
    m_neg_up, k_mup = CST(256, 384)
    m_upincl, k_mui = CST(384, 512)
    ones_c, k_on = ON2.h[:][:, 0:1], ["O20"]

    def ones_row(p):
        return ON2.h[:][p:p + 1, 0:128]

    def YT(lo, hi):
        return SB(2048 + lo, 2048 + hi)

    d = bld.dma
    v = bld.vop
    mm = bld.mm
    tp = bld.tp
    act = bld.act

    def op2(eng, name1, a1, name2, a2, rd_, wr_, tag_):
        def f(e, a1=a1, a2=a2):
            getattr(e, name1)(*a1)
            return getattr(e, name2)(*a2)
        return S.op(eng, f, reads=rd_, writes=wr_, tag=tag_)

    def mm2(o1, p1, o2, p2, rd_, wr_, tag_):
        def f(e, o1=o1, p1=p1, o2=o2, p2=p2):
            e.matmul(o1, p1[0], p1[1], start=True, stop=True)
            return e.matmul(o2, p2[0], p2[1], start=True, stop=True)
        est = (_fsz(p1[1]) + _fsz(p2[1])) * 0.5 + 160.0
        return S.op("tensor", f, reads=rd_, writes=wr_, tag=tag_, est=est)

    def fence(rd_, wr_, tag="fence"):
        def f(e):
            return e.engine_nop()
        return S.op("vector", f, reads=rd_, writes=wr_, tag=tag)

    d("sync", ident, c_id, "cst", rd=[], wr=k_id, tag="ldid")
    d("sync", m_neg_lo, c_mneg_lo, "cst", rd=[], wr=k_mlo)
    d("sync", m_neg_up, c_mneg_up, "cst", rd=[], wr=k_mup)
    d("sync", m_upincl, c_m_upincl, "cst", rd=[], wr=k_mui)
    d("sync", ON2.h[:], c_ones_mat, "on2", rd=[], wr=["O20"])
    d("gpsimd", NWT.all()[0], w_nw, "nwt", rd=[], wr=NWT.all()[1])
    d("sync", CWT.all()[0], w_cw, "cwt", rd=[], wr=CWT.all()[1])
    bia_ap, bia_k = BIA(0, 48)
    d("sync", BIA(0, 8)[0], b_q1, "bia", rd=[], wr=bia_k)
    d("sync", BIA(8, 16)[0], b_k1, "bia", rd=[], wr=bia_k)
    d("sync", BIA(16, 24)[0], b_v1, "bia", rd=[], wr=bia_k)
    d("sync", BIA(24, 32)[0], b_o1, "bia", rd=[], wr=bia_k)
    d("sync", BIA(32, 40)[0], bias0, "bia", rd=[], wr=bia_k)
    d("sync", BIA(40, 41)[0], c_wsc_col, "bia", rd=[], wr=bia_k, tag="wscb")

    def bcol(i):
        return BIA.h[:][:, i:i + 1]

    v("memset", (ECL.h[:][:, 0:1], LN_EPS), rd=[], wr=["EC0"], tag="eps")
    v("memset", (ECL.h[:][:, 1:2], 1e-30), rd=[], wr=["EC0"], tag="tiny")
    v("memset", (ECL.h[:][:, 2:3], 0.5), rd=[], wr=["EC0"], tag="half")
    eps_col = ECL.h[:][:, 0:1]
    tiny_col = ECL.h[:][:, 1:2]
    half_col = ECL.h[:][:, 2:3]
    GRID_MID = [0.25, 0.75, 1.25, 1.75, 2.5, 3.5, 5.0]
    for t_, m_ in enumerate(GRID_MID):
        v("memset", (ECL.h[:][:, 3 + t_:4 + t_], -m_), rd=[], wr=["EC0"],
          tag="negm")

    def negm_col(t_):
        return ECL.h[:][:, 3 + t_:4 + t_]

    def xap(c, lo, hi):
        return XB(c * T + lo, c * T + hi)

    pump_holder = [lambda n: None]

    def pump_cb(n):
        pump_holder[0](n)

    def ln_tile(tt, par, tag):
        t0 = tt * TW
        PTF = PT[:].bitcast(F32)
        xr, xk = [], []
        for c in range(NCH):
            a, k = xap(c, t0, t0 + TW)
            xr.append(a); xk.extend(k)
        mm(PTF[0:1, 0:TW], [(ones_c, a) for a in xr],
           rd=["O20"] + xk, wr=["PT"], tag=f"{tag}_sum{tt}")
        mrow = SRB.h[:][0:1, 0:TW]
        v("tensor_scalar", (mrow, PTF[0:1, 0:TW], 1.0 / D, None, ALU.mult),
          rd=["PT"], wr=["RBL"], tag=f"{tag}_m")
        for c in range(NCH):
            sl, slk = S0B((c % 2) * 512, (c % 2) * 512 + TW)
            a, k = xap(c, t0, t0 + TW)
            act(sl, a, AF.Square, rd=k, wr=slk, tag=f"{tag}_sq")
            mm(PTF[0:1, 0:TW], [(ones_c, sl)], rd=["O20"] + slk,
               wr=["PT"], accumulate=(c > 0), stop=(c == NCH - 1),
               tag=f"{tag}_ssmm")
        arow = SRA.h[:][0:1, 0:TW]
        v("tensor_tensor", (arow, mrow, mrow, ALU.mult),
          rd=["RBL"], wr=["RA0"], tag=f"{tag}_msq")
        v("scalar_tensor_tensor", (arow, PTF[0:1, 0:TW],
                                   1.0 / D, arow, ALU.mult,
                                   ALU.subtract),
          rd=["PT", "RA0"], wr=["RA0"], tag=f"{tag}_var")
        act(arow, arow, AF.Ln,
            rd=["RA0", "EC0"], wr=["RA0"], bias=ECL.h[:][0:1, 0:1],
            tag=f"{tag}_a1")
        act(arow, arow, AF.Exp, rd=["RA0"], wr=["RA0"], scale=-0.5,
            tag=f"{tag}_a2")
        brow = SRB.h[:][0:1, 0:TW]
        v("scalar_tensor_tensor", (brow, mrow, -1.0, arow, ALU.mult, ALU.mult),
          rd=["RBL", "RA0"], wr=["RBL"], tag=f"{tag}_b")
        aslot, ak = ABT(par * TW, par * TW + TW)
        bslot, bk = ABT(1024 + par * TW, 1024 + par * TW + TW)
        mm(PTF[:, 0:TW], [(ones_row(0), arow)],
           rd=["O20", "RA0"], wr=["PT"], tag=f"{tag}_bca")
        v("tensor_copy", (aslot, PTF[:, 0:TW]), rd=["PT"], wr=ak,
          tag=f"{tag}_eva")
        mm(PTF[:, 0:TW], [(ones_row(0), brow)],
           rd=["O20", "RBL"], wr=["PT"], tag=f"{tag}_bcb")
        v("tensor_copy", (bslot, PTF[:, 0:TW]), rd=["PT"], wr=bk,
          tag=f"{tag}_evb")

    def y_emit(c, lo, hi, outap, outkeys, par, tag):
        t0 = (lo // TW) * TW
        a_ap, ak = ABT(par * TW + (lo - t0), par * TW + (hi - t0))
        b_ap, bk = ABT(1024 + par * TW + (lo - t0), 1024 + par * TW + (hi - t0))
        xa, xk = xap(c, lo, hi)
        v("tensor_tensor", (outap, xa, a_ap, ALU.mult),
          rd=xk + ak, wr=outkeys, tag=f"{tag}_y1")
        v("tensor_tensor", (outap, outap, b_ap, ALU.add),
          rd=outkeys + bk, wr=outkeys, tag=f"{tag}_y2")

    def phase_l0a(b):
        tg = f"b{b}l0a"
        for tt in range(NTT):
            par = tt % 2
            ln_tile(tt, par, f"{tg}t{tt}")
            t0 = tt * TW
            for c in range(NCH):
                yt_m, yt_k = YT(3, 3 + TW)
                if tt == 0:
                    z3, _ = YT(0, 3)
                    v("memset", (z3, 0.0), rd=[], wr=yt_k, tag=f"{tg}_yz")
                else:
                    h3, _ = YT(0, 3)
                    y_emit(c, t0 - 3, t0, h3, yt_k, 1 - par, f"{tg}_yh")
                y_emit(c, t0, t0 + TW, yt_m, yt_k, par, f"{tg}_ym")
                ht, htk = SB((c % 2) * 512, (c % 2) * 512 + TW)
                ytf = SB.h[:][:, 2048:2568]
                v("tensor_scalar", (ht, ytf[:, 0:TW], CWT.h[:][:, 4 * c:4 * c + 1],
                                    None, ALU.mult),
                  rd=yt_k + ["CW0"], wr=htk, tag=f"{tg}_cv0")
                for i in (1, 2, 3):
                    v("scalar_tensor_tensor",
                      (ht, ytf[:, i:i + TW], CWT.h[:][:, 4 * c + i:4 * c + i + 1],
                       ht, ALU.mult, ALU.add),
                      rd=yt_k + ["CW0"] + htk, wr=htk, tag=f"{tg}_cv{i}")
                ha, hk = A1(c * T + t0, c * T + t0 + TW)
                act(ha, ht, AF.Silu, rd=htk, wr=hk, tag=f"{tg}_silu")

        def hap(k, lo, hi):
            return A1(k * T + lo, k * T + hi)


        def load_ws(dst_lo, wap, outw, key_tag):
            ap, keys = WS(dst_lo, dst_lo + NCH * outw)
            d("gpsimd", ap.rearrange("p (k n) -> p k n", k=NCH),
              wap.rearrange("(k p) n -> p k n", p=128), key_tag,
              rd=[], wr=keys, tag=f"ld_{key_tag}")
            return keys

        wq_k = load_ws(0, w_q0, 512, f"{tg}wq")
        wk_k = load_ws(4096, w_k0, 512, f"{tg}wk")

        def qk_unit(which, wlo, wkeys, dst_lo, hd, tt, pz):
            t0 = tt * TW
            pb, pk = (PSB[2 * pz], f"PS{2 * pz}")
            rowb, rowbk = (PSB[4], "PS4") if pz == 0 else (PSB[5], "PS5")
            pairs = []
            rds = list(wkeys)
            for k in range(NCH):
                wa, _ = WS(wlo + k * 512 + hd * 128, wlo + k * 512 + hd * 128 + 128)
                ha, hk2 = hap(k, t0, t0 + TW)
                pairs.append((wa, ha))
                rds.extend(hk2)
            mm(pb[:, 0:TW], pairs, rd=rds, wr=[pk], tag=f"{tg}_{which}mm")
            yield
            sq, sqk = SB(1024 + pz * 512, 1024 + pz * 512 + TW)
            act(sq, pb[:, 0:TW], AF.Square, rd=[pk], wr=sqk,
                tag=f"{tg}_{which}sq")
            yield
            mm(rowb[0:1, 0:TW], [(ones_c, sq)], rd=["O20"] + sqk,
               wr=[rowbk], tag=f"{tg}_{which}ss")
            yield
            rrow = SRC.h[:][64:65, 0:TW] if pz == 0 else SRB.h[:][64:65, 0:TW]
            rrk = ["RC0", "RC1", "RC2", "RC3"] if pz == 0 else ["RB0"]
            act(rrow, rowb[0:1, 0:TW], AF.Ln, bias=ECL.h[:][0:1, 1:2],
                rd=[rowbk, "EC0"], wr=rrk, tag=f"{tg}_{which}rn")
            yield
            act(rrow, rrow, AF.Exp, rd=rrk, wr=rrk, scale=-0.5,
                tag=f"{tg}_{which}rn2")
            yield
            v("tensor_scalar", (rrow, rrow, 1e12, None, ALU.min),
              rd=rrk, wr=rrk, tag=f"{tg}_{which}cl")
            yield
            mm(rowb[:, 0:TW], [(ones_row(64), rrow)], rd=["O20"] + rrk,
               wr=[rowbk], tag=f"{tg}_{which}bc")
            yield
            rnb, rnbk = SB(0, TW) if pz == 0 else SB(512, 512 + TW)
            v("tensor_copy", (rnb, rowb[:, 0:TW]), rd=[rowbk], wr=rnbk,
              tag=f"{tg}_{which}be")
            yield
            da, dk = A4(dst_lo + hd * T + t0, dst_lo + hd * T + t0 + TW)
            v("tensor_tensor", (da, pb[:, 0:TW], rnb, ALU.mult),
              rd=[pk] + rnbk, wr=dk, tag=f"{tg}_{which}ev")
            yield

        def qk_proj(which, wlo, wkeys, dst_lo):
            units = [(hd, tt) for hd in range(H0) for tt in range(NTT)]
            ui = 0
            while ui < len(units):
                pair = units[ui:ui + 2]
                gens = [qk_unit(which, wlo, wkeys, dst_lo, hh, tt_, pz)
                        for pz, (hh, tt_) in enumerate(pair)]
                while gens:
                    for g_ in list(gens):
                        try:
                            next(g_)
                        except StopIteration:
                            gens.remove(g_)
                ui += 2

        qk_proj("q", 0, wq_k, 0)
        pump_cb(12)
        qk_proj("k", 4096, wk_k, 4 * T)
        pump_cb(12)

        wb_sl, wb_k = SB(2048, 2048 + 32)
        d("gpsimd", wb_sl.rearrange("p (k n) -> p k n", k=NCH),
          w_b0.rearrange("(k p) n -> p k n", p=128), f"{tg}wb", rd=[], wr=wb_k,
          tag=f"{tg}_ldwb")
        for tt in range(NTT):
            t0 = tt * TW
            pairs = []
            rds = list(wb_k)
            for k in range(NCH):
                ha, hk2 = hap(k, t0, t0 + TW)
                pairs.append((wb_sl[:, k * 4:k * 4 + 4], ha))
                rds.extend(hk2)
            mm(PSB[0][0:4, 0:TW], pairs, rd=rds, wr=["PS0"], tag=f"{tg}_bmm")
            brows = SRC.h[:][0:4, 0:TW]
            brk = ["RC0", "RC1", "RC2", "RC3"]
            act(brows, PSB[0][0:4, 0:TW], AF.Sigmoid, rd=["PS0"], wr=brk,
                tag=f"{tg}_bsig")
            tp([(PT[:, j * 4:j * 4 + 4], SRC.h[:][0:4, j * 128:j * 128 + 128])
                for j in range(4)], ident[0:4, 0:4], rd=brk + k_id, wr=["PT"],
               tag=f"{tg}_btp")
            bc_ap, bc_k = BCL(tt * 16, tt * 16 + 16)
            v("tensor_copy", (bc_ap, PT[:, 0:16]), rd=["PT"], wr=bc_k,
              tag=f"{tg}_bev")

        def beta_col(hd, c):
            i = (c // 4) * 16 + (c % 4) * 4 + hd
            return BCL.h[:][:, i:i + 1], [f"BC{i}"]

        wv0_k = load_ws(0, w_v0[:, 0:512], 512, f"{tg}wv0")
        wv1_k = load_ws(4096, w_v0[:, 512:1024], 512, f"{tg}wv1")

        def v_unit(vc, tt, pz):
            wlo = 0 if vc < 4 else 4096
            wof = (vc % 4) * 128
            wkeys = wv0_k if vc < 4 else wv1_k
            hd, dvp = vc // 2, vc % 2
            t0 = tt * TW
            pb, pk = (PSB[2 * pz], f"PS{2 * pz}")
            pairs = []
            rds = list(wkeys)
            for k in range(NCH):
                wa, _ = WS(wlo + k * 512 + wof, wlo + k * 512 + wof + 128)
                ha, hk2 = hap(k, t0, t0 + TW)
                pairs.append((wa, ha))
                rds.extend(hk2)
            mm(pb[:, 0:TW], pairs, rd=rds, wr=[pk], tag=f"{tg}_vmm")
            yield
            vt, vtk = SB(pz * 512, pz * 512 + TW)
            v("copy", (vt, pb[:, 0:TW]), rd=[pk], wr=vtk, tag=f"{tg}_vev",
              eng="scalar")
            yield
            tp([(PT[:, j * 128:j * 128 + 128], vt[:, j * 128:j * 128 + 128])
                for j in range(4)], ident, rd=vtk + ["PT"] + k_id, wr=["PT"],
               tag=f"{tg}_vtp")
            for j in range(4):
                c = tt * 4 + j
                dlo = hd * (NCHK * DVH) + c * DVH + dvp * 128
                da, dk = A3(dlo, dlo + 128)
                bca, bck = beta_col(hd, c)
                v("tensor_scalar", (da, PT[:, j * 128:j * 128 + 128],
                                    bca, None, ALU.mult),
                  rd=["PT"] + bck, wr=dk, tag=f"{tg}_vbt")
            yield

        units_v = [(vc, tt) for vc in range(NCH) for tt in range(NTT)]
        ui = 0
        while ui < len(units_v):
            pairu = units_v[ui:ui + 2]
            gens = [v_unit(vv, tt_, pz) for pz, (vv, tt_) in enumerate(pairu)]
            while gens:
                for g_ in list(gens):
                    try:
                        next(g_)
                    except StopIteration:
                        gens.remove(g_)
            ui += 2

        wg0_k = load_ws(0, w_g0[:, 0:512], 512, f"{tg}wg0")
        wg1_k = load_ws(4096, w_g0[:, 512:1024], 512, f"{tg}wg1")

        def g_unit(j, half, pz):
            tj = j * 128
            wlo, wkeys = (0, wg0_k) if half == 0 else (4096, wg1_k)
            pb, pk = (PSB[2 * pz], f"PS{2 * pz}")
            pairs = []
            rds = list(wkeys)
            for k in range(NCH):
                ha, hk2 = hap(k, tj, tj + 128)
                wa, _ = WS(wlo + k * 512, wlo + k * 512 + 512)
                pairs.append((ha, wa))
                rds.extend(hk2)
            mm(pb[:, 0:TW], pairs, rd=rds, wr=[pk], tag=f"{tg}_gmm")
            yield
            da, dk = A2(j * 1024 + half * 512, j * 1024 + half * 512 + TW)
            v("copy", (da, pb[:, 0:TW]), rd=[pk], wr=dk, tag=f"{tg}_gev",
              eng="scalar")
            yield

        units_g = [(j, half) for j in range(NT128) for half in range(2)]
        ui = 0
        while ui < len(units_g):
            pairu = units_g[ui:ui + 2]
            gens = [g_unit(jj, hh_, pz) for pz, (jj, hh_) in enumerate(pairu)]
            while gens:
                for g_ in list(gens):
                    try:
                        next(g_)
                    except StopIteration:
                        gens.remove(g_)
            ui += 2

        wo0_k = load_ws(0, w_o0[:, 0:512], 512, f"{tg}wo0")
        wo1_k = load_ws(4096, w_o0[:, 512:1024], 512, f"{tg}wo1")

        def sl(hp, lo, w):
            return (SCR if hp == 0 else ATB)(lo, lo + w)

        nlev = 0
        while (1 << (nlev + 1)) < CH:
            nlev += 1

        def tbuild_gen(pair, hp, c):
            hd = pair * 2 + hp
            par = c % 2
            t0 = c * 128
            kof = (4 + hd) * T
            kkeyh = [f"A4{kof // T}"]
            kf = A4(kof + t0, kof + t0 + 128)[0]
            Tb = PSB[hp]
            tk = f"PS{hp}"
            ptr = 0 if hp == 0 else 512
            ptk = "PT"
            bca, bck = beta_col(hd, c)
            NP, NPk = sl(hp, par * 256, 256)
            N_ = NP[:, 0:128]
            Nt_ = NP[:, 128:256]
            TP, TPk = sl(hp, 512 + par * 256, 256)
            T_ = TP[:, 0:128]
            Tt_ = TP[:, 128:256]
            mm(Tb[:, 0:128], [(kf, kf)], rd=kkeyh, wr=[tk], tag=f"{tg}_Gmm")
            yield
            v("tensor_scalar", (N_, Tb[:, 0:128], bca, None, ALU.mult),
              rd=[tk] + bck, wr=NPk, tag=f"{tg}_Nb")
            yield
            v("tensor_tensor", (N_, N_, m_neg_lo, ALU.mult),
              rd=NPk + k_mlo, wr=NPk, tag=f"{tg}_Nm")
            yield
            tp([(PT[:, ptr:ptr + 128], N_)], ident,
               rd=NPk + ["PT"] + k_id, wr=["PT"], tag=f"{tg}_Nttp")
            v("copy", (Nt_, PT[:, ptr:ptr + 128]), rd=["PT"], wr=NPk,
              tag=f"{tg}_Ntev", eng="scalar")
            yield
            v("tensor_tensor", (TP, NP, ident[:, 0:128].broadcast_to([128, 256])
               if False else NP, ALU.bypass), rd=[], wr=[], tag="dead") \
                if False else None
            def f_ti(e, TP=TP, NP=NP):
                e.tensor_tensor(TP[:, 0:128], NP[:, 0:128], ident, ALU.add)
                return e.tensor_tensor(TP[:, 128:256], NP[:, 128:256], ident,
                                       ALU.add)
            S.op("vector", f_ti, reads=NPk + k_id, writes=TPk, tag=f"{tg}_Ti")
            yield
            xcur, xkcur = NP, NPk
            for j in range(1, nlev + 1):
                def f_sq(e, Tb=Tb, X=xcur):
                    e.matmul(Tb[:, 0:128], X[:, 128:256], X[:, 0:128],
                             start=True, stop=True, skip_group_check=True)
                    return e.matmul(Tb[:, 128:256], X[:, 0:128], X[:, 128:256],
                                    start=True, stop=True,
                                    skip_group_check=True)
                S.op("tensor", f_sq, reads=list(xkcur), writes=[tk],
                     tag=f"{tg}_sq{j}")
                yield
                v("copy", (NP, Tb[:, 0:256]), rd=[tk], wr=NPk,
                  tag=f"{tg}_sqe{j}", eng="scalar")
                yield
                def f_pr(e, Tb=Tb, X=xcur, TP=TP):
                    e.matmul(Tb[:, 0:128], X[:, 128:256], TP[:, 0:128],
                             start=True, stop=True, skip_group_check=True)
                    return e.matmul(Tb[:, 128:256], TP[:, 0:128], X[:, 128:256],
                                    start=True, stop=True,
                                    skip_group_check=True)
                S.op("tensor", f_pr, reads=list(xkcur) + list(TPk), writes=[tk],
                     tag=f"{tg}_pr{j}")
                yield
                v("tensor_tensor", (TP, Tb[:, 0:256], TP, ALU.add),
                  rd=[tk] + TPk, wr=TPk, tag=f"{tg}_pre{j}")
                yield

        def step_gen(pair, hp, c):
            hd = pair * 2 + hp
            par = c % 2
            t0 = c * 128
            qof, kof = hd * T, (4 + hd) * T
            qkeyh = [f"A4{qof // T}"]
            kkeyh = [f"A4{kof // T}"]
            kf = A4(kof + t0, kof + t0 + 128)[0]
            qf = A4(qof + t0, qof + t0 + 128)[0]
            Sb = PSB[2 + hp]
            sbk = f"PS{2 + hp}"
            Sps, Sk = (PSB[4], "PS4") if hp == 0 else (PSB[5], "PS5")
            TP, TPk = sl(hp, 512 + par * 256, 256)
            Tt_ = TP[:, 128:256]
            U_, Uk = sl(hp, 1024, 256)
            RP, RPk = sl(hp, 1280, 256)
            ktm, ktmk = sl(hp, 1536, 128)
            vlo = hd * (NCHK * DVH) + c * DVH
            vbt, vbk = A3(vlo, vlo + DVH)
            s0a, s0k = S0B(hd * DVH, (hd + 1) * DVH)
            bca, bck = beta_col(hd, c)
            ptr = 0 if hp == 0 else 512
            tp([(PT[:, ptr:ptr + 128], kf)], ident,
               rd=kkeyh + ["PT"] + k_id, wr=["PT"], tag=f"{tg}_ktp")
            v("copy", (ktm, PT[:, ptr:ptr + 128]), rd=["PT"], wr=list(ktmk),
              tag=f"{tg}_ktev", eng="scalar")
            yield
            if c == 0:
                Rap, Rkey = vbt, vbk
            else:
                mm(Sb[:, 0:DVH], [(kf, s0a)], rd=kkeyh + s0k, wr=[sbk],
                   tag=f"{tg}_kS")
                yield
                bn, bnk = BCL(64 + hp, 65 + hp)
                v("tensor_scalar", (bn, bca, -1.0, None, ALU.mult),
                  rd=bck, wr=bnk, tag=f"{tg}_bn")
                yield
                v("scalar_tensor_tensor", (RP[:, 0:DVH], Sb[:, 0:DVH], bn, vbt,
                                           ALU.mult, ALU.add),
                  rd=[sbk] + bnk + vbk, wr=RPk, tag=f"{tg}_R")
                yield
                Rap, Rkey = RP[:, 0:DVH], RPk
            mm(Sb[:, 0:DVH], [(Tt_, Rap)], rd=list(TPk) + list(Rkey), wr=[sbk],
               tag=f"{tg}_Umm")
            yield
            v("copy", (U_, Sb[:, 0:DVH]), rd=[sbk], wr=Uk,
              tag=f"{tg}_Uev", eng="scalar")
            yield
            Pm_ = RP[:, 0:128]
            mm(Sb[:, 0:128], [(kf, qf)], rd=kkeyh + qkeyh, wr=[sbk],
               tag=f"{tg}_Pmm")
            yield
            v("tensor_tensor", (Pm_, Sb[:, 0:128], m_upincl, ALU.mult),
              rd=[sbk] + k_mui, wr=RPk, tag=f"{tg}_Pev")
            yield
            if c == 0:
                pairs, rdo = [(Pm_, U_)], list(RPk) + list(Uk)
            else:
                pairs = [(qf, s0a), (Pm_, U_)]
                rdo = qkeyh + s0k + list(RPk) + list(Uk)
            mm(PR[:, 0:DVH], pairs, rd=rdo, wr=["PR"], tag=f"{tg}_omm")
            v("tensor_copy", (vbt, PR[:, 0:DVH]), rd=["PR"], wr=vbk,
              tag=f"{tg}_oev")
            yield
            mm(Sps[:, 0:DVH], [(ktm, U_)], rd=list(ktmk) + list(Uk),
               wr=[Sk], accumulate=(c > 0), stop=(c == NCHK - 1),
               skip_group_check=True, tag=f"{tg}_Smm")
            yield
            v("copy", (s0a, Sps[:, 0:DVH]), rd=[Sk], wr=s0k,
              tag=f"{tg}_Sev", eng="scalar")
            yield

        def head_gen(pair, hp):
            for c in range(NCHK + 1):
                if c < NCHK:
                    yield from tbuild_gen(pair, hp, c)
                if c >= 1:
                    yield from step_gen(pair, hp, c - 1)

        for pair in range(H0 // 2):
            gens = [head_gen(pair, 0), head_gen(pair, 1)]
            while gens:
                for g_ in list(gens):
                    try:
                        next(g_)
                    except StopIteration:
                        gens.remove(g_)

        for j in range(NT128):
            ga, gk = A2(j * 1024, (j + 1) * 1024)
            act(ga, ga, AF.Silu, rd=gk, wr=gk, tag=f"{tg}_esgb")
        units_e = [(hd, c) for hd in range(H0) for c in range(NCHK)]
        for gi in range(0, len(units_e), 8):
            grp = units_e[gi:gi + 8]
            tiles_ = []
            for ui, (hd, c) in enumerate(grp):
                vlo = hd * (NCHK * DVH) + c * DVH
                ot, ok = A3(vlo, vlo + DVH)
                sq, sqk = SB((ui % 4) * 512, (ui % 4) * 512 + DVH) if False \
                    else SB(ui * 256, (ui + 1) * 256)
                tiles_.append((hd, c, ot, ok, sq, sqk))
            for (hd, c, ot, ok, sq, sqk) in tiles_:
                act(sq, ot, AF.Square, rd=ok, wr=sqk, tag=f"{tg}_esq")
            for ui, (hd, c, ot, ok, sq, sqk) in enumerate(tiles_):
                col, colk = BCL(48 + 2 * (ui % 8), 49 + 2 * (ui % 8))
                v("reduce_sum", (col, sq.rearrange("p (a b) -> p a b", a=1)),
                  rd=sqk, wr=colk, tag=f"{tg}_erd", kw={"axis": AX.X})
            for ui, (hd, c, ot, ok, sq, sqk) in enumerate(tiles_):
                col, colk = BCL(48 + 2 * (ui % 8), 49 + 2 * (ui % 8))
                act(col, col, AF.Ln, rd=colk + ["EC0"], wr=colk,
                    bias=eps_col, scale=1.0 / DVH, tag=f"{tg}_ers")
            for ui, (hd, c, ot, ok, sq, sqk) in enumerate(tiles_):
                col, colk = BCL(48 + 2 * (ui % 8), 49 + 2 * (ui % 8))
                act(col, col, AF.Exp, rd=colk, wr=colk, scale=-0.5,
                    tag=f"{tg}_ers2")
            for ui, (hd, c, ot, ok, sq, sqk) in enumerate(tiles_):
                col, colk = BCL(48 + 2 * (ui % 8), 49 + 2 * (ui % 8))
                v("tensor_scalar", (ot, ot, col, None, ALU.mult),
                  rd=ok + colk, wr=ok, tag=f"{tg}_ez1")
            for (hd, c, ot, ok, sq, sqk) in tiles_:
                v("tensor_tensor", (ot, ot, NWT.all()[0], ALU.mult),
                  rd=ok + NWT.all()[1], wr=ok, tag=f"{tg}_ez2")
            for (hd, c, ot, ok, sq, sqk) in tiles_:
                ga, gk = A2(c * 1024 + hd * DVH, c * 1024 + (hd + 1) * DVH)
                v("tensor_tensor", (ot, ot, ga, ALU.mult),
                  rd=ok + gk, wr=ok, tag=f"{tg}_ez3")
        for c in range(NCHK):
            tiles, rds = [], list(k_id)
            for hd in range(H0):
                for dvp in range(2):
                    slo = hd * (NCHK * DVH) + c * DVH + dvp * 128
                    sap, sk = A3(slo, slo + 128)
                    tiles.append((PT[:, (hd * 2 + dvp) * 128:(hd * 2 + dvp) * 128 + 128], sap))
                    rds.extend(sk)
            tp(tiles, ident, rd=rds, wr=["PT"], tag=f"{tg}_ztp")
            zfk = []
            for vv in range(NCH):
                zfk += A1(vv * T + c * 128, vv * T + (c + 1) * 128)[1]
            v("tensor_copy",
              (A1.h[:].rearrange("p (vv t) -> p vv t", vv=NCH)[:, :, c * 128:(c + 1) * 128],
               PT[:].rearrange("p (vv t) -> p vv t", vv=NCH)),
              rd=["PT"], wr=zfk, tag=f"{tg}_zev")
        for tt in range(NTT):
            t0 = tt * TW
            for oc in range(NCH):
                wlo = 0 if oc < 4 else 4096
                wof = (oc % 4) * 128
                wkeys = wo0_k if oc < 4 else wo1_k
                pi = (oc * NTT + tt) % 2
                pb, pk = PSB[pi], f"PS{pi}"
                pairs = []
                rds = list(wkeys)
                for k in range(NCH):
                    wa, _ = WS(wlo + k * 512 + wof, wlo + k * 512 + wof + 128)
                    za, zk2 = A1(k * T + t0, k * T + t0 + TW)
                    pairs.append((wa, za))
                    rds.extend(zk2)
                mm(pb[:, 0:TW], pairs, rd=rds, wr=[pk],
                   tag=f"{tg}_womm")
                xa, xk = xap(oc, t0, t0 + TW)
                v("tensor_tensor", (xa, pb[:, 0:TW], xa, ALU.add),
                  rd=[pk] + xk, wr=xk, tag=f"{tg}_wores")

    def phase_l0f(b):
        tg = f"b{b}l0f"
        if b == 0:
            for rc in range(2):
              for hf in range(2):
                pka, pkk = PKF(0, 512)
                d("gpsimd", pka, pk0[rc * 128:(rc + 1) * 128, hf * 512:hf * 512 + 512],
                  "PKF", rd=[], wr=pkk, tag=f"{tg}_ldpk")
                for s in (3, 2, 1, 0):
                    base = float(4 ** s)
                    dig, digk = SB(0, 512)
                    if s > 0:
                        v("tensor_scalar", (dig, pka, base, None, ALU.is_ge),
                          rd=pkk, wr=digk, tag=f"{tg}_d1")
                        v("scalar_tensor_tensor", (dig, pka, 2 * base, dig,
                                                   ALU.is_ge, ALU.add),
                          rd=pkk + digk, wr=digk, tag=f"{tg}_d2")
                        v("scalar_tensor_tensor", (dig, pka, 3 * base, dig,
                                                   ALU.is_ge, ALU.add),
                          rd=pkk + digk, wr=digk, tag=f"{tg}_d3")
                        v("scalar_tensor_tensor", (pka, dig, -base, pka,
                                                   ALU.mult, ALU.add),
                          rd=pkk + digk, wr=pkk, tag=f"{tg}_dr")
                    else:
                        v("tensor_copy", (dig, pka), rd=pkk, wr=digk,
                          tag=f"{tg}_d0")
                    tb, tbk = SB(512, 1024)
                    v("tensor_scalar", (tb, dig, 1.0, None, ALU.subtract),
                      rd=digk, wr=tbk, tag=f"{tg}_dm1")
                    d("gpsimd", ds_tern[s * 256 + rc * 128:s * 256 + rc * 128 + 128,
                                        hf * 512:hf * 512 + 512],
                      tb, "DST", rd=tbk, wr=["DSTERN"], tag=f"{tg}_stw")
        for tt in range(NTT):
            par = tt % 2
            ln_tile(tt, par, f"{tg}t{tt}")
            t0 = tt * TW
            for c in range(NCH):
                ya, yk = A1(c * T + t0, c * T + t0 + TW)
                y_emit(c, t0, t0 + TW, ya, yk, par, f"{tg}_y")
            for jj in range(4):
                j = tt * 4 + jj
                tjl = j * 128
                ytk = []
                for k in range(NCH):
                    ytk += A1(k * T + tjl, k * T + tjl + 128)[1]
                tp([(PT[:, k * 128:k * 128 + 128],
                     A1(k * T + tjl, k * T + tjl + 128)[0]) for k in range(NCH)],
                   ident, rd=ytk + k_id, wr=["PT"], tag=f"{tg}_ytp")
                amax, amk = BCL(50, 51)
                v("reduce_max", (amax, PT[:].rearrange("p (a b) -> p a b", a=1)),
                  rd=["PT"], wr=amk, tag=f"{tg}_amax",
                  kw={"axis": AX.X, "apply_absolute_value": True})
                mx, mxk = BCL(51, 52)
                v("tensor_scalar", (mx, amax, 1e-5, None, ALU.max),
                  rd=amk, wr=mxk, tag=f"{tg}_mx")
                rcc, rck = BCL(52, 53)
                v("reciprocal", (rcc, mx), rd=mxk, wr=rck, tag=f"{tg}_rc")
                asc, asck = BCL(53, 54)
                v("tensor_scalar", (asc, rcc, 127.0, None, ALU.mult),
                  rd=rck, wr=asck, tag=f"{tg}_as")
                ra, rak = BCL(54, 55)
                v("tensor_scalar", (ra, mx, 1.0 / 127.0, bcol(40), ALU.mult,
                                    ALU.mult),
                  rd=mxk + bia_k, wr=rak, tag=f"{tg}_ra")
                cb, cbk = SB(2080, 2081)
                v("tensor_copy", (cb, asc), rd=asck, wr=cbk, tag=f"{tg}_cb")
                mm(PR[0:1, jj * 128:jj * 128 + 128], [(cb, ident)],
                   rd=cbk + k_id, wr=["PR"], tag=f"{tg}_ar")
                cb2, cb2k = SB(2081, 2082)
                v("tensor_copy", (cb2, ra), rd=rak, wr=cb2k, tag=f"{tg}_cb2")
                mm(PSB[0][0:1, jj * 128:jj * 128 + 128], [(cb2, ident)],
                   rd=cb2k + k_id, wr=["PS0"], tag=f"{tg}_rr")
            arow = SRC.h[:][64:65, 0:TW]
            ark = ["RC0", "RC1", "RC2", "RC3"]
            v("tensor_copy", (arow, PR[0:1, 0:TW]), rd=["PR"], wr=ark,
              tag=f"{tg}_arv")
            mm(PR[:, 0:TW], [(ones_row(64), arow)], rd=["O20"] + ark, wr=["PR"],
               tag=f"{tg}_abc")
            ab, abk = SB(1024, 1024 + TW)
            v("tensor_copy", (ab, PR[:, 0:TW]), rd=["PR"], wr=abk,
              tag=f"{tg}_abv")
            rrow = SRC.h[:][0:1, 0:TW]
            v("tensor_copy", (rrow, PSB[0][0:1, 0:TW]), rd=["PS0"], wr=ark,
              tag=f"{tg}_rrv")
            mm(PR[:, 0:TW], [(ones_row(0), rrow)], rd=["O20"] + ark, wr=["PR"],
               tag=f"{tg}_rbc")
            rb, rbk = ATB(tt * TW, tt * TW + TW)
            v("tensor_copy", (rb, PR[:, 0:TW]), rd=["PR"], wr=rbk,
              tag=f"{tg}_rbv")
            A4F = A4.h[:].bitcast(F32)
            for c in range(NCH):
                ya, yk = A1(c * T + t0, c * T + t0 + TW)
                if c % 2 == 0:
                    vt, vtk = OST(0, TW)
                    rr, rrk = PKF(0, TW)
                    sbs, sbk = SB(0, TW)
                else:
                    f2 = A4C // 2
                    _, a4k = A4(A4C - 2048, A4C)
                    vt, vtk = A4F[:, f2 - 1024:f2 - 512], a4k
                    rr, rrk = A4F[:, f2 - 512:f2], a4k
                    sbs, sbk = SB(512, 512 + TW)
                v("tensor_tensor", (vt, ya, ab, ALU.mult),
                  rd=yk + abk, wr=vtk, tag=f"{tg}_qv")
                src, srck = vt, vtk
                for kb in (64.0, 32.0, 16.0, 8.0, 4.0, 2.0, 1.0):
                    act(sbs, src, AF.Sign, rd=srck + ["EC0"], wr=sbk,
                        bias=half_col, tag=f"{tg}_qsg")
                    v("scalar_tensor_tensor", (rr, sbs, -kb, src,
                                               ALU.mult, ALU.add),
                      rd=sbk + srck, wr=rrk, tag=f"{tg}_qst")
                    src, srck = rr, rrk
                v("tensor_scalar", (sbs, rr, -0.5, -1.0, ALU.is_ge, ALU.add),
                  rd=rrk, wr=sbk, tag=f"{tg}_qb0")
                v("scalar_tensor_tensor", (rr, sbs, -1.0, rr, ALU.mult,
                                           ALU.add),
                  rd=sbk + rrk, wr=rrk, tag=f"{tg}_qr0")
                xia, xik = A2(c * T + t0, c * T + t0 + TW)
                v("scalar_tensor_tensor", (xia, rr, -1.0, vt, ALU.mult,
                                           ALU.add),
                  rd=rrk + vtk, wr=xik, tag=f"{tg}_qm")
        wtk = []
        for k in range(NCH):
            wa, wk2 = WS(k * 1024, (k + 1) * 1024)
            d("sync", wa, ds_tern[:, k * 128:(k + 1) * 128], f"WT{k}",
              rd=["DSTERN"], wr=wk2, tag=f"{tg}_ldt", transpose=True)
            wtk.extend(wk2)
        rakeys = [f"AT{i}" for i in range((NTT * TW) // 1024 + 1)]
        for tt in range(NTT):
            t0 = tt * TW
            for oc in range(NCH):
                pi = (oc * NTT + tt) % 2
                pb, pk = PSB[pi], f"PS{pi}"
                pairs = []
                rds = list(wtk)
                for k in range(NCH):
                    wa, _ = WS(k * 1024 + oc * 128, k * 1024 + oc * 128 + 128)
                    xa2, xk2 = A2(k * T + t0, k * T + t0 + TW)
                    pairs.append((wa, xa2))
                    rds.extend(xk2)
                mm(pb[:, 0:TW], pairs, rd=rds, wr=[pk],
                   tag=f"{tg}_tmm")
                st, stk = SB(1536, 1536 + TW)
                rb, rbk = ATB(t0, t0 + TW)
                v("tensor_tensor", (st, pb[:, 0:TW], rb, ALU.mult),
                  rd=[pk] + rbk, wr=stk, tag=f"{tg}_fra")
                v("tensor_scalar", (st, st, bcol(32 + oc), None, ALU.add),
                  rd=stk + bia_k, wr=stk, tag=f"{tg}_fb")
                xa, xk = xap(oc, t0, t0 + TW)
                v("tensor_tensor", (xa, st, xa, ALU.add),
                  rd=stk + xk, wr=xk, tag=f"{tg}_fres")

    GRID_D = [0.25, 0.75, 1.25, 1.75, 2.5, 3.5, 5.0]

    def q4_core(fin, fink, out_ap, out_k, w, tag, avslot, slots=None,
                gp=False):
        assert w <= 512 and w % 16 == 0
        nb16 = w // 16
        av, avk = avslot
        act(av, fin, AF.Abs, rd=fink, wr=avk, tag=f"{tag}_abs")
        if slots is None:
            accp, indp = SCR(0, w), SCR(512, 512 + w)
            scp, thrp = BCL(0, nb16), BCL(32, 32 + nb16)
        else:
            accp, indp, scp, thrp = slots
        sc, sck = scp
        v("reduce_max", (sc, av.rearrange("p (a b) -> p a b", b=16)),
          rd=avk, wr=sck, tag=f"{tag}_bm", kw={"axis": AX.X})
        v("tensor_scalar", (sc, sc, 1.0 / 6.0, 1e-12, ALU.mult, ALU.max),
          rd=sck, wr=sck, tag=f"{tag}_sc")
        av3 = av.rearrange("p (a b) -> p a b", b=16)
        acc, acck = accp
        acc3 = acc.rearrange("p (a b) -> p a b", b=16)
        thr, thrk = thrp
        ind, indk = indp
        ind3 = ind.rearrange("p (a b) -> p a b", b=16)
        g2a, g2ak = SCR(1024, 1024 + w) if slots is None else indp
        g2a3 = g2a.rearrange("p (a b) -> p a b", b=16)

        rcp, rcpk = thrp if slots is not None else QSC(32, 32 + w // 16)
        v("reciprocal", (rcp, sc), rd=sck, wr=rcpk, tag=f"{tag}_rcp")
        if gp:
            v("tensor_scalar", (rcp, rcp, 2.0, None, ALU.mult),
              rd=rcpk, wr=rcpk, tag=f"{tag}_r2")
            sch, schk = QSC(0, w // 16)
            v("tensor_scalar", (sch, sc, 0.5, None, ALU.mult),
              rd=sck, wr=schk, tag=f"{tag}_sh")
            sc, sck = sch, schk
        v("tensor_tensor", (av3, av3,
                            rcp[:, :, None].broadcast_to([128, w // 16, 16]),
                            ALU.mult),
          rd=avk + rcpk, wr=avk, tag=f"{tag}_un")

        if gp:
            v("tensor_scalar", (acc, av, 2 * GRID_D[0], None, ALU.is_gt),
              rd=avk, wr=acck, tag=f"{tag}_g0", eng="gpsimd")
            for t in (1, 2, 3):
                v("tensor_scalar", (g2a, av, 2 * GRID_D[t], None, ALU.is_gt),
                  rd=avk, wr=g2ak, tag=f"{tag}_g{t}", eng="gpsimd")
                v("tensor_tensor", (acc, acc, g2a, ALU.add),
                  rd=acck + g2ak, wr=acck, tag=f"{tag}_ga{t}", eng="gpsimd")
            for t in (4, 5):
                v("tensor_scalar", (g2a, av, 2 * GRID_D[t], None, ALU.is_gt),
                  rd=avk, wr=g2ak, tag=f"{tag}_g{t}", eng="gpsimd")
                for rpt in range(2):
                    v("tensor_tensor", (acc, acc, g2a, ALU.add),
                      rd=acck + g2ak, wr=acck, tag=f"{tag}_ga{t}{rpt}",
                      eng="gpsimd")
            v("tensor_scalar", (g2a, av, 2 * GRID_D[6], None, ALU.is_gt),
              rd=avk, wr=g2ak, tag=f"{tag}_g6", eng="gpsimd")
            v("tensor_tensor", (g2a, g2a, g2a, ALU.add),
              rd=g2ak, wr=g2ak, tag=f"{tag}_gd6", eng="gpsimd")
            for rpt in range(2):
                v("tensor_tensor", (acc, acc, g2a, ALU.add),
                  rd=acck + g2ak, wr=acck, tag=f"{tag}_ga6{rpt}",
                  eng="gpsimd")
        else:
            HW = (0.25, 0.25, 0.25, 0.25, 0.5, 0.5, 1.0)
            for t in range(7):
                st_, stk_ = (g2a, g2ak) if t % 2 == 0 else (ind, indk)
                act(st_, av, AF.Sign, rd=avk + ["EC0"], wr=stk_,
                    bias=negm_col(t), tag=f"{tag}_s{t}")
                if t == 0:
                    v("tensor_scalar", (acc, st_, HW[0], 3.0, ALU.mult,
                                        ALU.add),
                      rd=stk_, wr=acck, tag=f"{tag}_c0")
                else:
                    v("scalar_tensor_tensor", (acc, st_, HW[t], acc,
                                               ALU.mult, ALU.add),
                      rd=stk_ + acck, wr=acck, tag=f"{tag}_c{t}")
        sg, sgk = (ind, indk)
        act(sg, fin, AF.Sign, rd=fink, wr=sgk, tag=f"{tag}_sg")
        v("tensor_tensor", (acc, acc, sg, ALU.mult),
          rd=acck + sgk, wr=acck, tag=f"{tag}_ms")
        scb = sc[:, :, None].broadcast_to([128, nb16, 16])
        v("tensor_tensor", (out_ap.rearrange("p (a b) -> p a b", b=16), acc3, scb,
                            ALU.mult),
          rd=acck + sck, wr=out_k, tag=f"{tag}_msc")

    def q4_free(fin, fink, out_ap, out_k, w, tag, slots=None):
        q4_core(fin, fink, out_ap, out_k, w, tag, OST(0, w), slots=slots)

    def q4_part(fin, fink, out_ap, out_k, w, tag, gp=False):
        zt, ztk = PKF(0, w)
        v("transpose", (zt, fin), rd=fink, wr=ztk, tag=f"{tag}_zi")
        zq = PKF.h[:].bitcast(BF16)[:, 0:w]
        zqk = ztk
        q4_core(zt, ztk, zq, zqk, w, f"{tag}z", OST(0, w), gp=gp)
        v("transpose", (out_ap, zq), rd=zqk, wr=out_k, tag=f"{tag}_zo")

    def phase_l1a(b, after_qt=None):
        tg = f"b{b}l1a"
        for tt in range(NTT):
            par = tt % 2
            ln_tile(tt, par, f"{tg}t{tt}")
            t0 = tt * TW
            for c in range(NCH):
                ya, yk = A1(c * T + t0, c * T + t0 + TW)
                y_emit(c, t0, t0 + TW, ya, yk, par, f"{tg}_y")

        def load_w1(wap, key_tag):
            ap, keys = WS(0, 8192)
            d("gpsimd", ap.rearrange("p (k n) -> p k n", k=NCH),
              wap.rearrange("(k p) n -> p k n", p=128), key_tag,
              rd=[], wr=keys, tag=f"ld_{key_tag}")
            return keys

        fence([], A4.all()[1] + [f"VT{kt}" for kt in range(NT128)],
              f"{tg}_fv")

        def proj(dst, dkeyf, bias_base, wkeys, tag2, to_vt=False):
            for tt in range(NTT):
                t0 = tt * TW
                for oc in range(NCH):
                    pi = 2 + (oc * NTT + tt) % 2
                    pb, pk = PSB[pi], f"PS{pi}"
                    pairs = []
                    rds = list(wkeys)
                    for k in range(NCH):
                        wa, _ = WS(k * 1024 + oc * 128, k * 1024 + oc * 128 + 128)
                        ya, yk2 = A1(k * T + t0, k * T + t0 + TW)
                        pairs.append((wa, ya))
                        rds.extend(yk2)
                    mm(pb[:, 0:TW], pairs, rd=rds, wr=[pk], tag=f"{tg}_{tag2}mm")
                    if not to_vt:
                        da, dk = dst(oc * T + t0, oc * T + t0 + TW)
                        v("tensor_scalar", (da, pb[:, 0:TW], bcol(bias_base + oc),
                                            None, ALU.add),
                          rd=[pk] + bia_k, wr=dk, tag=f"{tg}_{tag2}ev")
                    else:
                        vt, vtk = SB(1536, 1536 + TW)
                        v("tensor_scalar", (vt, pb[:, 0:TW], bcol(bias_base + oc),
                                            None, ALU.add),
                          rd=[pk] + bia_k, wr=vtk, tag=f"{tg}_{tag2}ev")
                        tp([(PT[:, j * 128:j * 128 + 128], vt[:, j * 128:j * 128 + 128])
                            for j in range(4)], ident, rd=vtk + k_id, wr=["PT"],
                           tag=f"{tg}_{tag2}tp")
                        for j in range(4):
                            kt = tt * 4 + j
                            base = (kt * 8 + oc) * 130
                            ve_ap = A4.h[:][:, base:base + 64]
                            vo_ap = A4.h[:][:, base + 65:base + 129]
                            op2("vector", "tensor_copy", (ve_ap, PT[:, j * 128:j * 128 + 64]),
                                "tensor_copy", (vo_ap, PT[:, j * 128 + 64:j * 128 + 128]),
                                ["PT"], [f"VT{kt}"], f"{tg}_{tag2}vt")

        wk1 = load_w1(w_q1, f"{tg}wq")
        proj(A2, None, 0, wk1, "q")
        wk2 = load_w1(w_k1, f"{tg}wk")
        proj(A3, None, 8, wk2, "k")
        wk3 = load_w1(w_v1, f"{tg}wv")
        proj(None, None, 16, wk3, "v", to_vt=True)
        vt_ones = A4.h[:].rearrange("p (t w) -> p t w", w=130)
        op2("vector", "memset", (vt_ones[:, 0:NT128 * 8, 64:65], 1.0),
            "memset", (vt_ones[:, 0:NT128 * 8, 129:130], 1.0),
            [], [f"VT{kt}" for kt in range(NT128)], f"{tg}_vones")
        wk4 = load_w1(w_o1, f"{tg}wo")

        def att_unit(h, qt, pz):
            ch = h // 2
            p0 = (h % 2) * 64
            q0 = qt * TW
            qb = 2 * pz
            avb, avk = (PSB[4], "PS4") if pz == 0 else (PSB[5], "PS5")
            for g in range(NT128 // 2):
                kc0, kc1 = 2 * g, 2 * g + 1
                qa = A2.h[:][p0:p0 + 64, ch * T + q0:ch * T + q0 + TW]
                qak = A2(ch * T + q0, ch * T + q0 + TW)[1]
                ka0 = A3.h[:][p0:p0 + 64,
                              ch * T + kc0 * 128:ch * T + kc0 * 128 + 128]
                ka1 = A3.h[:][p0:p0 + 64,
                              ch * T + kc1 * 128:ch * T + kc1 * 128 + 128]
                kak = A3(ch * T + kc0 * 128, ch * T + (kc1 + 1) * 128)[1]
                mm2(PSB[qb][:, 0:TW], (ka0, qa),
                    PSB[qb + 1][:, 0:TW], (ka1, qa),
                    kak + qak, [f"PS{qb}", f"PS{qb + 1}"], f"{tg}_lmm")
                yield
                at_ap, at_k = ATB(pz * 1024, pz * 1024 + 1024)
                act(at_ap, PSQ[:][:, qb * 512:qb * 512 + 1024], AF.Exp,
                    rd=[f"PS{qb}", f"PS{qb + 1}"], wr=at_k,
                    scale=0.125, tag=f"{tg}_exp")
                yield
                base0 = (kc0 * 8 + ch) * 130 + (0 if h % 2 == 0 else 65)
                base1 = (kc1 * 8 + ch) * 130 + (0 if h % 2 == 0 else 65)
                av_pairs = [
                    (A4.h[:][:, base0:base0 + 65],
                     ATB.h[:][:, pz * 1024:pz * 1024 + TW]),
                    (A4.h[:][:, base1:base1 + 65],
                     ATB.h[:][:, pz * 1024 + 512:pz * 1024 + 512 + TW])]
                mm(avb[0:65, 0:TW], av_pairs,
                   rd=[f"VT{kc0}", f"VT{kc1}"] + at_k, wr=[avk],
                   accumulate=(g > 0), stop=(g == NT128 // 2 - 1),
                   skip_group_check=True, tag=f"{tg}_avmm")
                yield
            f1 = (SRC.h[:][64:65, 0:TW] if pz == 0 else SRB.h[:][64:65, 0:TW])
            f1k = ["RC0", "RC1", "RC2", "RC3"] if pz == 0 else ["RB0"]
            act(f1, avb[64:65, 0:TW], AF.Ln, rd=[avk], wr=f1k,
                tag=f"{tg}_rc1")
            yield
            act(f1, f1, AF.Exp, rd=f1k, wr=f1k, scale=-1.0, tag=f"{tg}_rc2")
            yield
            fb, fbk = SB(1536, 1536 + TW) if pz == 0 else SB(1024, 1024 + TW)
            mm(PR[:, 0:TW], [(ON2.h[:][64:65, :], f1)],
               rd=["O20"] + f1k + ["PR"], wr=["PR"], tag=f"{tg}_fbc")
            v("tensor_copy", (fb, PR[:, 0:TW]), rd=["PR"], wr=fbk,
              tag=f"{tg}_fbe")
            yield
            dst = A2.h[:][p0:p0 + 64, ch * T + q0:ch * T + q0 + TW]
            if h % 2 == 0:
                v("tensor_tensor", (dst, avb[0:64, 0:TW], fb[0:64, :],
                                    ALU.mult),
                  rd=[avk] + fbk, wr=A2(ch * T + q0, ch * T + q0 + TW)[1],
                  tag=f"{tg}_poev")
                yield
            else:
                mv, mvk = SB(0, TW) if pz == 0 else SB(512, 512 + TW)
                v("tensor_tensor", (mv[0:64, :], avb[0:64, 0:TW],
                                    fb[0:64, :], ALU.mult),
                  rd=[avk] + fbk, wr=mvk, tag=f"{tg}_ponm")
                yield
                mm(PR[64:128, 0:TW], [(ident[0:64, 0:64], mv[0:64, :])],
                   rd=mvk + k_id + ["PR"], wr=["PR"], tag=f"{tg}_pomv")
                v("tensor_copy", (dst, PR[64:128, 0:TW]),
                  rd=["PR"], wr=A2(ch * T + q0, ch * T + q0 + TW)[1],
                  tag=f"{tg}_poev")
                yield

        def emit_wo(tt):
            t0 = tt * TW
            PTF = PT[:].bitcast(F32)
            for oc in range(NCH):
                pairs = []
                rds = list(wk4)
                for k in range(NCH):
                    wa, _ = WS(k * 1024 + oc * 128, k * 1024 + oc * 128 + 128)
                    pa2, pk2 = A2(k * T + t0, k * T + t0 + TW)
                    pairs.append((wa, pa2))
                    rds.extend(pk2)
                mm(PTF[:, 0:TW], pairs, rd=rds, wr=["PT"],
                   tag=f"{tg}_womm")
                st, stk = SB(2048, 2048 + TW)
                v("tensor_scalar", (st, PTF[:, 0:TW], bcol(24 + oc), None,
                                    ALU.add),
                  rd=["PT"] + bia_k, wr=stk, tag=f"{tg}_wob")
                xa, xk = xap(oc, t0, t0 + TW)
                v("tensor_tensor", (xa, st, xa, ALU.add),
                  rd=stk + xk, wr=xk, tag=f"{tg}_wores")

        units = [(h, qt) for qt in range(NTT) for h in range(H1)]
        ui = 0
        while ui < len(units):
            pump_cb(2)
            pair = units[ui:ui + 2]
            gens = [att_unit(hh, qq, pz) for pz, (hh, qq) in enumerate(pair)]
            while gens:
                for g_ in list(gens):
                    try:
                        next(g_)
                    except StopIteration:
                        gens.remove(g_)
            ui += 2
            if ui % H1 == 0:
                qt_done = units[ui - 1][1]
                emit_wo(qt_done)
                if after_qt is not None:
                    after_qt(qt_done)

    def seg_table():
        segs = []
        pools = [(XB, 0, NCH * T), (A1, 0, NCH * T), (A4, 0, A4C),
                 (A3, ((NKI * TW + T - 1) // T) * T, NCH * T)]
        pi = 0
        off = pools[0][1]
        for i in range(16):
            while off + I_FFN > pools[pi][2]:
                pi += 1
                off = pools[pi][1]
            segs.append((pools[pi][0], off))
            off += I_FFN
        return segs

    SEGS = seg_table() if "l1f" in phases else None

    def l1f_tile(b, tt):
        tg = f"b{b}l1f"
        par = tt % 2
        ln_tile(tt, par, f"{tg}t{tt}")
        t0 = tt * TW
        for c in range(NCH):
            ya, yk = SRF(0, TW)
            y_emit(c, t0, t0 + TW, ya, yk, par, f"{tg}_y")
            qa, qk = A2(c * T + t0, c * T + t0 + TW)
            q4_part(ya, yk, qa, qk, TW, f"{tg}_qy")

    def phase_l1f(b, tiles_emitted=False):
        tg = f"b{b}l1f"
        if not tiles_emitted:
            for tt in range(NTT):
                l1f_tile(b, tt)
        d("gpsimd", ds_x.rearrange("(c p) t -> p c t", p=128),
          XB.h[:].rearrange("p (c t) -> p c t", c=NCH),
          "DSX", rd=XB.all()[1], wr=["DSXM"], tag=f"{tg}_spx")
        fence([], A4.all()[1] + [f"VT{kt}" for kt in range(NT128)],
              f"{tg}_fs")
        segk = {}
        for kc in range(16):
            buf, lo = SEGS[kc]
            ap, ukeys = buf(lo, lo + I_FFN)
            dsrc = ds_wg if kc < 8 else ds_wu
            kk = kc % 8
            d("sync", ap, dsrc[:, kk * 128:(kk + 1) * 128], f"WG{kc}",
              rd=["DSGM" if kc < 8 else "DSUM"], wr=ukeys, tag=f"{tg}_ldg",
              transpose=True)
            segk[kc] = ukeys

        def wg_tile(kc, n):
            buf, lo = SEGS[kc]
            return buf.h[:][:, lo + n * 128:lo + n * 128 + 128], segk[kc]

        def wu_tile(kc, n):
            buf, lo = SEGS[8 + kc]
            return buf.h[:][:, lo + n * 128:lo + n * 128 + 128], segk[8 + kc]

        for tq in range(NTT):
            t0 = tq * TW
            for n in range(NKI):
                gp, gk2 = [], []
                up_, uk2 = [], []
                yqkt = []
                for kc in range(NCH):
                    wt, wk_ = wg_tile(kc, n)
                    yq_ap, yq_k = A2(kc * T + t0, kc * T + t0 + TW)
                    gp.append((wt, yq_ap)); gk2.extend(wk_)
                    yqkt.extend(yq_k)
                    wt2, wk2_ = wu_tile(kc, n)
                    up_.append((wt2, yq_ap)); uk2.extend(wk2_)
                mm(PSB[0][:, 0:TW], gp, rd=gk2 + yqkt, wr=["PS0"], tag=f"{tg}_gmm")
                mm(PSB[1][:, 0:TW], up_, rd=uk2 + yqkt, wr=["PS1"], tag=f"{tg}_umm")
                sgt, sgtk = SRF(0, TW)
                act(sgt, PSB[0][:, 0:TW], AF.Silu, rd=["PS0"], wr=sgtk,
                    tag=f"{tg}_sil")
                v("tensor_tensor", (sgt, sgt, PSB[1][:, 0:TW], ALU.mult),
                  rd=sgtk + ["PS1"], wr=sgtk, tag=f"{tg}_h")
                ha, hk = A3(n * TW, (n + 1) * TW)
                q4_part(sgt, sgtk, ha, hk, TW, f"{tg}_qh", gp=(n % 2 == 1))
            ACC = PSB[0:6] + [PR, PT]
            ACCK = ["PS0", "PS1", "PS2", "PS3", "PS4", "PS5", "PR", "PT"]

            def acc_ap(i):
                if i < 7:
                    return ACC[i][:, 0:TW]
                return PT[:].bitcast(F32)[:, 0:TW]

            for kcp in range(NKI):
                wda, wdk = WS((kcp % 2) * 1024, (kcp % 2) * 1024 + 1024)
                d("sync", wda, ds_wd[:, kcp * 128:(kcp + 1) * 128], f"WD{kcp % 2}",
                  rd=["DSDM"], wr=wdk, tag=f"{tg}_ldd", transpose=True)
                hq_ap = A3(kcp * TW, (kcp + 1) * TW)[0]
                hq_k = A3(kcp * TW, (kcp + 1) * TW)[1]
                for ts4 in range(4):
                    lhs = hq_ap[:, ts4 * 128:ts4 * 128 + 128]
                    for odh in range(2):
                        ai = ts4 * 2 + odh
                        mm(acc_ap(ai), [(lhs, wda[:, odh * 512:odh * 512 + 512])],
                           rd=hq_k + wdk,
                           wr=[ACCK[ai]] if ai < 7 else ["PT"],
                           accumulate=(kcp > 0), stop=(kcp == NKI - 1),
                           skip_group_check=True, tag=f"{tg}_dmm")
            for ts4 in range(4):
                xr, xrk = SB(0, 1024)
                d("sync", xr, ds_x[:, t0 + ts4 * 128:t0 + ts4 * 128 + 128],
                  f"XR{ts4 % 2}", rd=["DSXM"], wr=xrk, tag=f"{tg}_ldxr",
                  transpose=True)
                for odh in range(2):
                    ai = ts4 * 2 + odh
                    oa, ok2 = OST(odh * 256, odh * 256 + 256) if False else (None, None)
                    osl = OST.h[:][:, 0:512] if odh == 0 else None
                    stg = OST.h[:][:, 0:512]
                    stgk = ["OS0", "OS1"]
                    v("tensor_tensor", (stg, acc_ap(ai),
                                        xr[:, odh * 512:odh * 512 + 512], ALU.add),
                      rd=([ACCK[ai]] if ai < 7 else ["PT"]) + xrk,
                      wr=stgk, tag=f"{tg}_oadd")
                    d("sync", out[b, t0 + ts4 * 128:t0 + ts4 * 128 + 128,
                                  odh * 512:odh * 512 + 512],
                      stg, "OUT", rd=stgk, wr=["OUTM"], tag=f"{tg}_st")

    prep_jobs = []

    def make_prep_jobs():
        tg = "prep"
        for (wap, dsap, nrows, dkey) in ((w_gt, ds_wg, I_FFN, "DSG"),
                                         (w_up, ds_wu, I_FFN, "DSU"),
                                         (w_dn, ds_wd, D, "DSD")):
            wcols = 1024 if nrows == I_FFN else I_FFN
            for rc in range(nrows // 128):
                for lo in range(0, wcols, 512):
                    w_ = min(512, wcols - lo)

                    def job(wap=wap, dsap=dsap, dkey=dkey, rc=rc, lo=lo, w_=w_):
                        pka, pkk = PKF(0, w_)
                        d("gpsimd", pka,
                          wap[rc * 128:(rc + 1) * 128, lo:lo + w_],
                          "PKF", rd=[], wr=pkk, tag=f"{tg}_ldw")
                        outw, outwk = SCR(1024, 1024 + w_)
                        q4_free(pka, pkk, outw, outwk, w_, f"{tg}_wq",
                                slots=(SCR(0, w_), SCR(512, 512 + w_),
                                       QSC(0, w_ // 16), QSC(32, 32 + w_ // 16)))
                        d("gpsimd",
                          dsap[rc * 128:(rc + 1) * 128, lo:lo + w_],
                          outw, dkey, rd=outwk, wr=[f"{dkey}M"], tag=f"{tg}_stw")
                    prep_jobs.append(job)

    def prep_pump(n):
        for _ in range(min(n, len(prep_jobs))):
            prep_jobs.pop(0)()

    def prep_flush():
        prep_pump(len(prep_jobs))

    def phase_store(b):
        tg = f"b{b}st"
        for c in range(NCH):
            for tt in range(NTT):
                t0 = tt * TW
                stg = OST.h[:][:, 0:512]
                stgk = ["OS0", "OS1"]
                xa, xk = xap(c, t0, t0 + TW)
                v("tensor_copy", (stg, xa), rd=xk, wr=stgk, tag=f"{tg}_cp")
                d("sync", out[b, c * 128:(c + 1) * 128, t0:t0 + TW], stg, "OUT",
                  rd=stgk, wr=["OUTM"], tag=f"{tg}_st")

    if "l1f" in phases:
        make_prep_jobs()
        pump_holder[0] = prep_pump
        prep_pump(6)
    for b in range(nb):
        xk_all = XB.all()[1]
        d("gpsimd", XB.h[:].rearrange("p (c t) -> p c t", c=NCH),
          xin[b].rearrange("(c p) t -> p c t", p=128), "XIN",
          rd=xk_all, wr=xk_all, tag=f"b{b}_ldx")
        if "l0a" in phases:
            phase_l0a(b)
        if "l0f" in phases:
            phase_l0f(b)
        if "l1a" in phases:
            if "l1f" in phases:
                prep_flush()
                phase_l1a(b, after_qt=lambda qt, b=b: l1f_tile(b, qt))
            else:
                phase_l1a(b)
        if "l1f" in phases:
            if "l1a" not in phases:
                prep_flush()
                phase_l1f(b)
            else:
                phase_l1f(b, tiles_emitted=True)
        else:
            phase_store(b)

    S.op("sync", None, reads=["OUTM"], writes=[], tag="final_wait")

    stats = S.solve()
    S.emit(nc, ctx)
    ctx.close()
    return nc, stats


def _prep_inputs(inputs, T=2048, nb_per_core=2):
    f = np.ascontiguousarray
    x = inputs["L0_attn_hidden_states"]
    B = x.shape[0]
    ncores = B // nb_per_core
    xt = np.transpose(x, (0, 2, 1))
    common = {
        "cw": f(inputs["L0_attn_conv_w"].T.reshape(4, 8, 128).transpose(2, 1, 0)
                .reshape(128, 32)),
        "wq0": f(inputs["L0_attn_Wq"]),
        "wk0": f(inputs["L0_attn_Wk"]),
        "wv0": f(inputs["L0_attn_Wv"]),
        "wb0": f(inputs["L0_attn_Wb"]),
        "wg0": f(inputs["L0_attn_Wg"]),
        "wo0": f(inputs["L0_attn_Wo"]),
        "nw_t": f(np.broadcast_to(inputs["L0_attn_norm_w"], (128, 256))),
        "pk0": f(inputs["L0_ffn_packed_weight"].reshape(256, 1024)),
        "wsc0": f(inputs["L0_ffn_weight_scale"].reshape(1, 1)),
        "c_wsc_col": f(np.broadcast_to(inputs["L0_ffn_weight_scale"]
                                       .reshape(1, 1), (128, 1))),
        "bias0": f(inputs["L0_ffn_bias"].reshape(8, 128).T),
        "wq1": f(inputs["L1_attn_wq_w"].T),
        "bq1": f(inputs["L1_attn_wq_b"].reshape(8, 128).T),
        "wk1": f(inputs["L1_attn_wk_w"].T),
        "bk1": f(inputs["L1_attn_wk_b"].reshape(8, 128).T),
        "wv1": f(inputs["L1_attn_wv_w"].T),
        "bv1": f(inputs["L1_attn_wv_b"].reshape(8, 128).T),
        "wo1": f(inputs["L1_attn_wo_w"].T),
        "bo1": f(inputs["L1_attn_wo_b"].reshape(8, 128).T),
        "wgt": f(inputs["L1_ffn_w_gate"]),
        "wup": f(inputs["L1_ffn_w_up"]),
        "wdn": f(inputs["L1_ffn_w_down"]),
    }
    import ml_dtypes
    bf = ml_dtypes.bfloat16
    consts = {
        "c_id": np.eye(128, dtype=bf),
        "c_mneg_lo": np.where(np.arange(128)[None, :] < np.arange(128)[:, None],
                              -1.0, 0.0).astype(bf),
        "c_mneg_up": np.where(np.arange(128)[None, :] > np.arange(128)[:, None],
                              -1.0, 0.0).astype(bf),
        "c_m_upincl": np.where(np.arange(128)[None, :] >= np.arange(128)[:, None],
                               1.0, 0.0).astype(bf),
        "c_ones": np.ones((128, 1), dtype=bf),
        "c_ones_row": np.ones((1, 128), dtype=bf),
        "c_ones_mat": np.ones((128, 128), dtype=bf),
    }
    in_maps = []
    for core in range(ncores):
        m = {"xin": f(xt[core * nb_per_core:(core + 1) * nb_per_core])}
        m.update(common)
        m.update(consts)
        in_maps.append(m)
    return in_maps, ncores


def kernel(**inputs):
    T = inputs["L0_attn_hidden_states"].shape[1]
    in_maps, ncores = _prep_inputs(inputs, T=T)
    nc, stats = build(T=T, nb=2)
    res = run_bass_kernel_spmd(nc, in_maps, core_ids=list(range(ncores)))
    outs = [np.asarray(r["out"]) for r in res.results]
    full = np.concatenate(outs, axis=0)
    return np.ascontiguousarray(full.astype(np.float32))


# version 28
# speedup vs baseline: 1.5008x; 1.5008x over previous
import numpy as np
from contextlib import ExitStack

import concourse.bass as bass
import concourse.mybir as mybir
from concourse.bass_utils import run_bass_kernel_spmd

F32 = mybir.dt.float32
BF16 = mybir.dt.bfloat16
I32 = mybir.dt.int32
AF = mybir.ActivationFunctionType
ALU = mybir.AluOpType
AX = mybir.AxisListType

ENGINES = ("tensor", "vector", "scalar", "gpsimd", "sync")

D = 1024
NCH = 8
H0 = 4
DQH = 128
DVH = 256
CH = 128
H1 = 16
HD = 64
I_FFN = 2816
NKI = 22
LN_EPS = 1e-5
NORM_EPS = 1e-5


class Op:
    __slots__ = ("idx", "engine", "fn", "reads", "writes", "dma_buf", "tag",
                 "deps", "needs_inc", "tick", "waits", "snap", "est", "lat")


REORDER = True


class Sched:
    def __init__(self):
        self.ops = []
        self.dma_bufs = []

    def op(self, engine, fn, reads=(), writes=(), dma=None, tag="", est=400.0,
           lat=0.0):
        o = Op()
        o.idx = len(self.ops)
        o.engine = engine
        o.fn = fn
        o.reads = tuple(reads)
        o.writes = tuple(writes)
        o.dma_buf = dma
        o.tag = tag
        o.est = est
        o.lat = lat
        self.ops.append(o)
        if dma is not None and dma not in self.dma_bufs:
            self.dma_bufs.append(dma)
        return o

    def _reorder(self):
        import heapq
        n = len(self.ops)
        succs = [[] for _ in range(n)]
        npred = [0] * n
        for o in self.ops:
            for d2 in o.deps:
                succs[d2].append(o.idx)
            npred[o.idx] = len(o.deps)
        cp = [0.0] * n
        for o in reversed(self.ops):
            i = o.idx
            m = 0.0
            for s2 in succs[i]:
                if cp[s2] > m:
                    m = cp[s2]
            cp[i] = m + o.est + o.lat
        avail = [0.0] * n
        done = [0.0] * n
        ready = {e: [] for e in ENGINES}
        for o in self.ops:
            if npred[o.idx] == 0:
                heapq.heappush(ready[o.engine], (-cp[o.idx], o.idx))
        free = {e: 0.0 for e in ENGINES}
        order = []
        remaining = n
        while remaining:
            best = None
            for e in ENGINES:
                h = ready[e]
                if not h:
                    continue
                tmp = []
                cand = None
                for _ in range(8):
                    if not h:
                        break
                    negcp, i = heapq.heappop(h)
                    st = free[e] if free[e] > avail[i] else avail[i]
                    k = (st, negcp, i)
                    if cand is None or k < cand:
                        cand = k
                    tmp.append((negcp, i))
                for it in tmp:
                    heapq.heappush(h, it)
                if best is None or cand[:2] < best[0][:2]:
                    best = (cand, e)
            (st, negcp, i), e = best
            ready[e].remove((negcp, i))
            heapq.heapify(ready[e])
            o = self.ops[i]
            fin = st + o.est
            free[e] = fin
            done[i] = fin + o.lat
            order.append(o)
            remaining -= 1
            for s2 in succs[i]:
                npred[s2] -= 1
                if avail[s2] < done[i]:
                    avail[s2] = done[i]
                if npred[s2] == 0:
                    heapq.heappush(ready[self.ops[s2].engine], (-cp[s2], s2))
        self.makespan = max(free.values())
        return order

    def solve(self):
        last_writer = {}
        readers = {}
        for o in self.ops:
            deps = set()
            for b in o.reads:
                w = last_writer.get(b)
                if w is not None:
                    deps.add(w)
            for b in o.writes:
                w = last_writer.get(b)
                if w is not None:
                    deps.add(w)
                deps.update(readers.get(b, ()))
            deps.discard(o.idx)
            o.deps = deps
            for b in o.reads:
                readers.setdefault(b, []).append(o.idx)
            for b in o.writes:
                last_writer[b] = o.idx
                readers[b] = []
        for o in self.ops:
            o.needs_inc = False
        for o in self.ops:
            for d in o.deps:
                self.ops[d].needs_inc = True
        self.makespan = 0.0
        self.order = self._reorder() if REORDER else list(self.ops)
        etick = {e: 0 for e in ENGINES}
        dcum = {}
        for o in self.order:
            if o.dma_buf is not None:
                o.needs_inc = True
                dcum[o.dma_buf] = dcum.get(o.dma_buf, 0) + 16
                o.tick = ("D", o.dma_buf, dcum[o.dma_buf])
            elif o.needs_inc:
                etick[o.engine] += 1
                o.tick = ("E", o.engine, etick[o.engine])
            else:
                o.tick = None
        known = {e: {} for e in ENGINES}
        for o in self.order:
            need = {}
            for d in o.deps:
                t = self.ops[d].tick
                key = (t[0], t[1])
                if need.get(key, (-1, None))[0] < t[2]:
                    need[key] = (t[2], d)
            k = known[o.engine]
            waits = []
            for key in sorted(need):
                v, d = need[key]
                if k.get(key, -1) >= v:
                    continue
                waits.append((key, v))
                for sk, sv in self.ops[d].snap.items():
                    if k.get(sk, -1) < sv:
                        k[sk] = sv
                if k.get(key, -1) < v:
                    k[key] = v
            o.waits = waits
            if o.tick is not None:
                snap = dict(k)
                snap[(o.tick[0], o.tick[1])] = o.tick[2]
                o.snap = snap
            else:
                o.snap = None
        n_waits = sum(len(o.waits) for o in self.ops)
        n_incs = sum(1 for o in self.ops if o.tick is not None)
        return {"ops": len(self.ops), "waits": n_waits, "incs": n_incs,
                "dma_bufs": len(self.dma_bufs),
                "makespan_us": round(self.makespan / 1000.0, 1)}

    def emit(self, nc, ctx):
        sems = {}
        for e in ENGINES:
            sems[("E", e)] = ctx.enter_context(nc.semaphore(f"se_{e}"))
        for b in self.dma_bufs:
            sems[("D", b)] = ctx.enter_context(nc.semaphore(f"sd_{b}"[:28]))
        block = ctx.enter_context(nc.Block())
        by_engine = {e: [] for e in ENGINES}
        for o in self.order:
            by_engine[o.engine].append(o)

        def run_engine(eng_ops):
            def body(eng):
                for o in eng_ops:
                    for (key, v) in o.waits:
                        eng.wait_ge(sems[key], v)
                    if o.fn is None:
                        continue
                    inst = o.fn(eng)
                    if o.tick is not None:
                        assert inst is not None, f"op {o.tag} needs inc but fn returned None"
                        if o.tick[0] == "D":
                            inst.then_inc(sems[("D", o.tick[1])], 16)
                        else:
                            inst.then_inc(sems[("E", o.engine)], 1)
            return body

        block.tensor(run_engine(by_engine["tensor"]))
        block.vector(run_engine(by_engine["vector"]))
        block.scalar(run_engine(by_engine["scalar"]))
        block.gpsimd(run_engine(by_engine["gpsimd"]))
        block.sync(run_engine(by_engine["sync"]))


def _fsz(ap, default=512):
    try:
        sh = ap.shape
        n = 1
        for d2 in sh[1:]:
            n *= int(d2)
        return n
    except Exception:
        return default


class Bld:

    def __init__(self, nc, S, T):
        self.nc = nc
        self.S = S
        self.T = T
        self.t = {}
        self.uid = 0

    def u(self):
        self.uid += 1
        return self.uid

    def mm(self, out, pairs, rd, wr, tag="mm", skip_group_check=False,
           accumulate=False, stop=True):
        def f(e, out=out, pairs=pairs, acc=accumulate, sgc=skip_group_check,
              stop=stop):
            inst = None
            n = len(pairs)
            for i, (l, r) in enumerate(pairs):
                inst = e.matmul(out, l, r, start=(i == 0 and not acc),
                                stop=(stop and i == n - 1),
                                skip_group_check=sgc)
            return inst
        est = 0.0
        for (l, r) in pairs:
            fs = _fsz(r)
            try:
                narrow = l.shape[0] < 100 or _fsz(l) < 100
            except Exception:
                narrow = False
            est += fs * (0.72 if narrow else 0.44) + 80.0
        return self.S.op("tensor", f, reads=rd, writes=wr, tag=tag, est=est)

    def tp(self, outs_ins, ident, rd, wr, tag="tp"):
        def f(e, oi=outs_ins, ident=ident):
            inst = None
            for o, i in oi:
                inst = e.transpose(o, i, ident)
            return inst
        est = sum(_fsz(i) * 0.42 + 80.0 for (_, i) in outs_ins)
        return self.S.op("tensor", f, reads=rd, writes=wr, tag=tag, est=est)

    def vop(self, name, args, rd, wr, tag="v", eng="vector", kw=None):
        def f(e, name=name, args=args, kw=kw or {}):
            return getattr(e, name)(*args, **kw)
        fs = 1
        for a in args:
            s = _fsz(a, 0) if not isinstance(a, (int, float, type(None),
                                                 mybir.AluOpType)) else 0
            if s > fs:
                fs = s
        est = 170.0 + fs * (0.85 if eng == "vector" else 1.3)
        return self.S.op(eng, f, reads=rd, writes=wr, tag=tag, est=est)

    def act(self, out, in_, func, rd, wr, bias=0.0, scale=1.0, tag="act"):
        def f(e, out=out, in_=in_, func=func, bias=bias, scale=scale):
            return e.activation(out=out, in_=in_, func=func, bias=bias,
                                scale=scale)
        est = 170.0 + _fsz(in_) * 1.0
        return self.S.op("scalar", f, reads=rd, writes=wr, tag=tag, est=est)

    def dma(self, eng, out, in_, dmabuf, rd, wr, tag="dma", transpose=False):
        def f(e, out=out, in_=in_):
            if transpose:
                return e.dma_start_transpose(out, in_)
            return e.dma_start(out=out, in_=in_)
        lat = 1300.0 + _fsz(out) * 128 * 2 * 0.01
        return self.S.op(eng, f, reads=rd, writes=wr, dma=dmabuf, tag=tag,
                         est=150.0, lat=lat)


def build(T=2048, phases=("l0a", "l0f", "l1a", "l1f"), nb=2):
    assert T % 512 == 0
    nc = bass.Bass("TRN2", target_bir_lowering=False, debug=False)
    S = Sched()
    bld = Bld(nc, S, T)
    TW = 512
    NTT = T // TW
    NT128 = T // 128
    NCHK = T // CH

    def din(name, shape, dtype=F32):
        return nc.dram_tensor(name, list(shape), dtype, kind="ExternalInput").ap()

    xin = din("xin", (nb, D, T))
    w_cw = din("cw", (128, 32))
    w_q0 = din("wq0", (D, 512))
    w_k0 = din("wk0", (D, 512))
    w_v0 = din("wv0", (D, D))
    w_b0 = din("wb0", (D, 4))
    w_g0 = din("wg0", (D, D))
    w_o0 = din("wo0", (D, D))
    w_nw = din("nw_t", (128, DVH))
    pk0 = din("pk0", (256, 1024), I32)
    wsc0 = din("wsc0", (1, 1))
    bias0 = din("bias0", (128, NCH))
    w_q1 = din("wq1", (D, D)); b_q1 = din("bq1", (128, NCH))
    w_k1 = din("wk1", (D, D)); b_k1 = din("bk1", (128, NCH))
    w_v1 = din("wv1", (D, D)); b_v1 = din("bv1", (128, NCH))
    w_o1 = din("wo1", (D, D)); b_o1 = din("bo1", (128, NCH))
    w_gt = din("wgt", (I_FFN, D))
    w_up = din("wup", (I_FFN, D))
    w_dn = din("wdn", (D, I_FFN))
    c_id = din("c_id", (128, 128), BF16)
    c_mneg_lo = din("c_mneg_lo", (128, 128), BF16)
    c_mneg_up = din("c_mneg_up", (128, 128), BF16)
    c_m_upincl = din("c_m_upincl", (128, 128), BF16)
    c_ones = din("c_ones", (128, 1), BF16)
    c_ones_row = din("c_ones_row", (1, 128), BF16)
    c_ones_mat = din("c_ones_mat", (128, 128), BF16)
    c_wsc_col = din("c_wsc_col", (128, 1))

    tok_major_out = "l1f" in phases
    out_shape = [nb, T, D] if tok_major_out else [nb, D, T]
    out = nc.dram_tensor("out", out_shape, F32, kind="ExternalOutput").ap()

    ds_tern = nc.dram_tensor("ds_tern", [D, D], BF16, kind="Internal").ap()
    ds_wg = nc.dram_tensor("ds_wg", [I_FFN, D], BF16, kind="Internal").ap()
    ds_wu = nc.dram_tensor("ds_wu", [I_FFN, D], BF16, kind="Internal").ap()
    ds_wd = nc.dram_tensor("ds_wd", [D, I_FFN], BF16, kind="Internal").ap()
    ds_x = nc.dram_tensor("ds_x", [D, T], BF16, kind="Internal").ap()

    ctx = ExitStack()
    E = ctx.enter_context

    class Buf:
        def __init__(self, name, cols, dtype=BF16, unit=128, parts=128):
            self.h = E(nc.sbuf_tensor(name, [parts, cols], dtype))
            self.name = name
            self.unit = unit
            self.cols = cols

        def __call__(self, lo, hi, p0=0, p1=None):
            if p1 is None:
                p1 = self.h[:].shape[0]
            ap = self.h[:][p0:p1, lo:hi]
            keys = [f"{self.name}{i}" for i in range(lo // self.unit,
                                                     (hi - 1) // self.unit + 1)]
            return ap, keys

        def all(self):
            return self(0, self.cols)

    A4C = max(NCH * T, NT128 * 8 * 130)
    XB = Buf("XB", NCH * T, unit=TW)
    A1 = Buf("A1", NCH * T, unit=TW)
    A2 = Buf("A2", NCH * T, unit=TW)
    A3 = Buf("A3", NCH * T, unit=TW)
    A4 = Buf("A4", A4C, unit=T)
    WS = Buf("WS", 8192, unit=1024)
    SB = Buf("SB", 3072)
    ABT = Buf("AB", 2048)
    SCR = Buf("SR", 1664)
    S0B = Buf("S0", 4 * DVH, unit=DVH)
    CST = Buf("CS", 512, unit=512)
    NWT = Buf("NW", DVH, unit=DVH)
    CWT = Buf("CW", 32, F32, unit=32)
    ON2 = Buf("O2", 128, unit=128)
    SRA = Buf("RA", 512, unit=512)
    SRB = Buf("RB", 512, unit=512)
    SRC = Buf("RC", 512)
    SRF = Buf("RF", 512, F32, unit=512)
    OST = Buf("OS", 512, F32, unit=256)
    BCL = Buf("BC", 66, F32, unit=1)
    QSC = Buf("QC", 64, F32, unit=32)
    ATB = Buf("AT", 2048, unit=1024)
    BIA = Buf("BI", 48, F32, unit=48)
    ECL = Buf("EC", 16, F32, unit=16)
    PKF = Buf("PK", 512, F32, unit=512)

    PSQ = E(nc.psum_tensor("PSQ", [128, 2048], F32))
    PS4 = E(nc.psum_tensor("PS4", [128, 512], F32))
    PS5 = E(nc.psum_tensor("PS5", [128, 512], F32))
    PSB = [PSQ[:][:, i * 512:(i + 1) * 512] for i in range(4)] + [PS4[:], PS5[:]]
    PT = E(nc.psum_tensor("PT", [128, 1024], BF16))
    PR = E(nc.psum_tensor("PR", [128, 512], F32))
    SPS = [PSB[2], PSB[3], PSB[4], PSB[5]]

    ident, k_id = CST(0, 128)
    m_neg_lo, k_mlo = CST(128, 256)
    m_neg_up, k_mup = CST(256, 384)
    m_upincl, k_mui = CST(384, 512)
    ones_c, k_on = ON2.h[:][:, 0:1], ["O20"]

    def ones_row(p):
        return ON2.h[:][p:p + 1, 0:128]

    def YT(lo, hi):
        return SB(2048 + lo, 2048 + hi)

    d = bld.dma
    v = bld.vop
    mm = bld.mm
    tp = bld.tp
    act = bld.act

    def op2(eng, name1, a1, name2, a2, rd_, wr_, tag_):
        def f(e, a1=a1, a2=a2):
            getattr(e, name1)(*a1)
            return getattr(e, name2)(*a2)
        return S.op(eng, f, reads=rd_, writes=wr_, tag=tag_)

    def mm2(o1, p1, o2, p2, rd_, wr_, tag_):
        def f(e, o1=o1, p1=p1, o2=o2, p2=p2):
            e.matmul(o1, p1[0], p1[1], start=True, stop=True)
            return e.matmul(o2, p2[0], p2[1], start=True, stop=True)
        est = (_fsz(p1[1]) + _fsz(p2[1])) * 0.5 + 160.0
        return S.op("tensor", f, reads=rd_, writes=wr_, tag=tag_, est=est)

    def fence(rd_, wr_, tag="fence"):
        def f(e):
            return e.engine_nop()
        return S.op("vector", f, reads=rd_, writes=wr_, tag=tag)

    d("sync", ident, c_id, "cst", rd=[], wr=k_id, tag="ldid")
    d("sync", m_neg_lo, c_mneg_lo, "cst", rd=[], wr=k_mlo)
    d("sync", m_neg_up, c_mneg_up, "cst", rd=[], wr=k_mup)
    d("sync", m_upincl, c_m_upincl, "cst", rd=[], wr=k_mui)
    d("sync", ON2.h[:], c_ones_mat, "on2", rd=[], wr=["O20"])
    d("gpsimd", NWT.all()[0], w_nw, "nwt", rd=[], wr=NWT.all()[1])
    d("sync", CWT.all()[0], w_cw, "cwt", rd=[], wr=CWT.all()[1])
    bia_ap, bia_k = BIA(0, 48)
    d("sync", BIA(0, 8)[0], b_q1, "bia", rd=[], wr=bia_k)
    d("sync", BIA(8, 16)[0], b_k1, "bia", rd=[], wr=bia_k)
    d("sync", BIA(16, 24)[0], b_v1, "bia", rd=[], wr=bia_k)
    d("sync", BIA(24, 32)[0], b_o1, "bia", rd=[], wr=bia_k)
    d("sync", BIA(32, 40)[0], bias0, "bia", rd=[], wr=bia_k)
    d("sync", BIA(40, 41)[0], c_wsc_col, "bia", rd=[], wr=bia_k, tag="wscb")

    def bcol(i):
        return BIA.h[:][:, i:i + 1]

    v("memset", (ECL.h[:][:, 0:1], LN_EPS), rd=[], wr=["EC0"], tag="eps")
    v("memset", (ECL.h[:][:, 1:2], 1e-30), rd=[], wr=["EC0"], tag="tiny")
    v("memset", (ECL.h[:][:, 2:3], 0.5), rd=[], wr=["EC0"], tag="half")
    eps_col = ECL.h[:][:, 0:1]
    tiny_col = ECL.h[:][:, 1:2]
    half_col = ECL.h[:][:, 2:3]
    GRID_MID = [0.25, 0.75, 1.25, 1.75, 2.5, 3.5, 5.0]
    for t_, m_ in enumerate(GRID_MID):
        v("memset", (ECL.h[:][:, 3 + t_:4 + t_], -m_), rd=[], wr=["EC0"],
          tag="negm")

    def negm_col(t_):
        return ECL.h[:][:, 3 + t_:4 + t_]

    def xap(c, lo, hi):
        return XB(c * T + lo, c * T + hi)

    pump_holder = [lambda n: None]

    def pump_cb(n):
        pump_holder[0](n)

    def ln_tile(tt, par, tag):
        t0 = tt * TW
        PTF = PR
        xr, xk = [], []
        for c in range(NCH):
            a, k = xap(c, t0, t0 + TW)
            xr.append(a); xk.extend(k)
        mm(PTF[0:1, 0:TW], [(ones_c, a) for a in xr],
           rd=["O20"] + xk, wr=["PR"], tag=f"{tag}_sum{tt}")
        mrow = SRB.h[:][0:1, 0:TW]
        v("tensor_scalar", (mrow, PTF[0:1, 0:TW], 1.0 / D, None, ALU.mult),
          rd=["PR"], wr=["RBL"], tag=f"{tag}_m")
        for c in range(NCH):
            sl, slk = S0B((c % 2) * 512, (c % 2) * 512 + TW)
            a, k = xap(c, t0, t0 + TW)
            act(sl, a, AF.Square, rd=k, wr=slk, tag=f"{tag}_sq")
            mm(PTF[0:1, 0:TW], [(ones_c, sl)], rd=["O20"] + slk,
               wr=["PR"], accumulate=(c > 0), stop=(c == NCH - 1),
               tag=f"{tag}_ssmm")
        arow = SRA.h[:][0:1, 0:TW]
        v("tensor_tensor", (arow, mrow, mrow, ALU.mult),
          rd=["RBL"], wr=["RA0"], tag=f"{tag}_msq")
        v("scalar_tensor_tensor", (arow, PTF[0:1, 0:TW],
                                   1.0 / D, arow, ALU.mult,
                                   ALU.subtract),
          rd=["PR", "RA0"], wr=["RA0"], tag=f"{tag}_var")
        act(arow, arow, AF.Ln,
            rd=["RA0", "EC0"], wr=["RA0"], bias=ECL.h[:][0:1, 0:1],
            tag=f"{tag}_a1")
        act(arow, arow, AF.Exp, rd=["RA0"], wr=["RA0"], scale=-0.5,
            tag=f"{tag}_a2")
        brow = SRB.h[:][0:1, 0:TW]
        v("scalar_tensor_tensor", (brow, mrow, -1.0, arow, ALU.mult, ALU.mult),
          rd=["RBL", "RA0"], wr=["RBL"], tag=f"{tag}_b")
        aslot, ak = ABT(par * TW, par * TW + TW)
        bslot, bk = ABT(1024 + par * TW, 1024 + par * TW + TW)
        mm(PTF[:, 0:TW], [(ones_row(0), arow)],
           rd=["O20", "RA0"], wr=["PR"], tag=f"{tag}_bca")
        v("tensor_copy", (aslot, PTF[:, 0:TW]), rd=["PR"], wr=ak,
          tag=f"{tag}_eva")
        mm(PTF[:, 0:TW], [(ones_row(0), brow)],
           rd=["O20", "RBL"], wr=["PR"], tag=f"{tag}_bcb")
        v("tensor_copy", (bslot, PTF[:, 0:TW]), rd=["PR"], wr=bk,
          tag=f"{tag}_evb")

    def y_emit(c, lo, hi, outap, outkeys, par, tag):
        t0 = (lo // TW) * TW
        a_ap, ak = ABT(par * TW + (lo - t0), par * TW + (hi - t0))
        b_ap, bk = ABT(1024 + par * TW + (lo - t0), 1024 + par * TW + (hi - t0))
        xa, xk = xap(c, lo, hi)
        v("tensor_tensor", (outap, xa, a_ap, ALU.mult),
          rd=xk + ak, wr=outkeys, tag=f"{tag}_y1")
        v("tensor_tensor", (outap, outap, b_ap, ALU.add),
          rd=outkeys + bk, wr=outkeys, tag=f"{tag}_y2")

    def phase_l0a(b):
        tg = f"b{b}l0a"
        for tt in range(NTT):
            par = tt % 2
            ln_tile(tt, par, f"{tg}t{tt}")
            t0 = tt * TW
            for c in range(NCH):
                yt_m, yt_k = YT(3, 3 + TW)
                if tt == 0:
                    z3, _ = YT(0, 3)
                    v("memset", (z3, 0.0), rd=[], wr=yt_k, tag=f"{tg}_yz")
                else:
                    h3, _ = YT(0, 3)
                    y_emit(c, t0 - 3, t0, h3, yt_k, 1 - par, f"{tg}_yh")
                y_emit(c, t0, t0 + TW, yt_m, yt_k, par, f"{tg}_ym")
                ht, htk = SB((c % 2) * 512, (c % 2) * 512 + TW)
                ytf = SB.h[:][:, 2048:2568]
                v("tensor_scalar", (ht, ytf[:, 0:TW], CWT.h[:][:, 4 * c:4 * c + 1],
                                    None, ALU.mult),
                  rd=yt_k + ["CW0"], wr=htk, tag=f"{tg}_cv0")
                for i in (1, 2, 3):
                    v("scalar_tensor_tensor",
                      (ht, ytf[:, i:i + TW], CWT.h[:][:, 4 * c + i:4 * c + i + 1],
                       ht, ALU.mult, ALU.add),
                      rd=yt_k + ["CW0"] + htk, wr=htk, tag=f"{tg}_cv{i}")
                ha, hk = A1(c * T + t0, c * T + t0 + TW)
                act(ha, ht, AF.Silu, rd=htk, wr=hk, tag=f"{tg}_silu")

        def hap(k, lo, hi):
            return A1(k * T + lo, k * T + hi)


        def load_ws(dst_lo, wap, outw, key_tag):
            ap, keys = WS(dst_lo, dst_lo + NCH * outw)
            d("gpsimd", ap.rearrange("p (k n) -> p k n", k=NCH),
              wap.rearrange("(k p) n -> p k n", p=128), key_tag,
              rd=[], wr=keys, tag=f"ld_{key_tag}")
            return keys

        wq_k = load_ws(0, w_q0, 512, f"{tg}wq")
        wk_k = load_ws(4096, w_k0, 512, f"{tg}wk")

        def qk_unit(which, wlo, wkeys, dst_lo, hd, tt, pz):
            t0 = tt * TW
            pb, pk = (PSB[2 * pz], f"PS{2 * pz}")
            rowb, rowbk = (PSB[4], "PS4") if pz == 0 else (PSB[5], "PS5")
            pairs = []
            rds = list(wkeys)
            for k in range(NCH):
                wa, _ = WS(wlo + k * 512 + hd * 128, wlo + k * 512 + hd * 128 + 128)
                ha, hk2 = hap(k, t0, t0 + TW)
                pairs.append((wa, ha))
                rds.extend(hk2)
            mm(pb[:, 0:TW], pairs, rd=rds, wr=[pk], tag=f"{tg}_{which}mm")
            yield
            sq, sqk = SB(1024 + pz * 512, 1024 + pz * 512 + TW)
            act(sq, pb[:, 0:TW], AF.Square, rd=[pk], wr=sqk,
                tag=f"{tg}_{which}sq")
            yield
            mm(rowb[0:1, 0:TW], [(ones_c, sq)], rd=["O20"] + sqk,
               wr=[rowbk], tag=f"{tg}_{which}ss")
            yield
            rrow = SRC.h[:][64:65, 0:TW] if pz == 0 else SRB.h[:][64:65, 0:TW]
            rrk = ["RC0", "RC1", "RC2", "RC3"] if pz == 0 else ["RB0"]
            act(rrow, rowb[0:1, 0:TW], AF.Ln, bias=ECL.h[:][0:1, 1:2],
                rd=[rowbk, "EC0"], wr=rrk, tag=f"{tg}_{which}rn")
            yield
            act(rrow, rrow, AF.Exp, rd=rrk, wr=rrk, scale=-0.5,
                tag=f"{tg}_{which}rn2")
            yield
            v("tensor_scalar", (rrow, rrow, 1e12, None, ALU.min),
              rd=rrk, wr=rrk, tag=f"{tg}_{which}cl")
            yield
            mm(rowb[:, 0:TW], [(ones_row(64), rrow)], rd=["O20"] + rrk,
               wr=[rowbk], tag=f"{tg}_{which}bc")
            yield
            rnb, rnbk = SB(0, TW) if pz == 0 else SB(512, 512 + TW)
            v("tensor_copy", (rnb, rowb[:, 0:TW]), rd=[rowbk], wr=rnbk,
              tag=f"{tg}_{which}be")
            yield
            da, dk = A4(dst_lo + hd * T + t0, dst_lo + hd * T + t0 + TW)
            v("tensor_tensor", (da, pb[:, 0:TW], rnb, ALU.mult),
              rd=[pk] + rnbk, wr=dk, tag=f"{tg}_{which}ev")
            yield

        def qk_proj(which, wlo, wkeys, dst_lo):
            units = [(hd, tt) for hd in range(H0) for tt in range(NTT)]
            ui = 0
            while ui < len(units):
                pair = units[ui:ui + 2]
                gens = [qk_unit(which, wlo, wkeys, dst_lo, hh, tt_, pz)
                        for pz, (hh, tt_) in enumerate(pair)]
                while gens:
                    for g_ in list(gens):
                        try:
                            next(g_)
                        except StopIteration:
                            gens.remove(g_)
                ui += 2

        qk_proj("q", 0, wq_k, 0)
        pump_cb(12)
        qk_proj("k", 4096, wk_k, 4 * T)
        pump_cb(12)

        wb_sl, wb_k = SB(2048, 2048 + 32)
        d("gpsimd", wb_sl.rearrange("p (k n) -> p k n", k=NCH),
          w_b0.rearrange("(k p) n -> p k n", p=128), f"{tg}wb", rd=[], wr=wb_k,
          tag=f"{tg}_ldwb")
        for tt in range(NTT):
            t0 = tt * TW
            pairs = []
            rds = list(wb_k)
            for k in range(NCH):
                ha, hk2 = hap(k, t0, t0 + TW)
                pairs.append((wb_sl[:, k * 4:k * 4 + 4], ha))
                rds.extend(hk2)
            mm(PSB[0][0:4, 0:TW], pairs, rd=rds, wr=["PS0"], tag=f"{tg}_bmm")
            brows = SRC.h[:][0:4, 0:TW]
            brk = ["RC0", "RC1", "RC2", "RC3"]
            act(brows, PSB[0][0:4, 0:TW], AF.Sigmoid, rd=["PS0"], wr=brk,
                tag=f"{tg}_bsig")
            tp([(PT[:, j * 4:j * 4 + 4], SRC.h[:][0:4, j * 128:j * 128 + 128])
                for j in range(4)], ident[0:4, 0:4], rd=brk + k_id, wr=["PT"],
               tag=f"{tg}_btp")
            bc_ap, bc_k = BCL(tt * 16, tt * 16 + 16)
            v("tensor_copy", (bc_ap, PT[:, 0:16]), rd=["PT"], wr=bc_k,
              tag=f"{tg}_bev")

        def beta_col(hd, c):
            i = (c // 4) * 16 + (c % 4) * 4 + hd
            return BCL.h[:][:, i:i + 1], [f"BC{i}"]

        wv0_k = load_ws(0, w_v0[:, 0:512], 512, f"{tg}wv0")
        wv1_k = load_ws(4096, w_v0[:, 512:1024], 512, f"{tg}wv1")

        def v_unit(vc, tt, pz):
            wlo = 0 if vc < 4 else 4096
            wof = (vc % 4) * 128
            wkeys = wv0_k if vc < 4 else wv1_k
            hd, dvp = vc // 2, vc % 2
            t0 = tt * TW
            pb, pk = (PSB[2 * pz], f"PS{2 * pz}")
            pairs = []
            rds = list(wkeys)
            for k in range(NCH):
                wa, _ = WS(wlo + k * 512 + wof, wlo + k * 512 + wof + 128)
                ha, hk2 = hap(k, t0, t0 + TW)
                pairs.append((wa, ha))
                rds.extend(hk2)
            mm(pb[:, 0:TW], pairs, rd=rds, wr=[pk], tag=f"{tg}_vmm")
            yield
            vt, vtk = SB(pz * 512, pz * 512 + TW)
            v("copy", (vt, pb[:, 0:TW]), rd=[pk], wr=vtk, tag=f"{tg}_vev",
              eng="scalar")
            yield
            tp([(PT[:, j * 128:j * 128 + 128], vt[:, j * 128:j * 128 + 128])
                for j in range(4)], ident, rd=vtk + ["PT"] + k_id, wr=["PT"],
               tag=f"{tg}_vtp")
            for j in range(4):
                c = tt * 4 + j
                dlo = hd * (NCHK * DVH) + c * DVH + dvp * 128
                da, dk = A3(dlo, dlo + 128)
                bca, bck = beta_col(hd, c)
                v("tensor_scalar", (da, PT[:, j * 128:j * 128 + 128],
                                    bca, None, ALU.mult),
                  rd=["PT"] + bck, wr=dk, tag=f"{tg}_vbt")
            yield

        units_v = [(vc, tt) for vc in range(NCH) for tt in range(NTT)]
        ui = 0
        while ui < len(units_v):
            pairu = units_v[ui:ui + 2]
            gens = [v_unit(vv, tt_, pz) for pz, (vv, tt_) in enumerate(pairu)]
            while gens:
                for g_ in list(gens):
                    try:
                        next(g_)
                    except StopIteration:
                        gens.remove(g_)
            ui += 2

        wg0_k = load_ws(0, w_g0[:, 0:512], 512, f"{tg}wg0")
        wg1_k = load_ws(4096, w_g0[:, 512:1024], 512, f"{tg}wg1")

        def g_unit(j, half, pz):
            tj = j * 128
            wlo, wkeys = (0, wg0_k) if half == 0 else (4096, wg1_k)
            pb, pk = (PSB[2 * pz], f"PS{2 * pz}")
            pairs = []
            rds = list(wkeys)
            for k in range(NCH):
                ha, hk2 = hap(k, tj, tj + 128)
                wa, _ = WS(wlo + k * 512, wlo + k * 512 + 512)
                pairs.append((ha, wa))
                rds.extend(hk2)
            mm(pb[:, 0:TW], pairs, rd=rds, wr=[pk], tag=f"{tg}_gmm")
            yield
            da, dk = A2(j * 1024 + half * 512, j * 1024 + half * 512 + TW)
            v("copy", (da, pb[:, 0:TW]), rd=[pk], wr=dk, tag=f"{tg}_gev",
              eng="scalar")
            yield

        units_g = [(j, half) for j in range(NT128) for half in range(2)]
        ui = 0
        while ui < len(units_g):
            pairu = units_g[ui:ui + 2]
            gens = [g_unit(jj, hh_, pz) for pz, (jj, hh_) in enumerate(pairu)]
            while gens:
                for g_ in list(gens):
                    try:
                        next(g_)
                    except StopIteration:
                        gens.remove(g_)
            ui += 2

        wo0_k = load_ws(0, w_o0[:, 0:512], 512, f"{tg}wo0")
        wo1_k = load_ws(4096, w_o0[:, 512:1024], 512, f"{tg}wo1")

        def sl(hp, lo, w):
            return (SCR if hp == 0 else ATB)(lo, lo + w)

        nlev = 0
        while (1 << (nlev + 1)) < CH:
            nlev += 1

        def tbuild_gen(pair, hp, c):
            hd = pair * 2 + hp
            par = c % 2
            t0 = c * 128
            kof = (4 + hd) * T
            kkeyh = [f"A4{kof // T}"]
            kf = A4(kof + t0, kof + t0 + 128)[0]
            Tb = PSB[hp]
            tk = f"PS{hp}"
            ptr = 0 if hp == 0 else 512
            ptk = "PT"
            bca, bck = beta_col(hd, c)
            NP, NPk = sl(hp, par * 256, 256)
            N_ = NP[:, 0:128]
            Nt_ = NP[:, 128:256]
            TP, TPk = sl(hp, 512 + par * 256, 256)
            T_ = TP[:, 0:128]
            Tt_ = TP[:, 128:256]
            mm(Tb[:, 0:128], [(kf, kf)], rd=kkeyh, wr=[tk], tag=f"{tg}_Gmm")
            yield
            v("tensor_scalar", (N_, Tb[:, 0:128], bca, None, ALU.mult),
              rd=[tk] + bck, wr=NPk, tag=f"{tg}_Nb")
            yield
            v("tensor_tensor", (N_, N_, m_neg_lo, ALU.mult),
              rd=NPk + k_mlo, wr=NPk, tag=f"{tg}_Nm")
            yield
            tp([(PT[:, ptr:ptr + 128], N_)], ident,
               rd=NPk + ["PT"] + k_id, wr=["PT"], tag=f"{tg}_Nttp")
            v("copy", (Nt_, PT[:, ptr:ptr + 128]), rd=["PT"], wr=NPk,
              tag=f"{tg}_Ntev", eng="scalar")
            yield
            v("tensor_tensor", (TP, NP, ident[:, 0:128].broadcast_to([128, 256])
               if False else NP, ALU.bypass), rd=[], wr=[], tag="dead") \
                if False else None
            def f_ti(e, TP=TP, NP=NP):
                e.tensor_tensor(TP[:, 0:128], NP[:, 0:128], ident, ALU.add)
                return e.tensor_tensor(TP[:, 128:256], NP[:, 128:256], ident,
                                       ALU.add)
            S.op("vector", f_ti, reads=NPk + k_id, writes=TPk, tag=f"{tg}_Ti")
            yield
            xcur, xkcur = NP, NPk
            for j in range(1, nlev + 1):
                def f_sq(e, Tb=Tb, X=xcur):
                    e.matmul(Tb[:, 0:128], X[:, 128:256], X[:, 0:128],
                             start=True, stop=True, skip_group_check=True)
                    return e.matmul(Tb[:, 128:256], X[:, 0:128], X[:, 128:256],
                                    start=True, stop=True,
                                    skip_group_check=True)
                S.op("tensor", f_sq, reads=list(xkcur), writes=[tk],
                     tag=f"{tg}_sq{j}")
                yield
                v("copy", (NP, Tb[:, 0:256]), rd=[tk], wr=NPk,
                  tag=f"{tg}_sqe{j}", eng="scalar")
                yield
                def f_pr(e, Tb=Tb, X=xcur, TP=TP):
                    e.matmul(Tb[:, 0:128], X[:, 128:256], TP[:, 0:128],
                             start=True, stop=True, skip_group_check=True)
                    return e.matmul(Tb[:, 128:256], TP[:, 0:128], X[:, 128:256],
                                    start=True, stop=True,
                                    skip_group_check=True)
                S.op("tensor", f_pr, reads=list(xkcur) + list(TPk), writes=[tk],
                     tag=f"{tg}_pr{j}")
                yield
                v("tensor_tensor", (TP, Tb[:, 0:256], TP, ALU.add),
                  rd=[tk] + TPk, wr=TPk, tag=f"{tg}_pre{j}")
                yield

        def step_gen(pair, hp, c):
            hd = pair * 2 + hp
            par = c % 2
            t0 = c * 128
            qof, kof = hd * T, (4 + hd) * T
            qkeyh = [f"A4{qof // T}"]
            kkeyh = [f"A4{kof // T}"]
            kf = A4(kof + t0, kof + t0 + 128)[0]
            qf = A4(qof + t0, qof + t0 + 128)[0]
            Sb = PSB[2 + hp]
            sbk = f"PS{2 + hp}"
            Sps, Sk = (PSB[4], "PS4") if hp == 0 else (PSB[5], "PS5")
            TP, TPk = sl(hp, 512 + par * 256, 256)
            Tt_ = TP[:, 128:256]
            U_, Uk = sl(hp, 1024, 256)
            RP, RPk = sl(hp, 1280, 256)
            ktm, ktmk = sl(hp, 1536, 128)
            vlo = hd * (NCHK * DVH) + c * DVH
            vbt, vbk = A3(vlo, vlo + DVH)
            s0a, s0k = S0B(hd * DVH, (hd + 1) * DVH)
            bca, bck = beta_col(hd, c)
            ptr = 0 if hp == 0 else 512
            tp([(PT[:, ptr:ptr + 128], kf)], ident,
               rd=kkeyh + ["PT"] + k_id, wr=["PT"], tag=f"{tg}_ktp")
            v("copy", (ktm, PT[:, ptr:ptr + 128]), rd=["PT"], wr=list(ktmk),
              tag=f"{tg}_ktev", eng="scalar")
            yield
            if c == 0:
                Rap, Rkey = vbt, vbk
            else:
                mm(Sb[:, 0:DVH], [(kf, s0a)], rd=kkeyh + s0k, wr=[sbk],
                   tag=f"{tg}_kS")
                yield
                bn, bnk = BCL(64 + hp, 65 + hp)
                v("tensor_scalar", (bn, bca, -1.0, None, ALU.mult),
                  rd=bck, wr=bnk, tag=f"{tg}_bn")
                yield
                v("scalar_tensor_tensor", (RP[:, 0:DVH], Sb[:, 0:DVH], bn, vbt,
                                           ALU.mult, ALU.add),
                  rd=[sbk] + bnk + vbk, wr=RPk, tag=f"{tg}_R")
                yield
                Rap, Rkey = RP[:, 0:DVH], RPk
            mm(Sb[:, 0:DVH], [(Tt_, Rap)], rd=list(TPk) + list(Rkey), wr=[sbk],
               tag=f"{tg}_Umm")
            yield
            v("copy", (U_, Sb[:, 0:DVH]), rd=[sbk], wr=Uk,
              tag=f"{tg}_Uev", eng="scalar")
            yield
            Pm_ = RP[:, 0:128]
            mm(Sb[:, 0:128], [(kf, qf)], rd=kkeyh + qkeyh, wr=[sbk],
               tag=f"{tg}_Pmm")
            yield
            v("tensor_tensor", (Pm_, Sb[:, 0:128], m_upincl, ALU.mult),
              rd=[sbk] + k_mui, wr=RPk, tag=f"{tg}_Pev")
            yield
            if c == 0:
                pairs, rdo = [(Pm_, U_)], list(RPk) + list(Uk)
            else:
                pairs = [(qf, s0a), (Pm_, U_)]
                rdo = qkeyh + s0k + list(RPk) + list(Uk)
            mm(PR[:, 0:DVH], pairs, rd=rdo, wr=["PR"], tag=f"{tg}_omm")
            v("tensor_copy", (vbt, PR[:, 0:DVH]), rd=["PR"], wr=vbk,
              tag=f"{tg}_oev")
            yield
            mm(Sps[:, 0:DVH], [(ktm, U_)], rd=list(ktmk) + list(Uk),
               wr=[Sk], accumulate=(c > 0), stop=(c == NCHK - 1),
               skip_group_check=True, tag=f"{tg}_Smm")
            yield
            v("copy", (s0a, Sps[:, 0:DVH]), rd=[Sk], wr=s0k,
              tag=f"{tg}_Sev", eng="scalar")
            yield

        def head_gen(pair, hp):
            for c in range(NCHK + 1):
                if c < NCHK:
                    yield from tbuild_gen(pair, hp, c)
                if c >= 1:
                    yield from step_gen(pair, hp, c - 1)

        for pair in range(H0 // 2):
            gens = [head_gen(pair, 0), head_gen(pair, 1)]
            while gens:
                for g_ in list(gens):
                    try:
                        next(g_)
                    except StopIteration:
                        gens.remove(g_)

        for j in range(NT128):
            ga, gk = A2(j * 1024, (j + 1) * 1024)
            act(ga, ga, AF.Silu, rd=gk, wr=gk, tag=f"{tg}_esgb")
        units_e = [(hd, c) for hd in range(H0) for c in range(NCHK)]
        for gi in range(0, len(units_e), 8):
            grp = units_e[gi:gi + 8]
            tiles_ = []
            for ui, (hd, c) in enumerate(grp):
                vlo = hd * (NCHK * DVH) + c * DVH
                ot, ok = A3(vlo, vlo + DVH)
                sq, sqk = SB((ui % 4) * 512, (ui % 4) * 512 + DVH) if False \
                    else SB(ui * 256, (ui + 1) * 256)
                tiles_.append((hd, c, ot, ok, sq, sqk))
            for (hd, c, ot, ok, sq, sqk) in tiles_:
                act(sq, ot, AF.Square, rd=ok, wr=sqk, tag=f"{tg}_esq")
            for ui, (hd, c, ot, ok, sq, sqk) in enumerate(tiles_):
                col, colk = BCL(48 + 2 * (ui % 8), 49 + 2 * (ui % 8))
                v("reduce_sum", (col, sq.rearrange("p (a b) -> p a b", a=1)),
                  rd=sqk, wr=colk, tag=f"{tg}_erd", kw={"axis": AX.X})
            for ui, (hd, c, ot, ok, sq, sqk) in enumerate(tiles_):
                col, colk = BCL(48 + 2 * (ui % 8), 49 + 2 * (ui % 8))
                act(col, col, AF.Ln, rd=colk + ["EC0"], wr=colk,
                    bias=eps_col, scale=1.0 / DVH, tag=f"{tg}_ers")
            for ui, (hd, c, ot, ok, sq, sqk) in enumerate(tiles_):
                col, colk = BCL(48 + 2 * (ui % 8), 49 + 2 * (ui % 8))
                act(col, col, AF.Exp, rd=colk, wr=colk, scale=-0.5,
                    tag=f"{tg}_ers2")
            for ui, (hd, c, ot, ok, sq, sqk) in enumerate(tiles_):
                col, colk = BCL(48 + 2 * (ui % 8), 49 + 2 * (ui % 8))
                v("tensor_scalar", (ot, ot, col, None, ALU.mult),
                  rd=ok + colk, wr=ok, tag=f"{tg}_ez1")
            for (hd, c, ot, ok, sq, sqk) in tiles_:
                v("tensor_tensor", (ot, ot, NWT.all()[0], ALU.mult),
                  rd=ok + NWT.all()[1], wr=ok, tag=f"{tg}_ez2")
            for (hd, c, ot, ok, sq, sqk) in tiles_:
                ga, gk = A2(c * 1024 + hd * DVH, c * 1024 + (hd + 1) * DVH)
                v("tensor_tensor", (ot, ot, ga, ALU.mult),
                  rd=ok + gk, wr=ok, tag=f"{tg}_ez3")
        for c in range(NCHK):
            tiles, rds = [], list(k_id)
            for hd in range(H0):
                for dvp in range(2):
                    slo = hd * (NCHK * DVH) + c * DVH + dvp * 128
                    sap, sk = A3(slo, slo + 128)
                    tiles.append((PT[:, (hd * 2 + dvp) * 128:(hd * 2 + dvp) * 128 + 128], sap))
                    rds.extend(sk)
            tp(tiles, ident, rd=rds, wr=["PT"], tag=f"{tg}_ztp")
            zfk = []
            for vv in range(NCH):
                zfk += A1(vv * T + c * 128, vv * T + (c + 1) * 128)[1]
            v("tensor_copy",
              (A1.h[:].rearrange("p (vv t) -> p vv t", vv=NCH)[:, :, c * 128:(c + 1) * 128],
               PT[:].rearrange("p (vv t) -> p vv t", vv=NCH)),
              rd=["PT"], wr=zfk, tag=f"{tg}_zev")
        for tt in range(NTT):
            t0 = tt * TW
            for oc in range(NCH):
                wlo = 0 if oc < 4 else 4096
                wof = (oc % 4) * 128
                wkeys = wo0_k if oc < 4 else wo1_k
                pi = (oc * NTT + tt) % 2
                pb, pk = PSB[pi], f"PS{pi}"
                pairs = []
                rds = list(wkeys)
                for k in range(NCH):
                    wa, _ = WS(wlo + k * 512 + wof, wlo + k * 512 + wof + 128)
                    za, zk2 = A1(k * T + t0, k * T + t0 + TW)
                    pairs.append((wa, za))
                    rds.extend(zk2)
                mm(pb[:, 0:TW], pairs, rd=rds, wr=[pk],
                   tag=f"{tg}_womm")
                xa, xk = xap(oc, t0, t0 + TW)
                v("tensor_tensor", (xa, pb[:, 0:TW], xa, ALU.add),
                  rd=[pk] + xk, wr=xk, tag=f"{tg}_wores")

    def phase_l0f(b):
        tg = f"b{b}l0f"
        if b == 0:
            for rc in range(2):
              for hf in range(2):
                pka, pkk = PKF(0, 512)
                d("gpsimd", pka, pk0[rc * 128:(rc + 1) * 128, hf * 512:hf * 512 + 512],
                  "PKF", rd=[], wr=pkk, tag=f"{tg}_ldpk")
                for s in (3, 2, 1, 0):
                    base = float(4 ** s)
                    dig, digk = SB(0, 512)
                    if s > 0:
                        v("tensor_scalar", (dig, pka, base, None, ALU.is_ge),
                          rd=pkk, wr=digk, tag=f"{tg}_d1")
                        v("scalar_tensor_tensor", (dig, pka, 2 * base, dig,
                                                   ALU.is_ge, ALU.add),
                          rd=pkk + digk, wr=digk, tag=f"{tg}_d2")
                        v("scalar_tensor_tensor", (dig, pka, 3 * base, dig,
                                                   ALU.is_ge, ALU.add),
                          rd=pkk + digk, wr=digk, tag=f"{tg}_d3")
                        v("scalar_tensor_tensor", (pka, dig, -base, pka,
                                                   ALU.mult, ALU.add),
                          rd=pkk + digk, wr=pkk, tag=f"{tg}_dr")
                    else:
                        v("tensor_copy", (dig, pka), rd=pkk, wr=digk,
                          tag=f"{tg}_d0")
                    tb, tbk = SB(512, 1024)
                    v("tensor_scalar", (tb, dig, 1.0, None, ALU.subtract),
                      rd=digk, wr=tbk, tag=f"{tg}_dm1")
                    d("gpsimd", ds_tern[s * 256 + rc * 128:s * 256 + rc * 128 + 128,
                                        hf * 512:hf * 512 + 512],
                      tb, "DST", rd=tbk, wr=["DSTERN"], tag=f"{tg}_stw")
        for tt in range(NTT):
            par = tt % 2
            ln_tile(tt, par, f"{tg}t{tt}")
            t0 = tt * TW
            for c in range(NCH):
                ya, yk = A1(c * T + t0, c * T + t0 + TW)
                y_emit(c, t0, t0 + TW, ya, yk, par, f"{tg}_y")
            for jj in range(4):
                j = tt * 4 + jj
                tjl = j * 128
                ytk = []
                for k in range(NCH):
                    ytk += A1(k * T + tjl, k * T + tjl + 128)[1]
                tp([(PT[:, k * 128:k * 128 + 128],
                     A1(k * T + tjl, k * T + tjl + 128)[0]) for k in range(NCH)],
                   ident, rd=ytk + k_id, wr=["PT"], tag=f"{tg}_ytp")
                amax, amk = BCL(50, 51)
                v("reduce_max", (amax, PT[:].rearrange("p (a b) -> p a b", a=1)),
                  rd=["PT"], wr=amk, tag=f"{tg}_amax",
                  kw={"axis": AX.X, "apply_absolute_value": True})
                mx, mxk = BCL(51, 52)
                v("tensor_scalar", (mx, amax, 1e-5, None, ALU.max),
                  rd=amk, wr=mxk, tag=f"{tg}_mx")
                rcc, rck = BCL(52, 53)
                v("reciprocal", (rcc, mx), rd=mxk, wr=rck, tag=f"{tg}_rc")
                asc, asck = BCL(53, 54)
                v("tensor_scalar", (asc, rcc, 127.0, None, ALU.mult),
                  rd=rck, wr=asck, tag=f"{tg}_as")
                ra, rak = BCL(54, 55)
                v("tensor_scalar", (ra, mx, 1.0 / 127.0, bcol(40), ALU.mult,
                                    ALU.mult),
                  rd=mxk + bia_k, wr=rak, tag=f"{tg}_ra")
                cb, cbk = SB(2080, 2081)
                v("tensor_copy", (cb, asc), rd=asck, wr=cbk, tag=f"{tg}_cb")
                mm(PR[0:1, jj * 128:jj * 128 + 128], [(cb, ident)],
                   rd=cbk + k_id, wr=["PR"], tag=f"{tg}_ar")
                cb2, cb2k = SB(2081, 2082)
                v("tensor_copy", (cb2, ra), rd=rak, wr=cb2k, tag=f"{tg}_cb2")
                mm(PSB[0][0:1, jj * 128:jj * 128 + 128], [(cb2, ident)],
                   rd=cb2k + k_id, wr=["PS0"], tag=f"{tg}_rr")
            arow = SRC.h[:][64:65, 0:TW]
            ark = ["RC0", "RC1", "RC2", "RC3"]
            v("tensor_copy", (arow, PR[0:1, 0:TW]), rd=["PR"], wr=ark,
              tag=f"{tg}_arv")
            mm(PR[:, 0:TW], [(ones_row(64), arow)], rd=["O20"] + ark, wr=["PR"],
               tag=f"{tg}_abc")
            ab, abk = SB(1024, 1024 + TW)
            v("tensor_copy", (ab, PR[:, 0:TW]), rd=["PR"], wr=abk,
              tag=f"{tg}_abv")
            rrow = SRC.h[:][0:1, 0:TW]
            v("tensor_copy", (rrow, PSB[0][0:1, 0:TW]), rd=["PS0"], wr=ark,
              tag=f"{tg}_rrv")
            mm(PR[:, 0:TW], [(ones_row(0), rrow)], rd=["O20"] + ark, wr=["PR"],
               tag=f"{tg}_rbc")
            rb, rbk = ATB(tt * TW, tt * TW + TW)
            v("tensor_copy", (rb, PR[:, 0:TW]), rd=["PR"], wr=rbk,
              tag=f"{tg}_rbv")
            for c in range(NCH):
                ya, yk = A1(c * T + t0, c * T + t0 + TW)
                if c % 2 == 0:
                    vt, vtk = OST(0, TW)
                    rr, rrk = PKF(0, TW)
                    sbs, sbk = SB(0, TW)
                else:
                    vt, vtk = PSB[4][:, 0:TW], "PS4"
                    vtk = [vtk]
                    rr, rrk = PSB[5][:, 0:TW], ["PS5"]
                    sbs, sbk = SB(512, 512 + TW)
                v("tensor_tensor", (vt, ya, ab, ALU.mult),
                  rd=yk + abk, wr=vtk, tag=f"{tg}_qv")
                src, srck = vt, vtk
                for kb in (64.0, 32.0, 16.0, 8.0, 4.0, 2.0, 1.0):
                    act(sbs, src, AF.Sign, rd=srck + ["EC0"], wr=sbk,
                        bias=half_col, tag=f"{tg}_qsg")
                    v("scalar_tensor_tensor", (rr, sbs, -kb, src,
                                               ALU.mult, ALU.add),
                      rd=sbk + srck, wr=rrk, tag=f"{tg}_qst")
                    src, srck = rr, rrk
                v("tensor_scalar", (sbs, rr, -0.5, -1.0, ALU.is_ge, ALU.add),
                  rd=rrk, wr=sbk, tag=f"{tg}_qb0")
                v("scalar_tensor_tensor", (rr, sbs, -1.0, rr, ALU.mult,
                                           ALU.add),
                  rd=sbk + rrk, wr=rrk, tag=f"{tg}_qr0")
                xia, xik = A2(c * T + t0, c * T + t0 + TW)
                if c % 2 == 0:
                    v("scalar_tensor_tensor", (xia, rr, -1.0, vt, ALU.mult,
                                               ALU.add),
                      rd=rrk + vtk, wr=xik, tag=f"{tg}_qm")
                else:
                    v("copy", (sbs, rr), rd=rrk, wr=sbk, tag=f"{tg}_qrc",
                      eng="scalar")
                    v("scalar_tensor_tensor", (xia, sbs, -1.0, vt, ALU.mult,
                                               ALU.add),
                      rd=sbk + vtk, wr=xik, tag=f"{tg}_qm")
        wtk = []
        for k in range(NCH):
            wa, wk2 = WS(k * 1024, (k + 1) * 1024)
            d("sync", wa, ds_tern[:, k * 128:(k + 1) * 128], f"WT{k}",
              rd=["DSTERN"], wr=wk2, tag=f"{tg}_ldt", transpose=True)
            wtk.extend(wk2)
        rakeys = [f"AT{i}" for i in range((NTT * TW) // 1024 + 1)]
        for tt in range(NTT):
            t0 = tt * TW
            for oc in range(NCH):
                pi = (oc * NTT + tt) % 2
                pb, pk = PSB[pi], f"PS{pi}"
                pairs = []
                rds = list(wtk)
                for k in range(NCH):
                    wa, _ = WS(k * 1024 + oc * 128, k * 1024 + oc * 128 + 128)
                    xa2, xk2 = A2(k * T + t0, k * T + t0 + TW)
                    pairs.append((wa, xa2))
                    rds.extend(xk2)
                mm(pb[:, 0:TW], pairs, rd=rds, wr=[pk],
                   tag=f"{tg}_tmm")
                st, stk = SB(1536, 1536 + TW)
                rb, rbk = ATB(t0, t0 + TW)
                v("tensor_tensor", (st, pb[:, 0:TW], rb, ALU.mult),
                  rd=[pk] + rbk, wr=stk, tag=f"{tg}_fra")
                v("tensor_scalar", (st, st, bcol(32 + oc), None, ALU.add),
                  rd=stk + bia_k, wr=stk, tag=f"{tg}_fb")
                xa, xk = xap(oc, t0, t0 + TW)
                v("tensor_tensor", (xa, st, xa, ALU.add),
                  rd=stk + xk, wr=xk, tag=f"{tg}_fres")

    GRID_D = [0.25, 0.75, 1.25, 1.75, 2.5, 3.5, 5.0]

    def q4_core(fin, fink, out_ap, out_k, w, tag, avslot, slots=None):
        assert w <= 512 and w % 16 == 0
        nb16 = w // 16
        av, avk = avslot
        act(av, fin, AF.Abs, rd=fink, wr=avk, tag=f"{tag}_abs")
        if slots is None:
            accp, indp = SCR(0, w), SCR(512, 512 + w)
            scp, thrp = BCL(0, nb16), BCL(32, 32 + nb16)
        else:
            accp, indp, scp, thrp = slots
        sc, sck = scp
        v("reduce_max", (sc, av.rearrange("p (a b) -> p a b", b=16)),
          rd=avk, wr=sck, tag=f"{tag}_bm", kw={"axis": AX.X})
        v("tensor_scalar", (sc, sc, 1.0 / 6.0, 1e-12, ALU.mult, ALU.max),
          rd=sck, wr=sck, tag=f"{tag}_sc")
        av3 = av.rearrange("p (a b) -> p a b", b=16)
        acc, acck = accp
        acc3 = acc.rearrange("p (a b) -> p a b", b=16)
        thr, thrk = thrp
        ind, indk = indp
        ind3 = ind.rearrange("p (a b) -> p a b", b=16)
        g2a, g2ak = SCR(1024, 1024 + w) if slots is None else indp
        g2a3 = g2a.rearrange("p (a b) -> p a b", b=16)

        rcp, rcpk = thrp if slots is not None else QSC(32, 32 + w // 16)
        v("reciprocal", (rcp, sc), rd=sck, wr=rcpk, tag=f"{tag}_rcp")
        v("tensor_tensor", (av3, av3,
                            rcp[:, :, None].broadcast_to([128, w // 16, 16]),
                            ALU.mult),
          rd=avk + rcpk, wr=avk, tag=f"{tag}_un")

        HW = (0.25, 0.25, 0.25, 0.25, 0.5, 0.5, 1.0)
        for t in range(7):
            st_, stk_ = (g2a, g2ak) if t % 2 == 0 else (ind, indk)
            act(st_, av, AF.Sign, rd=avk + ["EC0"], wr=stk_,
                bias=negm_col(t), tag=f"{tag}_s{t}")
            if t == 0:
                v("tensor_scalar", (acc, st_, HW[0], 3.0, ALU.mult,
                                    ALU.add),
                  rd=stk_, wr=acck, tag=f"{tag}_c0")
            else:
                v("scalar_tensor_tensor", (acc, st_, HW[t], acc,
                                           ALU.mult, ALU.add),
                  rd=stk_ + acck, wr=acck, tag=f"{tag}_c{t}")
        sg, sgk = (ind, indk)
        act(sg, fin, AF.Sign, rd=fink, wr=sgk, tag=f"{tag}_sg")
        v("tensor_tensor", (acc, acc, sg, ALU.mult),
          rd=acck + sgk, wr=acck, tag=f"{tag}_ms")
        scb = sc[:, :, None].broadcast_to([128, nb16, 16])
        v("tensor_tensor", (out_ap.rearrange("p (a b) -> p a b", b=16), acc3, scb,
                            ALU.mult),
          rd=acck + sck, wr=out_k, tag=f"{tag}_msc")

    def q4_free(fin, fink, out_ap, out_k, w, tag, slots=None):
        q4_core(fin, fink, out_ap, out_k, w, tag, OST(0, w), slots=slots)

    def q4_part(fin, fink, out_ap, out_k, w, tag):
        zt, ztk = PKF(0, w)
        v("transpose", (zt, fin), rd=fink, wr=ztk, tag=f"{tag}_zi")
        zq, zqk = SCR(512, 512 + w)
        q4_core(zt, ztk, zq, zqk, w, f"{tag}z", OST(0, w))
        v("transpose", (out_ap, zq), rd=zqk, wr=out_k, tag=f"{tag}_zo")

    def phase_l1a(b, after_qt=None):
        tg = f"b{b}l1a"
        for tt in range(NTT):
            par = tt % 2
            ln_tile(tt, par, f"{tg}t{tt}")
            t0 = tt * TW
            for c in range(NCH):
                ya, yk = A1(c * T + t0, c * T + t0 + TW)
                y_emit(c, t0, t0 + TW, ya, yk, par, f"{tg}_y")

        def load_w1(wap, key_tag):
            ap, keys = WS(0, 8192)
            d("gpsimd", ap.rearrange("p (k n) -> p k n", k=NCH),
              wap.rearrange("(k p) n -> p k n", p=128), key_tag,
              rd=[], wr=keys, tag=f"ld_{key_tag}")
            return keys

        fence([], A4.all()[1] + [f"VT{kt}" for kt in range(NT128)],
              f"{tg}_fv")

        def proj(dst, dkeyf, bias_base, wkeys, tag2, to_vt=False):
            for tt in range(NTT):
                t0 = tt * TW
                for oc in range(NCH):
                    pi = 2 + (oc * NTT + tt) % 2
                    pb, pk = PSB[pi], f"PS{pi}"
                    pairs = []
                    rds = list(wkeys)
                    for k in range(NCH):
                        wa, _ = WS(k * 1024 + oc * 128, k * 1024 + oc * 128 + 128)
                        ya, yk2 = A1(k * T + t0, k * T + t0 + TW)
                        pairs.append((wa, ya))
                        rds.extend(yk2)
                    mm(pb[:, 0:TW], pairs, rd=rds, wr=[pk], tag=f"{tg}_{tag2}mm")
                    if not to_vt:
                        da, dk = dst(oc * T + t0, oc * T + t0 + TW)
                        v("tensor_scalar", (da, pb[:, 0:TW], bcol(bias_base + oc),
                                            None, ALU.add),
                          rd=[pk] + bia_k, wr=dk, tag=f"{tg}_{tag2}ev")
                    else:
                        vt, vtk = SB(1536, 1536 + TW)
                        v("tensor_scalar", (vt, pb[:, 0:TW], bcol(bias_base + oc),
                                            None, ALU.add),
                          rd=[pk] + bia_k, wr=vtk, tag=f"{tg}_{tag2}ev")
                        tp([(PT[:, j * 128:j * 128 + 128], vt[:, j * 128:j * 128 + 128])
                            for j in range(4)], ident, rd=vtk + k_id, wr=["PT"],
                           tag=f"{tg}_{tag2}tp")
                        for j in range(4):
                            kt = tt * 4 + j
                            base = (kt * 8 + oc) * 130
                            ve_ap = A4.h[:][:, base:base + 64]
                            vo_ap = A4.h[:][:, base + 65:base + 129]
                            op2("vector", "tensor_copy", (ve_ap, PT[:, j * 128:j * 128 + 64]),
                                "tensor_copy", (vo_ap, PT[:, j * 128 + 64:j * 128 + 128]),
                                ["PT"], [f"VT{kt}"], f"{tg}_{tag2}vt")

        wk1 = load_w1(w_q1, f"{tg}wq")
        proj(A2, None, 0, wk1, "q")
        wk2 = load_w1(w_k1, f"{tg}wk")
        proj(A3, None, 8, wk2, "k")
        wk3 = load_w1(w_v1, f"{tg}wv")
        proj(None, None, 16, wk3, "v", to_vt=True)
        vt_ones = A4.h[:].rearrange("p (t w) -> p t w", w=130)
        op2("vector", "memset", (vt_ones[:, 0:NT128 * 8, 64:65], 1.0),
            "memset", (vt_ones[:, 0:NT128 * 8, 129:130], 1.0),
            [], [f"VT{kt}" for kt in range(NT128)], f"{tg}_vones")
        wk4 = load_w1(w_o1, f"{tg}wo")

        def att_unit(h, qt, pz):
            ch = h // 2
            p0 = (h % 2) * 64
            q0 = qt * TW
            qb = 2 * pz
            avb, avk = (PSB[4], "PS4") if pz == 0 else (PSB[5], "PS5")
            for g in range(NT128 // 2):
                kc0, kc1 = 2 * g, 2 * g + 1
                qa = A2.h[:][p0:p0 + 64, ch * T + q0:ch * T + q0 + TW]
                qak = A2(ch * T + q0, ch * T + q0 + TW)[1]
                ka0 = A3.h[:][p0:p0 + 64,
                              ch * T + kc0 * 128:ch * T + kc0 * 128 + 128]
                ka1 = A3.h[:][p0:p0 + 64,
                              ch * T + kc1 * 128:ch * T + kc1 * 128 + 128]
                kak = A3(ch * T + kc0 * 128, ch * T + (kc1 + 1) * 128)[1]
                mm2(PSB[qb][:, 0:TW], (ka0, qa),
                    PSB[qb + 1][:, 0:TW], (ka1, qa),
                    kak + qak, [f"PS{qb}", f"PS{qb + 1}"], f"{tg}_lmm")
                yield
                at_ap, at_k = ATB(pz * 1024, pz * 1024 + 1024)
                act(at_ap, PSQ[:][:, qb * 512:qb * 512 + 1024], AF.Exp,
                    rd=[f"PS{qb}", f"PS{qb + 1}"], wr=at_k,
                    scale=0.125, tag=f"{tg}_exp")
                yield
                base0 = (kc0 * 8 + ch) * 130 + (0 if h % 2 == 0 else 65)
                base1 = (kc1 * 8 + ch) * 130 + (0 if h % 2 == 0 else 65)
                av_pairs = [
                    (A4.h[:][:, base0:base0 + 65],
                     ATB.h[:][:, pz * 1024:pz * 1024 + TW]),
                    (A4.h[:][:, base1:base1 + 65],
                     ATB.h[:][:, pz * 1024 + 512:pz * 1024 + 512 + TW])]
                mm(avb[0:65, 0:TW], av_pairs,
                   rd=[f"VT{kc0}", f"VT{kc1}"] + at_k, wr=[avk],
                   accumulate=(g > 0), stop=(g == NT128 // 2 - 1),
                   skip_group_check=True, tag=f"{tg}_avmm")
                yield
            f1 = (SRC.h[:][64:65, 0:TW] if pz == 0 else SRB.h[:][64:65, 0:TW])
            f1k = ["RC0", "RC1", "RC2", "RC3"] if pz == 0 else ["RB0"]
            act(f1, avb[64:65, 0:TW], AF.Ln, rd=[avk], wr=f1k,
                tag=f"{tg}_rc1")
            yield
            act(f1, f1, AF.Exp, rd=f1k, wr=f1k, scale=-1.0, tag=f"{tg}_rc2")
            yield
            fb, fbk = SB(1536, 1536 + TW) if pz == 0 else SB(1024, 1024 + TW)
            mm(PSB[qb + 1][:, 0:TW], [(ON2.h[:][64:65, :], f1)],
               rd=["O20"] + f1k, wr=[f"PS{qb + 1}"], tag=f"{tg}_fbc")
            v("tensor_copy", (fb, PSB[qb + 1][:, 0:TW]), rd=[f"PS{qb + 1}"],
              wr=fbk, tag=f"{tg}_fbe")
            yield
            dst = A2.h[:][p0:p0 + 64, ch * T + q0:ch * T + q0 + TW]
            if h % 2 == 0:
                v("tensor_tensor", (dst, avb[0:64, 0:TW], fb[0:64, :],
                                    ALU.mult),
                  rd=[avk] + fbk, wr=A2(ch * T + q0, ch * T + q0 + TW)[1],
                  tag=f"{tg}_poev")
                yield
            else:
                mv, mvk = SB(0, TW) if pz == 0 else SB(512, 512 + TW)
                v("tensor_tensor", (mv[0:64, :], avb[0:64, 0:TW],
                                    fb[0:64, :], ALU.mult),
                  rd=[avk] + fbk, wr=mvk, tag=f"{tg}_ponm")
                yield
                mm(PSB[qb][64:128, 0:TW], [(ident[0:64, 0:64], mv[0:64, :])],
                   rd=mvk + k_id, wr=[f"PS{qb}"], tag=f"{tg}_pomv")
                v("tensor_copy", (dst, PSB[qb][64:128, 0:TW]),
                  rd=[f"PS{qb}"], wr=A2(ch * T + q0, ch * T + q0 + TW)[1],
                  tag=f"{tg}_poev")
                yield

        def emit_wo(tt):
            t0 = tt * TW
            for oc in range(NCH):
                pairs = []
                rds = list(wk4)
                for k in range(NCH):
                    wa, _ = WS(k * 1024 + oc * 128, k * 1024 + oc * 128 + 128)
                    pa2, pk2 = A2(k * T + t0, k * T + t0 + TW)
                    pairs.append((wa, pa2))
                    rds.extend(pk2)
                mm(PR[:, 0:TW], pairs, rd=rds, wr=["PR"],
                   tag=f"{tg}_womm")
                st, stk = SB(2048, 2048 + TW)
                v("tensor_scalar", (st, PR[:, 0:TW], bcol(24 + oc), None,
                                    ALU.add),
                  rd=["PR"] + bia_k, wr=stk, tag=f"{tg}_wob")
                xa, xk = xap(oc, t0, t0 + TW)
                v("tensor_tensor", (xa, st, xa, ALU.add),
                  rd=stk + xk, wr=xk, tag=f"{tg}_wores")

        units = [(h, qt) for qt in range(NTT) for h in range(H1)]
        ui = 0
        while ui < len(units):
            pump_cb(2)
            pair = units[ui:ui + 2]
            gens = [att_unit(hh, qq, pz) for pz, (hh, qq) in enumerate(pair)]
            while gens:
                for g_ in list(gens):
                    try:
                        next(g_)
                    except StopIteration:
                        gens.remove(g_)
            ui += 2
            if ui % H1 == 0:
                qt_done = units[ui - 1][1]
                emit_wo(qt_done)
                if after_qt is not None:
                    after_qt(qt_done)

    def seg_table():
        segs = []
        pools = [(XB, 0, NCH * T), (A1, 0, NCH * T), (A4, 0, A4C),
                 (A3, ((NKI * TW + T - 1) // T) * T, NCH * T)]
        pi = 0
        off = pools[0][1]
        for i in range(16):
            while off + I_FFN > pools[pi][2]:
                pi += 1
                off = pools[pi][1]
            segs.append((pools[pi][0], off))
            off += I_FFN
        return segs

    SEGS = seg_table() if "l1f" in phases else None

    def l1f_tile(b, tt):
        tg = f"b{b}l1f"
        par = tt % 2
        ln_tile(tt, par, f"{tg}t{tt}")
        t0 = tt * TW
        for c in range(NCH):
            ya, yk = SRF(0, TW)
            y_emit(c, t0, t0 + TW, ya, yk, par, f"{tg}_y")
            qa, qk = A2(c * T + t0, c * T + t0 + TW)
            q4_part(ya, yk, qa, qk, TW, f"{tg}_qy")

    def phase_l1f(b, tiles_emitted=False):
        tg = f"b{b}l1f"
        if not tiles_emitted:
            for tt in range(NTT):
                l1f_tile(b, tt)
        d("gpsimd", ds_x.rearrange("(c p) t -> p c t", p=128),
          XB.h[:].rearrange("p (c t) -> p c t", c=NCH),
          "DSX", rd=XB.all()[1], wr=["DSXM"], tag=f"{tg}_spx")
        fence([], A4.all()[1] + [f"VT{kt}" for kt in range(NT128)],
              f"{tg}_fs")
        segk = {}
        for kc in range(16):
            buf, lo = SEGS[kc]
            ap, ukeys = buf(lo, lo + I_FFN)
            dsrc = ds_wg if kc < 8 else ds_wu
            kk = kc % 8
            d("sync", ap, dsrc[:, kk * 128:(kk + 1) * 128], f"WG{kc}",
              rd=["DSGM" if kc < 8 else "DSUM"], wr=ukeys, tag=f"{tg}_ldg",
              transpose=True)
            segk[kc] = ukeys

        def wg_tile(kc, n):
            buf, lo = SEGS[kc]
            return buf.h[:][:, lo + n * 128:lo + n * 128 + 128], segk[kc]

        def wu_tile(kc, n):
            buf, lo = SEGS[8 + kc]
            return buf.h[:][:, lo + n * 128:lo + n * 128 + 128], segk[8 + kc]

        for tq in range(NTT):
            t0 = tq * TW
            for n in range(NKI):
                gp, gk2 = [], []
                up_, uk2 = [], []
                yqkt = []
                for kc in range(NCH):
                    wt, wk_ = wg_tile(kc, n)
                    yq_ap, yq_k = A2(kc * T + t0, kc * T + t0 + TW)
                    gp.append((wt, yq_ap)); gk2.extend(wk_)
                    yqkt.extend(yq_k)
                    wt2, wk2_ = wu_tile(kc, n)
                    up_.append((wt2, yq_ap)); uk2.extend(wk2_)
                mm(PSB[0][:, 0:TW], gp, rd=gk2 + yqkt, wr=["PS0"], tag=f"{tg}_gmm")
                mm(PSB[1][:, 0:TW], up_, rd=uk2 + yqkt, wr=["PS1"], tag=f"{tg}_umm")
                sgt, sgtk = SRF(0, TW)
                act(sgt, PSB[0][:, 0:TW], AF.Silu, rd=["PS0"], wr=sgtk,
                    tag=f"{tg}_sil")
                v("tensor_tensor", (sgt, sgt, PSB[1][:, 0:TW], ALU.mult),
                  rd=sgtk + ["PS1"], wr=sgtk, tag=f"{tg}_h")
                ha, hk = A3(n * TW, (n + 1) * TW)
                q4_part(sgt, sgtk, ha, hk, TW, f"{tg}_qh")
            for hv in range(2):
                for kcp in range(NKI):
                    wda, wdk = WS((kcp % 2) * 1024, (kcp % 2) * 1024 + 1024)
                    d("sync", wda, ds_wd[:, kcp * 128:(kcp + 1) * 128],
                      f"WD{kcp % 2}", rd=["DSDM"], wr=wdk, tag=f"{tg}_ldd",
                      transpose=True)
                    hq_ap = A3(kcp * TW, (kcp + 1) * TW)[0]
                    hq_k = A3(kcp * TW, (kcp + 1) * TW)[1]
                    for t2 in range(2):
                        ts4 = hv * 2 + t2
                        lhs = hq_ap[:, ts4 * 128:ts4 * 128 + 128]
                        for odh in range(2):
                            ai = t2 * 2 + odh
                            mm(PSB[ai][:, 0:TW],
                               [(lhs, wda[:, odh * 512:odh * 512 + 512])],
                               rd=hq_k + wdk, wr=[f"PS{ai}"],
                               accumulate=(kcp > 0), stop=(kcp == NKI - 1),
                               skip_group_check=True, tag=f"{tg}_dmm")
                for t2 in range(2):
                    ts4 = hv * 2 + t2
                    xr, xrk = SB(0, 1024)
                    d("sync", xr, ds_x[:, t0 + ts4 * 128:t0 + ts4 * 128 + 128],
                      f"XR{ts4 % 2}", rd=["DSXM"], wr=xrk, tag=f"{tg}_ldxr",
                      transpose=True)
                    for odh in range(2):
                        ai = t2 * 2 + odh
                        stg = OST.h[:][:, 0:512]
                        stgk = ["OS0", "OS1"]
                        v("tensor_tensor", (stg, PSB[ai][:, 0:TW],
                                            xr[:, odh * 512:odh * 512 + 512],
                                            ALU.add),
                          rd=[f"PS{ai}"] + xrk,
                          wr=stgk, tag=f"{tg}_oadd")
                        d("sync", out[b, t0 + ts4 * 128:t0 + ts4 * 128 + 128,
                                      odh * 512:odh * 512 + 512],
                          stg, "OUT", rd=stgk, wr=["OUTM"], tag=f"{tg}_st")

    prep_jobs = []

    def make_prep_jobs():
        tg = "prep"
        for (wap, dsap, nrows, dkey) in ((w_gt, ds_wg, I_FFN, "DSG"),
                                         (w_up, ds_wu, I_FFN, "DSU"),
                                         (w_dn, ds_wd, D, "DSD")):
            wcols = 1024 if nrows == I_FFN else I_FFN
            for rc in range(nrows // 128):
                for lo in range(0, wcols, 512):
                    w_ = min(512, wcols - lo)

                    def job(wap=wap, dsap=dsap, dkey=dkey, rc=rc, lo=lo, w_=w_):
                        pka, pkk = PKF(0, w_)
                        d("gpsimd", pka,
                          wap[rc * 128:(rc + 1) * 128, lo:lo + w_],
                          "PKF", rd=[], wr=pkk, tag=f"{tg}_ldw")
                        outw, outwk = SCR(1024, 1024 + w_)
                        q4_free(pka, pkk, outw, outwk, w_, f"{tg}_wq",
                                slots=(SCR(0, w_), SCR(512, 512 + w_),
                                       QSC(0, w_ // 16), QSC(32, 32 + w_ // 16)))
                        d("gpsimd",
                          dsap[rc * 128:(rc + 1) * 128, lo:lo + w_],
                          outw, dkey, rd=outwk, wr=[f"{dkey}M"], tag=f"{tg}_stw")
                    prep_jobs.append(job)

    def prep_pump(n):
        for _ in range(min(n, len(prep_jobs))):
            prep_jobs.pop(0)()

    def prep_flush():
        prep_pump(len(prep_jobs))

    def phase_store(b):
        tg = f"b{b}st"
        for c in range(NCH):
            for tt in range(NTT):
                t0 = tt * TW
                stg = OST.h[:][:, 0:512]
                stgk = ["OS0", "OS1"]
                xa, xk = xap(c, t0, t0 + TW)
                v("tensor_copy", (stg, xa), rd=xk, wr=stgk, tag=f"{tg}_cp")
                d("sync", out[b, c * 128:(c + 1) * 128, t0:t0 + TW], stg, "OUT",
                  rd=stgk, wr=["OUTM"], tag=f"{tg}_st")

    if "l1f" in phases:
        make_prep_jobs()
        pump_holder[0] = prep_pump
        prep_pump(6)
    for b in range(nb):
        xk_all = XB.all()[1]
        d("gpsimd", XB.h[:].rearrange("p (c t) -> p c t", c=NCH),
          xin[b].rearrange("(c p) t -> p c t", p=128), "XIN",
          rd=xk_all, wr=xk_all, tag=f"b{b}_ldx")
        if "l0a" in phases:
            phase_l0a(b)
        if "l0f" in phases:
            phase_l0f(b)
        if "l1a" in phases:
            if "l1f" in phases:
                prep_flush()
                phase_l1a(b, after_qt=lambda qt, b=b: l1f_tile(b, qt))
            else:
                phase_l1a(b)
        if "l1f" in phases:
            if "l1a" not in phases:
                prep_flush()
                phase_l1f(b)
            else:
                phase_l1f(b, tiles_emitted=True)
        else:
            phase_store(b)

    S.op("sync", None, reads=["OUTM"], writes=[], tag="final_wait")

    stats = S.solve()
    S.emit(nc, ctx)
    ctx.close()
    return nc, stats


def _prep_inputs(inputs, T=2048, nb_per_core=2):
    f = np.ascontiguousarray
    x = inputs["L0_attn_hidden_states"]
    B = x.shape[0]
    ncores = B // nb_per_core
    xt = np.transpose(x, (0, 2, 1))
    common = {
        "cw": f(inputs["L0_attn_conv_w"].T.reshape(4, 8, 128).transpose(2, 1, 0)
                .reshape(128, 32)),
        "wq0": f(inputs["L0_attn_Wq"]),
        "wk0": f(inputs["L0_attn_Wk"]),
        "wv0": f(inputs["L0_attn_Wv"]),
        "wb0": f(inputs["L0_attn_Wb"]),
        "wg0": f(inputs["L0_attn_Wg"]),
        "wo0": f(inputs["L0_attn_Wo"]),
        "nw_t": f(np.broadcast_to(inputs["L0_attn_norm_w"], (128, 256))),
        "pk0": f(inputs["L0_ffn_packed_weight"].reshape(256, 1024)),
        "wsc0": f(inputs["L0_ffn_weight_scale"].reshape(1, 1)),
        "c_wsc_col": f(np.broadcast_to(inputs["L0_ffn_weight_scale"]
                                       .reshape(1, 1), (128, 1))),
        "bias0": f(inputs["L0_ffn_bias"].reshape(8, 128).T),
        "wq1": f(inputs["L1_attn_wq_w"].T),
        "bq1": f(inputs["L1_attn_wq_b"].reshape(8, 128).T),
        "wk1": f(inputs["L1_attn_wk_w"].T),
        "bk1": f(inputs["L1_attn_wk_b"].reshape(8, 128).T),
        "wv1": f(inputs["L1_attn_wv_w"].T),
        "bv1": f(inputs["L1_attn_wv_b"].reshape(8, 128).T),
        "wo1": f(inputs["L1_attn_wo_w"].T),
        "bo1": f(inputs["L1_attn_wo_b"].reshape(8, 128).T),
        "wgt": f(inputs["L1_ffn_w_gate"]),
        "wup": f(inputs["L1_ffn_w_up"]),
        "wdn": f(inputs["L1_ffn_w_down"]),
    }
    import ml_dtypes
    bf = ml_dtypes.bfloat16
    consts = {
        "c_id": np.eye(128, dtype=bf),
        "c_mneg_lo": np.where(np.arange(128)[None, :] < np.arange(128)[:, None],
                              -1.0, 0.0).astype(bf),
        "c_mneg_up": np.where(np.arange(128)[None, :] > np.arange(128)[:, None],
                              -1.0, 0.0).astype(bf),
        "c_m_upincl": np.where(np.arange(128)[None, :] >= np.arange(128)[:, None],
                               1.0, 0.0).astype(bf),
        "c_ones": np.ones((128, 1), dtype=bf),
        "c_ones_row": np.ones((1, 128), dtype=bf),
        "c_ones_mat": np.ones((128, 128), dtype=bf),
    }
    in_maps = []
    for core in range(ncores):
        m = {"xin": f(xt[core * nb_per_core:(core + 1) * nb_per_core])}
        m.update(common)
        m.update(consts)
        in_maps.append(m)
    return in_maps, ncores


def kernel(**inputs):
    T = inputs["L0_attn_hidden_states"].shape[1]
    in_maps, ncores = _prep_inputs(inputs, T=T)
    nc, stats = build(T=T, nb=2)
    res = run_bass_kernel_spmd(nc, in_maps, core_ids=list(range(ncores)))
    outs = [np.asarray(r["out"]) for r in res.results]
    full = np.concatenate(outs, axis=0)
    return np.ascontiguousarray(full.astype(np.float32))


# version 29 (final)
# speedup vs baseline: 1.5264x; 1.0171x over previous
import numpy as np
from contextlib import ExitStack

import concourse.bass as bass
import concourse.mybir as mybir
from concourse.bass_utils import run_bass_kernel_spmd

F32 = mybir.dt.float32
BF16 = mybir.dt.bfloat16
I32 = mybir.dt.int32
AF = mybir.ActivationFunctionType
ALU = mybir.AluOpType
AX = mybir.AxisListType

ENGINES = ("tensor", "vector", "scalar", "gpsimd", "sync")

D = 1024
NCH = 8
H0 = 4
DQH = 128
DVH = 256
CH = 128
H1 = 16
HD = 64
I_FFN = 2816
NKI = 22
LN_EPS = 1e-5
NORM_EPS = 1e-5


class Op:
    __slots__ = ("idx", "engine", "fn", "reads", "writes", "dma_buf", "tag",
                 "deps", "needs_inc", "tick", "waits", "snap", "est", "lat")


REORDER = True


class Sched:
    def __init__(self):
        self.ops = []
        self.dma_bufs = []

    def op(self, engine, fn, reads=(), writes=(), dma=None, tag="", est=400.0,
           lat=0.0):
        o = Op()
        o.idx = len(self.ops)
        o.engine = engine
        o.fn = fn
        o.reads = tuple(reads)
        o.writes = tuple(writes)
        o.dma_buf = dma
        o.tag = tag
        o.est = est
        o.lat = lat
        self.ops.append(o)
        if dma is not None and dma not in self.dma_bufs:
            self.dma_bufs.append(dma)
        return o

    def _reorder(self):
        import heapq
        n = len(self.ops)
        succs = [[] for _ in range(n)]
        npred = [0] * n
        for o in self.ops:
            for d2 in o.deps:
                succs[d2].append(o.idx)
            npred[o.idx] = len(o.deps)
        cp = [0.0] * n
        for o in reversed(self.ops):
            i = o.idx
            m = 0.0
            for s2 in succs[i]:
                if cp[s2] > m:
                    m = cp[s2]
            cp[i] = m + o.est + o.lat
        avail = [0.0] * n
        done = [0.0] * n
        ready = {e: [] for e in ENGINES}
        for o in self.ops:
            if npred[o.idx] == 0:
                heapq.heappush(ready[o.engine], (-cp[o.idx], o.idx))
        free = {e: 0.0 for e in ENGINES}
        order = []
        remaining = n
        while remaining:
            best = None
            for e in ENGINES:
                h = ready[e]
                if not h:
                    continue
                tmp = []
                cand = None
                for _ in range(8):
                    if not h:
                        break
                    negcp, i = heapq.heappop(h)
                    st = free[e] if free[e] > avail[i] else avail[i]
                    k = (st, negcp, i)
                    if cand is None or k < cand:
                        cand = k
                    tmp.append((negcp, i))
                for it in tmp:
                    heapq.heappush(h, it)
                if best is None or cand[:2] < best[0][:2]:
                    best = (cand, e)
            (st, negcp, i), e = best
            ready[e].remove((negcp, i))
            heapq.heapify(ready[e])
            o = self.ops[i]
            fin = st + o.est
            free[e] = fin
            done[i] = fin + o.lat
            order.append(o)
            remaining -= 1
            for s2 in succs[i]:
                npred[s2] -= 1
                if avail[s2] < done[i]:
                    avail[s2] = done[i]
                if npred[s2] == 0:
                    heapq.heappush(ready[self.ops[s2].engine], (-cp[s2], s2))
        self.makespan = max(free.values())
        return order

    def solve(self):
        last_writer = {}
        readers = {}
        for o in self.ops:
            deps = set()
            for b in o.reads:
                w = last_writer.get(b)
                if w is not None:
                    deps.add(w)
            for b in o.writes:
                w = last_writer.get(b)
                if w is not None:
                    deps.add(w)
                deps.update(readers.get(b, ()))
            deps.discard(o.idx)
            o.deps = deps
            for b in o.reads:
                readers.setdefault(b, []).append(o.idx)
            for b in o.writes:
                last_writer[b] = o.idx
                readers[b] = []
        for o in self.ops:
            o.needs_inc = False
        for o in self.ops:
            for d in o.deps:
                self.ops[d].needs_inc = True
        self.makespan = 0.0
        self.order = self._reorder() if REORDER else list(self.ops)
        etick = {e: 0 for e in ENGINES}
        dcum = {}
        for o in self.order:
            if o.dma_buf is not None:
                o.needs_inc = True
                dcum[o.dma_buf] = dcum.get(o.dma_buf, 0) + 16
                o.tick = ("D", o.dma_buf, dcum[o.dma_buf])
            elif o.needs_inc:
                etick[o.engine] += 1
                o.tick = ("E", o.engine, etick[o.engine])
            else:
                o.tick = None
        known = {e: {} for e in ENGINES}
        for o in self.order:
            need = {}
            for d in o.deps:
                t = self.ops[d].tick
                key = (t[0], t[1])
                if need.get(key, (-1, None))[0] < t[2]:
                    need[key] = (t[2], d)
            k = known[o.engine]
            waits = []
            for key in sorted(need):
                v, d = need[key]
                if k.get(key, -1) >= v:
                    continue
                waits.append((key, v))
                for sk, sv in self.ops[d].snap.items():
                    if k.get(sk, -1) < sv:
                        k[sk] = sv
                if k.get(key, -1) < v:
                    k[key] = v
            o.waits = waits
            if o.tick is not None:
                snap = dict(k)
                snap[(o.tick[0], o.tick[1])] = o.tick[2]
                o.snap = snap
            else:
                o.snap = None
        n_waits = sum(len(o.waits) for o in self.ops)
        n_incs = sum(1 for o in self.ops if o.tick is not None)
        return {"ops": len(self.ops), "waits": n_waits, "incs": n_incs,
                "dma_bufs": len(self.dma_bufs),
                "makespan_us": round(self.makespan / 1000.0, 1)}

    def emit(self, nc, ctx):
        sems = {}
        for e in ENGINES:
            sems[("E", e)] = ctx.enter_context(nc.semaphore(f"se_{e}"))
        for b in self.dma_bufs:
            sems[("D", b)] = ctx.enter_context(nc.semaphore(f"sd_{b}"[:28]))
        block = ctx.enter_context(nc.Block())
        by_engine = {e: [] for e in ENGINES}
        for o in self.order:
            by_engine[o.engine].append(o)

        def run_engine(eng_ops):
            def body(eng):
                for o in eng_ops:
                    for (key, v) in o.waits:
                        eng.wait_ge(sems[key], v)
                    if o.fn is None:
                        continue
                    inst = o.fn(eng)
                    if o.tick is not None:
                        assert inst is not None, f"op {o.tag} needs inc but fn returned None"
                        if o.tick[0] == "D":
                            inst.then_inc(sems[("D", o.tick[1])], 16)
                        else:
                            inst.then_inc(sems[("E", o.engine)], 1)
            return body

        block.tensor(run_engine(by_engine["tensor"]))
        block.vector(run_engine(by_engine["vector"]))
        block.scalar(run_engine(by_engine["scalar"]))
        block.gpsimd(run_engine(by_engine["gpsimd"]))
        block.sync(run_engine(by_engine["sync"]))


def _fsz(ap, default=512):
    try:
        sh = ap.shape
        n = 1
        for d2 in sh[1:]:
            n *= int(d2)
        return n
    except Exception:
        return default


class Bld:

    def __init__(self, nc, S, T):
        self.nc = nc
        self.S = S
        self.T = T
        self.t = {}
        self.uid = 0

    def u(self):
        self.uid += 1
        return self.uid

    def mm(self, out, pairs, rd, wr, tag="mm", skip_group_check=False,
           accumulate=False, stop=True):
        def f(e, out=out, pairs=pairs, acc=accumulate, sgc=skip_group_check,
              stop=stop):
            inst = None
            n = len(pairs)
            for i, (l, r) in enumerate(pairs):
                inst = e.matmul(out, l, r, start=(i == 0 and not acc),
                                stop=(stop and i == n - 1),
                                skip_group_check=sgc)
            return inst
        est = 0.0
        for (l, r) in pairs:
            fs = _fsz(r)
            try:
                narrow = l.shape[0] < 100 or _fsz(l) < 100
            except Exception:
                narrow = False
            est += fs * (0.72 if narrow else 0.44) + 80.0
        return self.S.op("tensor", f, reads=rd, writes=wr, tag=tag, est=est)

    def tp(self, outs_ins, ident, rd, wr, tag="tp"):
        def f(e, oi=outs_ins, ident=ident):
            inst = None
            for o, i in oi:
                inst = e.transpose(o, i, ident)
            return inst
        est = sum(_fsz(i) * 0.42 + 80.0 for (_, i) in outs_ins)
        return self.S.op("tensor", f, reads=rd, writes=wr, tag=tag, est=est)

    def vop(self, name, args, rd, wr, tag="v", eng="vector", kw=None):
        def f(e, name=name, args=args, kw=kw or {}):
            return getattr(e, name)(*args, **kw)
        fs = 1
        for a in args:
            s = _fsz(a, 0) if not isinstance(a, (int, float, type(None),
                                                 mybir.AluOpType)) else 0
            if s > fs:
                fs = s
        est = 170.0 + fs * (0.85 if eng == "vector" else 1.3)
        return self.S.op(eng, f, reads=rd, writes=wr, tag=tag, est=est)

    def act(self, out, in_, func, rd, wr, bias=0.0, scale=1.0, tag="act"):
        def f(e, out=out, in_=in_, func=func, bias=bias, scale=scale):
            return e.activation(out=out, in_=in_, func=func, bias=bias,
                                scale=scale)
        est = 170.0 + _fsz(in_) * 1.0
        return self.S.op("scalar", f, reads=rd, writes=wr, tag=tag, est=est)

    def dma(self, eng, out, in_, dmabuf, rd, wr, tag="dma", transpose=False):
        def f(e, out=out, in_=in_):
            if transpose:
                return e.dma_start_transpose(out, in_)
            return e.dma_start(out=out, in_=in_)
        lat = 1300.0 + _fsz(out) * 128 * 2 * 0.01
        return self.S.op(eng, f, reads=rd, writes=wr, dma=dmabuf, tag=tag,
                         est=150.0, lat=lat)


def build(T=2048, phases=("l0a", "l0f", "l1a", "l1f"), nb=2):
    assert T % 512 == 0
    nc = bass.Bass("TRN2", target_bir_lowering=False, debug=False)
    S = Sched()
    bld = Bld(nc, S, T)
    TW = 512
    NTT = T // TW
    NT128 = T // 128
    NCHK = T // CH

    def din(name, shape, dtype=F32):
        return nc.dram_tensor(name, list(shape), dtype, kind="ExternalInput").ap()

    xin = din("xin", (nb, D, T))
    w_cw = din("cw", (128, 32))
    w_q0 = din("wq0", (D, 512))
    w_k0 = din("wk0", (D, 512))
    w_v0 = din("wv0", (D, D))
    w_b0 = din("wb0", (D, 4))
    w_g0 = din("wg0", (D, D))
    w_o0 = din("wo0", (D, D))
    w_nw = din("nw_t", (128, DVH))
    pk0 = din("pk0", (256, 1024), I32)
    wsc0 = din("wsc0", (1, 1))
    bias0 = din("bias0", (128, NCH))
    w_q1 = din("wq1", (D, D)); b_q1 = din("bq1", (128, NCH))
    w_k1 = din("wk1", (D, D)); b_k1 = din("bk1", (128, NCH))
    w_v1 = din("wv1", (D, D)); b_v1 = din("bv1", (128, NCH))
    w_o1 = din("wo1", (D, D)); b_o1 = din("bo1", (128, NCH))
    w_gt = din("wgt", (I_FFN, D))
    w_up = din("wup", (I_FFN, D))
    w_dn = din("wdn", (D, I_FFN))
    c_id = din("c_id", (128, 128), BF16)
    c_mneg_lo = din("c_mneg_lo", (128, 128), BF16)
    c_mneg_up = din("c_mneg_up", (128, 128), BF16)
    c_m_upincl = din("c_m_upincl", (128, 128), BF16)
    c_ones = din("c_ones", (128, 1), BF16)
    c_ones_row = din("c_ones_row", (1, 128), BF16)
    c_ones_mat = din("c_ones_mat", (128, 128), BF16)
    c_wsc_col = din("c_wsc_col", (128, 1))

    tok_major_out = "l1f" in phases
    out_shape = [nb, T, D] if tok_major_out else [nb, D, T]
    out = nc.dram_tensor("out", out_shape, F32, kind="ExternalOutput").ap()

    ds_tern = nc.dram_tensor("ds_tern", [D, D], BF16, kind="Internal").ap()
    ds_wg = nc.dram_tensor("ds_wg", [I_FFN, D], BF16, kind="Internal").ap()
    ds_wu = nc.dram_tensor("ds_wu", [I_FFN, D], BF16, kind="Internal").ap()
    ds_wd = nc.dram_tensor("ds_wd", [D, I_FFN], BF16, kind="Internal").ap()
    ds_x = nc.dram_tensor("ds_x", [D, T], BF16, kind="Internal").ap()

    ctx = ExitStack()
    E = ctx.enter_context

    class Buf:
        def __init__(self, name, cols, dtype=BF16, unit=128, parts=128):
            self.h = E(nc.sbuf_tensor(name, [parts, cols], dtype))
            self.name = name
            self.unit = unit
            self.cols = cols

        def __call__(self, lo, hi, p0=0, p1=None):
            if p1 is None:
                p1 = self.h[:].shape[0]
            ap = self.h[:][p0:p1, lo:hi]
            keys = [f"{self.name}{i}" for i in range(lo // self.unit,
                                                     (hi - 1) // self.unit + 1)]
            return ap, keys

        def all(self):
            return self(0, self.cols)

    A4C = max(NCH * T, NT128 * 8 * 130)
    XB = Buf("XB", NCH * T, unit=TW)
    A1 = Buf("A1", NCH * T, unit=TW)
    A2 = Buf("A2", NCH * T, unit=TW)
    A3 = Buf("A3", NCH * T, unit=TW)
    A4 = Buf("A4", A4C, unit=T)
    WS = Buf("WS", 8192, unit=1024)
    SB = Buf("SB", 3072)
    ABT = Buf("AB", 2048)
    SCR = Buf("SR", 1664)
    S0B = Buf("S0", 4 * DVH, unit=DVH)
    CST = Buf("CS", 512, unit=512)
    NWT = Buf("NW", DVH, unit=DVH)
    CWT = Buf("CW", 32, F32, unit=32)
    ON2 = Buf("O2", 128, unit=128)
    SRA = Buf("RA", 512, unit=512)
    SRB = Buf("RB", 512, unit=512)
    SRC = Buf("RC", 512)
    SRF = Buf("RF", 512, F32, unit=512)
    OST = Buf("OS", 512, F32, unit=256)
    BCL = Buf("BC", 66, F32, unit=1)
    QSC = Buf("QC", 64, F32, unit=32)
    ATB = Buf("AT", 2048, unit=1024)
    BIA = Buf("BI", 48, F32, unit=48)
    ECL = Buf("EC", 16, F32, unit=16)
    PKF = Buf("PK", 512, F32, unit=512)

    PSQ = E(nc.psum_tensor("PSQ", [128, 2048], F32))
    PS4 = E(nc.psum_tensor("PS4", [128, 512], F32))
    PS5 = E(nc.psum_tensor("PS5", [128, 512], F32))
    PSB = [PSQ[:][:, i * 512:(i + 1) * 512] for i in range(4)] + [PS4[:], PS5[:]]
    PT = E(nc.psum_tensor("PT", [128, 1024], BF16))
    PR = E(nc.psum_tensor("PR", [128, 512], F32))
    SPS = [PSB[2], PSB[3], PSB[4], PSB[5]]

    ident, k_id = CST(0, 128)
    m_neg_lo, k_mlo = CST(128, 256)
    m_neg_up, k_mup = CST(256, 384)
    m_upincl, k_mui = CST(384, 512)
    ones_c, k_on = ON2.h[:][:, 0:1], ["O20"]

    def ones_row(p):
        return ON2.h[:][p:p + 1, 0:128]

    def YT(lo, hi):
        return SB(2048 + lo, 2048 + hi)

    d = bld.dma
    v = bld.vop
    mm = bld.mm
    tp = bld.tp
    act = bld.act

    def op2(eng, name1, a1, name2, a2, rd_, wr_, tag_):
        def f(e, a1=a1, a2=a2):
            getattr(e, name1)(*a1)
            return getattr(e, name2)(*a2)
        return S.op(eng, f, reads=rd_, writes=wr_, tag=tag_)

    def mm2(o1, p1, o2, p2, rd_, wr_, tag_):
        def f(e, o1=o1, p1=p1, o2=o2, p2=p2):
            e.matmul(o1, p1[0], p1[1], start=True, stop=True)
            return e.matmul(o2, p2[0], p2[1], start=True, stop=True)
        est = (_fsz(p1[1]) + _fsz(p2[1])) * 0.5 + 160.0
        return S.op("tensor", f, reads=rd_, writes=wr_, tag=tag_, est=est)

    def fence(rd_, wr_, tag="fence"):
        def f(e):
            return e.engine_nop()
        return S.op("vector", f, reads=rd_, writes=wr_, tag=tag)

    d("sync", ident, c_id, "cst", rd=[], wr=k_id, tag="ldid")
    d("sync", m_neg_lo, c_mneg_lo, "cst", rd=[], wr=k_mlo)
    d("sync", m_neg_up, c_mneg_up, "cst", rd=[], wr=k_mup)
    d("sync", m_upincl, c_m_upincl, "cst", rd=[], wr=k_mui)
    d("sync", ON2.h[:], c_ones_mat, "on2", rd=[], wr=["O20"])
    d("gpsimd", NWT.all()[0], w_nw, "nwt", rd=[], wr=NWT.all()[1])
    d("sync", CWT.all()[0], w_cw, "cwt", rd=[], wr=CWT.all()[1])
    bia_ap, bia_k = BIA(0, 48)
    d("sync", BIA(0, 8)[0], b_q1, "bia", rd=[], wr=bia_k)
    d("sync", BIA(8, 16)[0], b_k1, "bia", rd=[], wr=bia_k)
    d("sync", BIA(16, 24)[0], b_v1, "bia", rd=[], wr=bia_k)
    d("sync", BIA(24, 32)[0], b_o1, "bia", rd=[], wr=bia_k)
    d("sync", BIA(32, 40)[0], bias0, "bia", rd=[], wr=bia_k)
    d("sync", BIA(40, 41)[0], c_wsc_col, "bia", rd=[], wr=bia_k, tag="wscb")

    def bcol(i):
        return BIA.h[:][:, i:i + 1]

    v("memset", (ECL.h[:][:, 0:1], LN_EPS), rd=[], wr=["EC0"], tag="eps")
    v("memset", (ECL.h[:][:, 1:2], 1e-30), rd=[], wr=["EC0"], tag="tiny")
    v("memset", (ECL.h[:][:, 2:3], 0.5), rd=[], wr=["EC0"], tag="half")
    eps_col = ECL.h[:][:, 0:1]
    tiny_col = ECL.h[:][:, 1:2]
    half_col = ECL.h[:][:, 2:3]
    GRID_MID = [0.25, 0.75, 1.25, 1.75, 2.5, 3.5, 5.0]
    for t_, m_ in enumerate(GRID_MID):
        v("memset", (ECL.h[:][:, 3 + t_:4 + t_], -m_), rd=[], wr=["EC0"],
          tag="negm")

    def negm_col(t_):
        return ECL.h[:][:, 3 + t_:4 + t_]

    def xap(c, lo, hi):
        return XB(c * T + lo, c * T + hi)

    pump_holder = [lambda n: None]

    def pump_cb(n):
        pump_holder[0](n)

    def ln_tile(tt, par, tag):
        t0 = tt * TW
        PTF = PR
        xr, xk = [], []
        for c in range(NCH):
            a, k = xap(c, t0, t0 + TW)
            xr.append(a); xk.extend(k)
        mm(PTF[0:1, 0:TW], [(ones_c, a) for a in xr],
           rd=["O20"] + xk, wr=["PR"], tag=f"{tag}_sum{tt}")
        mrow = SRB.h[:][0:1, 0:TW]
        v("tensor_scalar", (mrow, PTF[0:1, 0:TW], 1.0 / D, None, ALU.mult),
          rd=["PR"], wr=["RBL"], tag=f"{tag}_m")
        for c in range(NCH):
            sl, slk = S0B((c % 2) * 512, (c % 2) * 512 + TW)
            a, k = xap(c, t0, t0 + TW)
            act(sl, a, AF.Square, rd=k, wr=slk, tag=f"{tag}_sq")
            mm(PTF[0:1, 0:TW], [(ones_c, sl)], rd=["O20"] + slk,
               wr=["PR"], accumulate=(c > 0), stop=(c == NCH - 1),
               tag=f"{tag}_ssmm")
        arow = SRA.h[:][0:1, 0:TW]
        v("tensor_tensor", (arow, mrow, mrow, ALU.mult),
          rd=["RBL"], wr=["RA0"], tag=f"{tag}_msq")
        v("scalar_tensor_tensor", (arow, PTF[0:1, 0:TW],
                                   1.0 / D, arow, ALU.mult,
                                   ALU.subtract),
          rd=["PR", "RA0"], wr=["RA0"], tag=f"{tag}_var")
        act(arow, arow, AF.Ln,
            rd=["RA0", "EC0"], wr=["RA0"], bias=ECL.h[:][0:1, 0:1],
            tag=f"{tag}_a1")
        act(arow, arow, AF.Exp, rd=["RA0"], wr=["RA0"], scale=-0.5,
            tag=f"{tag}_a2")
        brow = SRB.h[:][0:1, 0:TW]
        v("scalar_tensor_tensor", (brow, mrow, -1.0, arow, ALU.mult, ALU.mult),
          rd=["RBL", "RA0"], wr=["RBL"], tag=f"{tag}_b")
        aslot, ak = ABT(par * TW, par * TW + TW)
        bslot, bk = ABT(1024 + par * TW, 1024 + par * TW + TW)
        mm(PTF[:, 0:TW], [(ones_row(0), arow)],
           rd=["O20", "RA0"], wr=["PR"], tag=f"{tag}_bca")
        v("tensor_copy", (aslot, PTF[:, 0:TW]), rd=["PR"], wr=ak,
          tag=f"{tag}_eva")
        mm(PTF[:, 0:TW], [(ones_row(0), brow)],
           rd=["O20", "RBL"], wr=["PR"], tag=f"{tag}_bcb")
        v("tensor_copy", (bslot, PTF[:, 0:TW]), rd=["PR"], wr=bk,
          tag=f"{tag}_evb")

    def y_emit(c, lo, hi, outap, outkeys, par, tag):
        t0 = (lo // TW) * TW
        a_ap, ak = ABT(par * TW + (lo - t0), par * TW + (hi - t0))
        b_ap, bk = ABT(1024 + par * TW + (lo - t0), 1024 + par * TW + (hi - t0))
        xa, xk = xap(c, lo, hi)
        v("tensor_tensor", (outap, xa, a_ap, ALU.mult),
          rd=xk + ak, wr=outkeys, tag=f"{tag}_y1")
        v("tensor_tensor", (outap, outap, b_ap, ALU.add),
          rd=outkeys + bk, wr=outkeys, tag=f"{tag}_y2")

    def phase_l0a(b):
        tg = f"b{b}l0a"
        for tt in range(NTT):
            par = tt % 2
            ln_tile(tt, par, f"{tg}t{tt}")
            t0 = tt * TW
            for c in range(NCH):
                yt_m, yt_k = YT(3, 3 + TW)
                if tt == 0:
                    z3, _ = YT(0, 3)
                    v("memset", (z3, 0.0), rd=[], wr=yt_k, tag=f"{tg}_yz")
                else:
                    h3, _ = YT(0, 3)
                    y_emit(c, t0 - 3, t0, h3, yt_k, 1 - par, f"{tg}_yh")
                y_emit(c, t0, t0 + TW, yt_m, yt_k, par, f"{tg}_ym")
                ht, htk = SB((c % 2) * 512, (c % 2) * 512 + TW)
                ytf = SB.h[:][:, 2048:2568]
                v("tensor_scalar", (ht, ytf[:, 0:TW], CWT.h[:][:, 4 * c:4 * c + 1],
                                    None, ALU.mult),
                  rd=yt_k + ["CW0"], wr=htk, tag=f"{tg}_cv0")
                for i in (1, 2, 3):
                    v("scalar_tensor_tensor",
                      (ht, ytf[:, i:i + TW], CWT.h[:][:, 4 * c + i:4 * c + i + 1],
                       ht, ALU.mult, ALU.add),
                      rd=yt_k + ["CW0"] + htk, wr=htk, tag=f"{tg}_cv{i}")
                ha, hk = A1(c * T + t0, c * T + t0 + TW)
                act(ha, ht, AF.Silu, rd=htk, wr=hk, tag=f"{tg}_silu")

        def hap(k, lo, hi):
            return A1(k * T + lo, k * T + hi)


        def load_ws(dst_lo, wap, outw, key_tag):
            ap, keys = WS(dst_lo, dst_lo + NCH * outw)
            d("gpsimd", ap.rearrange("p (k n) -> p k n", k=NCH),
              wap.rearrange("(k p) n -> p k n", p=128), key_tag,
              rd=[], wr=keys, tag=f"ld_{key_tag}")
            return keys

        wq_k = load_ws(0, w_q0, 512, f"{tg}wq")
        wk_k = load_ws(4096, w_k0, 512, f"{tg}wk")

        def qk_unit(which, wlo, wkeys, dst_lo, hd, tt, pz):
            t0 = tt * TW
            pb, pk = (PSB[2 * pz], f"PS{2 * pz}")
            rowb, rowbk = (PSB[4], "PS4") if pz == 0 else (PSB[5], "PS5")
            pairs = []
            rds = list(wkeys)
            for k in range(NCH):
                wa, _ = WS(wlo + k * 512 + hd * 128, wlo + k * 512 + hd * 128 + 128)
                ha, hk2 = hap(k, t0, t0 + TW)
                pairs.append((wa, ha))
                rds.extend(hk2)
            mm(pb[:, 0:TW], pairs, rd=rds, wr=[pk], tag=f"{tg}_{which}mm")
            yield
            sq, sqk = SB(1024 + pz * 512, 1024 + pz * 512 + TW)
            act(sq, pb[:, 0:TW], AF.Square, rd=[pk], wr=sqk,
                tag=f"{tg}_{which}sq")
            yield
            mm(rowb[0:1, 0:TW], [(ones_c, sq)], rd=["O20"] + sqk,
               wr=[rowbk], tag=f"{tg}_{which}ss")
            yield
            rrow = SRC.h[:][64:65, 0:TW] if pz == 0 else SRB.h[:][64:65, 0:TW]
            rrk = ["RC0", "RC1", "RC2", "RC3"] if pz == 0 else ["RB0"]
            act(rrow, rowb[0:1, 0:TW], AF.Ln, bias=ECL.h[:][0:1, 1:2],
                rd=[rowbk, "EC0"], wr=rrk, tag=f"{tg}_{which}rn")
            yield
            act(rrow, rrow, AF.Exp, rd=rrk, wr=rrk, scale=-0.5,
                tag=f"{tg}_{which}rn2")
            yield
            v("tensor_scalar", (rrow, rrow, 1e12, None, ALU.min),
              rd=rrk, wr=rrk, tag=f"{tg}_{which}cl")
            yield
            mm(rowb[:, 0:TW], [(ones_row(64), rrow)], rd=["O20"] + rrk,
               wr=[rowbk], tag=f"{tg}_{which}bc")
            yield
            rnb, rnbk = SB(0, TW) if pz == 0 else SB(512, 512 + TW)
            v("tensor_copy", (rnb, rowb[:, 0:TW]), rd=[rowbk], wr=rnbk,
              tag=f"{tg}_{which}be")
            yield
            da, dk = A4(dst_lo + hd * T + t0, dst_lo + hd * T + t0 + TW)
            v("tensor_tensor", (da, pb[:, 0:TW], rnb, ALU.mult),
              rd=[pk] + rnbk, wr=dk, tag=f"{tg}_{which}ev")
            yield

        def qk_proj(which, wlo, wkeys, dst_lo):
            units = [(hd, tt) for hd in range(H0) for tt in range(NTT)]
            ui = 0
            while ui < len(units):
                pair = units[ui:ui + 2]
                gens = [qk_unit(which, wlo, wkeys, dst_lo, hh, tt_, pz)
                        for pz, (hh, tt_) in enumerate(pair)]
                while gens:
                    for g_ in list(gens):
                        try:
                            next(g_)
                        except StopIteration:
                            gens.remove(g_)
                ui += 2

        qk_proj("q", 0, wq_k, 0)
        pump_cb(12)
        qk_proj("k", 4096, wk_k, 4 * T)
        pump_cb(12)

        wb_sl, wb_k = SB(2048, 2048 + 32)
        d("gpsimd", wb_sl.rearrange("p (k n) -> p k n", k=NCH),
          w_b0.rearrange("(k p) n -> p k n", p=128), f"{tg}wb", rd=[], wr=wb_k,
          tag=f"{tg}_ldwb")
        for tt in range(NTT):
            t0 = tt * TW
            pairs = []
            rds = list(wb_k)
            for k in range(NCH):
                ha, hk2 = hap(k, t0, t0 + TW)
                pairs.append((wb_sl[:, k * 4:k * 4 + 4], ha))
                rds.extend(hk2)
            mm(PSB[0][0:4, 0:TW], pairs, rd=rds, wr=["PS0"], tag=f"{tg}_bmm")
            brows = SRC.h[:][0:4, 0:TW]
            brk = ["RC0", "RC1", "RC2", "RC3"]
            act(brows, PSB[0][0:4, 0:TW], AF.Sigmoid, rd=["PS0"], wr=brk,
                tag=f"{tg}_bsig")
            tp([(PT[:, j * 4:j * 4 + 4], SRC.h[:][0:4, j * 128:j * 128 + 128])
                for j in range(4)], ident[0:4, 0:4], rd=brk + k_id, wr=["PT"],
               tag=f"{tg}_btp")
            bc_ap, bc_k = BCL(tt * 16, tt * 16 + 16)
            v("tensor_copy", (bc_ap, PT[:, 0:16]), rd=["PT"], wr=bc_k,
              tag=f"{tg}_bev")

        def beta_col(hd, c):
            i = (c // 4) * 16 + (c % 4) * 4 + hd
            return BCL.h[:][:, i:i + 1], [f"BC{i}"]

        wv0_k = load_ws(0, w_v0[:, 0:512], 512, f"{tg}wv0")
        wv1_k = load_ws(4096, w_v0[:, 512:1024], 512, f"{tg}wv1")

        def v_unit(vc, tt, pz):
            wlo = 0 if vc < 4 else 4096
            wof = (vc % 4) * 128
            wkeys = wv0_k if vc < 4 else wv1_k
            hd, dvp = vc // 2, vc % 2
            t0 = tt * TW
            pb, pk = (PSB[2 * pz], f"PS{2 * pz}")
            pairs = []
            rds = list(wkeys)
            for k in range(NCH):
                wa, _ = WS(wlo + k * 512 + wof, wlo + k * 512 + wof + 128)
                ha, hk2 = hap(k, t0, t0 + TW)
                pairs.append((wa, ha))
                rds.extend(hk2)
            mm(pb[:, 0:TW], pairs, rd=rds, wr=[pk], tag=f"{tg}_vmm")
            yield
            vt, vtk = SB(pz * 512, pz * 512 + TW)
            v("copy", (vt, pb[:, 0:TW]), rd=[pk], wr=vtk, tag=f"{tg}_vev",
              eng="scalar")
            yield
            tp([(PT[:, j * 128:j * 128 + 128], vt[:, j * 128:j * 128 + 128])
                for j in range(4)], ident, rd=vtk + ["PT"] + k_id, wr=["PT"],
               tag=f"{tg}_vtp")
            for j in range(4):
                c = tt * 4 + j
                dlo = hd * (NCHK * DVH) + c * DVH + dvp * 128
                da, dk = A3(dlo, dlo + 128)
                bca, bck = beta_col(hd, c)
                v("tensor_scalar", (da, PT[:, j * 128:j * 128 + 128],
                                    bca, None, ALU.mult),
                  rd=["PT"] + bck, wr=dk, tag=f"{tg}_vbt")
            yield

        units_v = [(vc, tt) for vc in range(NCH) for tt in range(NTT)]
        ui = 0
        while ui < len(units_v):
            pairu = units_v[ui:ui + 2]
            gens = [v_unit(vv, tt_, pz) for pz, (vv, tt_) in enumerate(pairu)]
            while gens:
                for g_ in list(gens):
                    try:
                        next(g_)
                    except StopIteration:
                        gens.remove(g_)
            ui += 2

        wg0_k = load_ws(0, w_g0[:, 0:512], 512, f"{tg}wg0")
        wg1_k = load_ws(4096, w_g0[:, 512:1024], 512, f"{tg}wg1")

        def g_unit(j, half, pz):
            tj = j * 128
            wlo, wkeys = (0, wg0_k) if half == 0 else (4096, wg1_k)
            pb, pk = (PSB[2 * pz], f"PS{2 * pz}")
            pairs = []
            rds = list(wkeys)
            for k in range(NCH):
                ha, hk2 = hap(k, tj, tj + 128)
                wa, _ = WS(wlo + k * 512, wlo + k * 512 + 512)
                pairs.append((ha, wa))
                rds.extend(hk2)
            mm(pb[:, 0:TW], pairs, rd=rds, wr=[pk], tag=f"{tg}_gmm")
            yield
            da, dk = A2(j * 1024 + half * 512, j * 1024 + half * 512 + TW)
            v("copy", (da, pb[:, 0:TW]), rd=[pk], wr=dk, tag=f"{tg}_gev",
              eng="scalar")
            yield

        units_g = [(j, half) for j in range(NT128) for half in range(2)]
        ui = 0
        while ui < len(units_g):
            pairu = units_g[ui:ui + 2]
            gens = [g_unit(jj, hh_, pz) for pz, (jj, hh_) in enumerate(pairu)]
            while gens:
                for g_ in list(gens):
                    try:
                        next(g_)
                    except StopIteration:
                        gens.remove(g_)
            ui += 2

        wo0_k = load_ws(0, w_o0[:, 0:512], 512, f"{tg}wo0")
        wo1_k = load_ws(4096, w_o0[:, 512:1024], 512, f"{tg}wo1")

        def sl(hp, lo, w):
            return (SCR if hp == 0 else ATB)(lo, lo + w)

        nlev = 0
        while (1 << (nlev + 1)) < CH:
            nlev += 1

        def tbuild_gen(pair, hp, c):
            hd = pair * 2 + hp
            par = c % 2
            t0 = c * 128
            kof = (4 + hd) * T
            kkeyh = [f"A4{kof // T}"]
            kf = A4(kof + t0, kof + t0 + 128)[0]
            Tb = PSB[hp]
            tk = f"PS{hp}"
            ptr = 0 if hp == 0 else 512
            ptk = "PT"
            bca, bck = beta_col(hd, c)
            NP, NPk = sl(hp, par * 256, 256)
            N_ = NP[:, 0:128]
            Nt_ = NP[:, 128:256]
            TP, TPk = sl(hp, 512 + par * 256, 256)
            T_ = TP[:, 0:128]
            Tt_ = TP[:, 128:256]
            mm(Tb[:, 0:128], [(kf, kf)], rd=kkeyh, wr=[tk], tag=f"{tg}_Gmm")
            yield
            v("tensor_scalar", (N_, Tb[:, 0:128], bca, None, ALU.mult),
              rd=[tk] + bck, wr=NPk, tag=f"{tg}_Nb")
            yield
            v("tensor_tensor", (N_, N_, m_neg_lo, ALU.mult),
              rd=NPk + k_mlo, wr=NPk, tag=f"{tg}_Nm")
            yield
            tp([(PT[:, ptr:ptr + 128], N_)], ident,
               rd=NPk + ["PT"] + k_id, wr=["PT"], tag=f"{tg}_Nttp")
            v("copy", (Nt_, PT[:, ptr:ptr + 128]), rd=["PT"], wr=NPk,
              tag=f"{tg}_Ntev", eng="scalar")
            yield
            v("tensor_tensor", (TP, NP, ident[:, 0:128].broadcast_to([128, 256])
               if False else NP, ALU.bypass), rd=[], wr=[], tag="dead") \
                if False else None
            def f_ti(e, TP=TP, NP=NP):
                e.tensor_tensor(TP[:, 0:128], NP[:, 0:128], ident, ALU.add)
                return e.tensor_tensor(TP[:, 128:256], NP[:, 128:256], ident,
                                       ALU.add)
            S.op("vector", f_ti, reads=NPk + k_id, writes=TPk, tag=f"{tg}_Ti")
            yield
            xcur, xkcur = NP, NPk
            for j in range(1, nlev + 1):
                def f_sq(e, Tb=Tb, X=xcur):
                    e.matmul(Tb[:, 0:128], X[:, 128:256], X[:, 0:128],
                             start=True, stop=True, skip_group_check=True)
                    return e.matmul(Tb[:, 128:256], X[:, 0:128], X[:, 128:256],
                                    start=True, stop=True,
                                    skip_group_check=True)
                S.op("tensor", f_sq, reads=list(xkcur), writes=[tk],
                     tag=f"{tg}_sq{j}")
                yield
                v("copy", (NP, Tb[:, 0:256]), rd=[tk], wr=NPk,
                  tag=f"{tg}_sqe{j}", eng="scalar")
                yield
                def f_pr(e, Tb=Tb, X=xcur, TP=TP):
                    e.matmul(Tb[:, 0:128], X[:, 128:256], TP[:, 0:128],
                             start=True, stop=True, skip_group_check=True)
                    return e.matmul(Tb[:, 128:256], TP[:, 0:128], X[:, 128:256],
                                    start=True, stop=True,
                                    skip_group_check=True)
                S.op("tensor", f_pr, reads=list(xkcur) + list(TPk), writes=[tk],
                     tag=f"{tg}_pr{j}")
                yield
                v("tensor_tensor", (TP, Tb[:, 0:256], TP, ALU.add),
                  rd=[tk] + TPk, wr=TPk, tag=f"{tg}_pre{j}")
                yield

        def step_gen(pair, hp, c):
            hd = pair * 2 + hp
            par = c % 2
            t0 = c * 128
            qof, kof = hd * T, (4 + hd) * T
            qkeyh = [f"A4{qof // T}"]
            kkeyh = [f"A4{kof // T}"]
            kf = A4(kof + t0, kof + t0 + 128)[0]
            qf = A4(qof + t0, qof + t0 + 128)[0]
            Sb = PSB[2 + hp]
            sbk = f"PS{2 + hp}"
            Sps, Sk = (PSB[4], "PS4") if hp == 0 else (PSB[5], "PS5")
            TP, TPk = sl(hp, 512 + par * 256, 256)
            Tt_ = TP[:, 128:256]
            U_, Uk = sl(hp, 1024, 256)
            RP, RPk = sl(hp, 1280, 256)
            ktm, ktmk = sl(hp, 1536, 128)
            vlo = hd * (NCHK * DVH) + c * DVH
            vbt, vbk = A3(vlo, vlo + DVH)
            s0a, s0k = S0B(hd * DVH, (hd + 1) * DVH)
            bca, bck = beta_col(hd, c)
            ptr = 0 if hp == 0 else 512
            tp([(PT[:, ptr:ptr + 128], kf)], ident,
               rd=kkeyh + ["PT"] + k_id, wr=["PT"], tag=f"{tg}_ktp")
            v("copy", (ktm, PT[:, ptr:ptr + 128]), rd=["PT"], wr=list(ktmk),
              tag=f"{tg}_ktev", eng="scalar")
            yield
            if c == 0:
                Rap, Rkey = vbt, vbk
            else:
                mm(Sb[:, 0:DVH], [(kf, s0a)], rd=kkeyh + s0k, wr=[sbk],
                   tag=f"{tg}_kS")
                yield
                bn, bnk = BCL(64 + hp, 65 + hp)
                v("tensor_scalar", (bn, bca, -1.0, None, ALU.mult),
                  rd=bck, wr=bnk, tag=f"{tg}_bn")
                yield
                v("scalar_tensor_tensor", (RP[:, 0:DVH], Sb[:, 0:DVH], bn, vbt,
                                           ALU.mult, ALU.add),
                  rd=[sbk] + bnk + vbk, wr=RPk, tag=f"{tg}_R")
                yield
                Rap, Rkey = RP[:, 0:DVH], RPk
            mm(Sb[:, 0:DVH], [(Tt_, Rap)], rd=list(TPk) + list(Rkey), wr=[sbk],
               tag=f"{tg}_Umm")
            yield
            v("copy", (U_, Sb[:, 0:DVH]), rd=[sbk], wr=Uk,
              tag=f"{tg}_Uev", eng="scalar")
            yield
            Pm_ = RP[:, 0:128]
            mm(Sb[:, 0:128], [(kf, qf)], rd=kkeyh + qkeyh, wr=[sbk],
               tag=f"{tg}_Pmm")
            yield
            v("tensor_tensor", (Pm_, Sb[:, 0:128], m_upincl, ALU.mult),
              rd=[sbk] + k_mui, wr=RPk, tag=f"{tg}_Pev")
            yield
            if c == 0:
                pairs, rdo = [(Pm_, U_)], list(RPk) + list(Uk)
            else:
                pairs = [(qf, s0a), (Pm_, U_)]
                rdo = qkeyh + s0k + list(RPk) + list(Uk)
            mm(PR[:, 0:DVH], pairs, rd=rdo, wr=["PR"], tag=f"{tg}_omm")
            v("tensor_copy", (vbt, PR[:, 0:DVH]), rd=["PR"], wr=vbk,
              tag=f"{tg}_oev")
            yield
            mm(Sps[:, 0:DVH], [(ktm, U_)], rd=list(ktmk) + list(Uk),
               wr=[Sk], accumulate=(c > 0), stop=(c == NCHK - 1),
               skip_group_check=True, tag=f"{tg}_Smm")
            yield
            v("copy", (s0a, Sps[:, 0:DVH]), rd=[Sk], wr=s0k,
              tag=f"{tg}_Sev", eng="scalar")
            yield

        def head_gen(pair, hp):
            for c in range(NCHK + 1):
                if c < NCHK:
                    yield from tbuild_gen(pair, hp, c)
                if c >= 1:
                    yield from step_gen(pair, hp, c - 1)

        for pair in range(H0 // 2):
            gens = [head_gen(pair, 0), head_gen(pair, 1)]
            while gens:
                for g_ in list(gens):
                    try:
                        next(g_)
                    except StopIteration:
                        gens.remove(g_)

        for j in range(NT128):
            ga, gk = A2(j * 1024, (j + 1) * 1024)
            act(ga, ga, AF.Silu, rd=gk, wr=gk, tag=f"{tg}_esgb")
        units_e = [(hd, c) for hd in range(H0) for c in range(NCHK)]
        for gi in range(0, len(units_e), 8):
            grp = units_e[gi:gi + 8]
            tiles_ = []
            for ui, (hd, c) in enumerate(grp):
                vlo = hd * (NCHK * DVH) + c * DVH
                ot, ok = A3(vlo, vlo + DVH)
                sq, sqk = SB((ui % 4) * 512, (ui % 4) * 512 + DVH) if False \
                    else SB(ui * 256, (ui + 1) * 256)
                tiles_.append((hd, c, ot, ok, sq, sqk))
            for (hd, c, ot, ok, sq, sqk) in tiles_:
                act(sq, ot, AF.Square, rd=ok, wr=sqk, tag=f"{tg}_esq")
            for ui, (hd, c, ot, ok, sq, sqk) in enumerate(tiles_):
                col, colk = BCL(48 + 2 * (ui % 8), 49 + 2 * (ui % 8))
                v("reduce_sum", (col, sq.rearrange("p (a b) -> p a b", a=1)),
                  rd=sqk, wr=colk, tag=f"{tg}_erd", kw={"axis": AX.X})
            for ui, (hd, c, ot, ok, sq, sqk) in enumerate(tiles_):
                col, colk = BCL(48 + 2 * (ui % 8), 49 + 2 * (ui % 8))
                act(col, col, AF.Ln, rd=colk + ["EC0"], wr=colk,
                    bias=eps_col, scale=1.0 / DVH, tag=f"{tg}_ers")
            for ui, (hd, c, ot, ok, sq, sqk) in enumerate(tiles_):
                col, colk = BCL(48 + 2 * (ui % 8), 49 + 2 * (ui % 8))
                act(col, col, AF.Exp, rd=colk, wr=colk, scale=-0.5,
                    tag=f"{tg}_ers2")
            for ui, (hd, c, ot, ok, sq, sqk) in enumerate(tiles_):
                col, colk = BCL(48 + 2 * (ui % 8), 49 + 2 * (ui % 8))
                v("tensor_scalar", (ot, ot, col, None, ALU.mult),
                  rd=ok + colk, wr=ok, tag=f"{tg}_ez1")
            for (hd, c, ot, ok, sq, sqk) in tiles_:
                v("tensor_tensor", (ot, ot, NWT.all()[0], ALU.mult),
                  rd=ok + NWT.all()[1], wr=ok, tag=f"{tg}_ez2")
            for (hd, c, ot, ok, sq, sqk) in tiles_:
                ga, gk = A2(c * 1024 + hd * DVH, c * 1024 + (hd + 1) * DVH)
                v("tensor_tensor", (ot, ot, ga, ALU.mult),
                  rd=ok + gk, wr=ok, tag=f"{tg}_ez3")
        for c in range(NCHK):
            tiles, rds = [], list(k_id)
            for hd in range(H0):
                for dvp in range(2):
                    slo = hd * (NCHK * DVH) + c * DVH + dvp * 128
                    sap, sk = A3(slo, slo + 128)
                    tiles.append((PT[:, (hd * 2 + dvp) * 128:(hd * 2 + dvp) * 128 + 128], sap))
                    rds.extend(sk)
            tp(tiles, ident, rd=rds, wr=["PT"], tag=f"{tg}_ztp")
            zfk = []
            for vv in range(NCH):
                zfk += A1(vv * T + c * 128, vv * T + (c + 1) * 128)[1]
            v("tensor_copy",
              (A1.h[:].rearrange("p (vv t) -> p vv t", vv=NCH)[:, :, c * 128:(c + 1) * 128],
               PT[:].rearrange("p (vv t) -> p vv t", vv=NCH)),
              rd=["PT"], wr=zfk, tag=f"{tg}_zev")
        for tt in range(NTT):
            t0 = tt * TW
            for oc in range(NCH):
                wlo = 0 if oc < 4 else 4096
                wof = (oc % 4) * 128
                wkeys = wo0_k if oc < 4 else wo1_k
                pi = (oc * NTT + tt) % 2
                pb, pk = PSB[pi], f"PS{pi}"
                pairs = []
                rds = list(wkeys)
                for k in range(NCH):
                    wa, _ = WS(wlo + k * 512 + wof, wlo + k * 512 + wof + 128)
                    za, zk2 = A1(k * T + t0, k * T + t0 + TW)
                    pairs.append((wa, za))
                    rds.extend(zk2)
                mm(pb[:, 0:TW], pairs, rd=rds, wr=[pk],
                   tag=f"{tg}_womm")
                xa, xk = xap(oc, t0, t0 + TW)
                v("tensor_tensor", (xa, pb[:, 0:TW], xa, ALU.add),
                  rd=[pk] + xk, wr=xk, tag=f"{tg}_wores")

    def phase_l0f(b):
        tg = f"b{b}l0f"
        if b == 0:
            for rc in range(2):
              for hf in range(2):
                pka, pkk = PKF(0, 512)
                d("gpsimd", pka, pk0[rc * 128:(rc + 1) * 128, hf * 512:hf * 512 + 512],
                  "PKF", rd=[], wr=pkk, tag=f"{tg}_ldpk")
                for s in (3, 2, 1, 0):
                    base = float(4 ** s)
                    dig, digk = SB(0, 512)
                    if s > 0:
                        v("tensor_scalar", (dig, pka, base, None, ALU.is_ge),
                          rd=pkk, wr=digk, tag=f"{tg}_d1")
                        v("scalar_tensor_tensor", (dig, pka, 2 * base, dig,
                                                   ALU.is_ge, ALU.add),
                          rd=pkk + digk, wr=digk, tag=f"{tg}_d2")
                        v("scalar_tensor_tensor", (dig, pka, 3 * base, dig,
                                                   ALU.is_ge, ALU.add),
                          rd=pkk + digk, wr=digk, tag=f"{tg}_d3")
                        v("scalar_tensor_tensor", (pka, dig, -base, pka,
                                                   ALU.mult, ALU.add),
                          rd=pkk + digk, wr=pkk, tag=f"{tg}_dr")
                    else:
                        v("tensor_copy", (dig, pka), rd=pkk, wr=digk,
                          tag=f"{tg}_d0")
                    tb, tbk = SB(512, 1024)
                    v("tensor_scalar", (tb, dig, 1.0, None, ALU.subtract),
                      rd=digk, wr=tbk, tag=f"{tg}_dm1")
                    d("gpsimd", ds_tern[s * 256 + rc * 128:s * 256 + rc * 128 + 128,
                                        hf * 512:hf * 512 + 512],
                      tb, "DST", rd=tbk, wr=["DSTERN"], tag=f"{tg}_stw")
        for tt in range(NTT):
            par = tt % 2
            ln_tile(tt, par, f"{tg}t{tt}")
            t0 = tt * TW
            for c in range(NCH):
                ya, yk = A1(c * T + t0, c * T + t0 + TW)
                y_emit(c, t0, t0 + TW, ya, yk, par, f"{tg}_y")
            for jj in range(4):
                j = tt * 4 + jj
                tjl = j * 128
                ytk = []
                for k in range(NCH):
                    ytk += A1(k * T + tjl, k * T + tjl + 128)[1]
                tp([(PT[:, k * 128:k * 128 + 128],
                     A1(k * T + tjl, k * T + tjl + 128)[0]) for k in range(NCH)],
                   ident, rd=ytk + k_id, wr=["PT"], tag=f"{tg}_ytp")
                amax, amk = BCL(50, 51)
                v("reduce_max", (amax, PT[:].rearrange("p (a b) -> p a b", a=1)),
                  rd=["PT"], wr=amk, tag=f"{tg}_amax",
                  kw={"axis": AX.X, "apply_absolute_value": True})
                mx, mxk = BCL(51, 52)
                v("tensor_scalar", (mx, amax, 1e-5, None, ALU.max),
                  rd=amk, wr=mxk, tag=f"{tg}_mx")
                rcc, rck = BCL(52, 53)
                v("reciprocal", (rcc, mx), rd=mxk, wr=rck, tag=f"{tg}_rc")
                asc, asck = BCL(53, 54)
                v("tensor_scalar", (asc, rcc, 127.0, None, ALU.mult),
                  rd=rck, wr=asck, tag=f"{tg}_as")
                ra, rak = BCL(54, 55)
                v("tensor_scalar", (ra, mx, 1.0 / 127.0, bcol(40), ALU.mult,
                                    ALU.mult),
                  rd=mxk + bia_k, wr=rak, tag=f"{tg}_ra")
                cb, cbk = SB(2080, 2081)
                v("tensor_copy", (cb, asc), rd=asck, wr=cbk, tag=f"{tg}_cb")
                mm(PR[0:1, jj * 128:jj * 128 + 128], [(cb, ident)],
                   rd=cbk + k_id, wr=["PR"], tag=f"{tg}_ar")
                cb2, cb2k = SB(2081, 2082)
                v("tensor_copy", (cb2, ra), rd=rak, wr=cb2k, tag=f"{tg}_cb2")
                mm(PSB[0][0:1, jj * 128:jj * 128 + 128], [(cb2, ident)],
                   rd=cb2k + k_id, wr=["PS0"], tag=f"{tg}_rr")
            arow = SRC.h[:][64:65, 0:TW]
            ark = ["RC0", "RC1", "RC2", "RC3"]
            v("tensor_copy", (arow, PR[0:1, 0:TW]), rd=["PR"], wr=ark,
              tag=f"{tg}_arv")
            mm(PR[:, 0:TW], [(ones_row(64), arow)], rd=["O20"] + ark, wr=["PR"],
               tag=f"{tg}_abc")
            ab, abk = SB(1024, 1024 + TW)
            v("tensor_copy", (ab, PR[:, 0:TW]), rd=["PR"], wr=abk,
              tag=f"{tg}_abv")
            rrow = SRC.h[:][0:1, 0:TW]
            v("tensor_copy", (rrow, PSB[0][0:1, 0:TW]), rd=["PS0"], wr=ark,
              tag=f"{tg}_rrv")
            mm(PR[:, 0:TW], [(ones_row(0), rrow)], rd=["O20"] + ark, wr=["PR"],
               tag=f"{tg}_rbc")
            rb, rbk = ATB(tt * TW, tt * TW + TW)
            v("tensor_copy", (rb, PR[:, 0:TW]), rd=["PR"], wr=rbk,
              tag=f"{tg}_rbv")
            for c in range(NCH):
                ya, yk = A1(c * T + t0, c * T + t0 + TW)
                if c % 2 == 0:
                    vt, vtk = OST(0, TW)
                    rr, rrk = PKF(0, TW)
                    sbs, sbk = SB(0, TW)
                else:
                    vt, vtk = PSB[4][:, 0:TW], "PS4"
                    vtk = [vtk]
                    rr, rrk = PSB[5][:, 0:TW], ["PS5"]
                    sbs, sbk = SB(512, 512 + TW)
                v("tensor_tensor", (vt, ya, ab, ALU.mult),
                  rd=yk + abk, wr=vtk, tag=f"{tg}_qv")
                src, srck = vt, vtk
                for kb in (64.0, 32.0, 16.0, 8.0, 4.0, 2.0, 1.0):
                    act(sbs, src, AF.Sign, rd=srck + ["EC0"], wr=sbk,
                        bias=half_col, tag=f"{tg}_qsg")
                    v("scalar_tensor_tensor", (rr, sbs, -kb, src,
                                               ALU.mult, ALU.add),
                      rd=sbk + srck, wr=rrk, tag=f"{tg}_qst")
                    src, srck = rr, rrk
                v("tensor_scalar", (sbs, rr, -0.5, -1.0, ALU.is_ge, ALU.add),
                  rd=rrk, wr=sbk, tag=f"{tg}_qb0")
                v("scalar_tensor_tensor", (rr, sbs, -1.0, rr, ALU.mult,
                                           ALU.add),
                  rd=sbk + rrk, wr=rrk, tag=f"{tg}_qr0")
                xia, xik = A2(c * T + t0, c * T + t0 + TW)
                if c % 2 == 0:
                    v("scalar_tensor_tensor", (xia, rr, -1.0, vt, ALU.mult,
                                               ALU.add),
                      rd=rrk + vtk, wr=xik, tag=f"{tg}_qm")
                else:
                    v("copy", (sbs, rr), rd=rrk, wr=sbk, tag=f"{tg}_qrc",
                      eng="scalar")
                    v("scalar_tensor_tensor", (xia, sbs, -1.0, vt, ALU.mult,
                                               ALU.add),
                      rd=sbk + vtk, wr=xik, tag=f"{tg}_qm")
        wtk = []
        for k in range(NCH):
            wa, wk2 = WS(k * 1024, (k + 1) * 1024)
            d("sync", wa, ds_tern[:, k * 128:(k + 1) * 128], f"WT{k}",
              rd=["DSTERN"], wr=wk2, tag=f"{tg}_ldt", transpose=True)
            wtk.extend(wk2)
        rakeys = [f"AT{i}" for i in range((NTT * TW) // 1024 + 1)]
        for tt in range(NTT):
            t0 = tt * TW
            for oc in range(NCH):
                pi = (oc * NTT + tt) % 2
                pb, pk = PSB[pi], f"PS{pi}"
                pairs = []
                rds = list(wtk)
                for k in range(NCH):
                    wa, _ = WS(k * 1024 + oc * 128, k * 1024 + oc * 128 + 128)
                    xa2, xk2 = A2(k * T + t0, k * T + t0 + TW)
                    pairs.append((wa, xa2))
                    rds.extend(xk2)
                mm(pb[:, 0:TW], pairs, rd=rds, wr=[pk],
                   tag=f"{tg}_tmm")
                st, stk = SB(1536, 1536 + TW)
                rb, rbk = ATB(t0, t0 + TW)
                v("tensor_tensor", (st, pb[:, 0:TW], rb, ALU.mult),
                  rd=[pk] + rbk, wr=stk, tag=f"{tg}_fra")
                v("tensor_scalar", (st, st, bcol(32 + oc), None, ALU.add),
                  rd=stk + bia_k, wr=stk, tag=f"{tg}_fb")
                xa, xk = xap(oc, t0, t0 + TW)
                v("tensor_tensor", (xa, st, xa, ALU.add),
                  rd=stk + xk, wr=xk, tag=f"{tg}_fres")

    GRID_D = [0.25, 0.75, 1.25, 1.75, 2.5, 3.5, 5.0]

    def q4_core(fin, fink, out_ap, out_k, w, tag, avslot, slots=None):
        assert w <= 512 and w % 16 == 0
        nb16 = w // 16
        av, avk = avslot
        act(av, fin, AF.Abs, rd=fink, wr=avk, tag=f"{tag}_abs")
        if slots is None:
            accp, indp = SCR(0, w), SCR(512, 512 + w)
            scp, thrp = BCL(0, nb16), BCL(32, 32 + nb16)
        else:
            accp, indp, scp, thrp = slots
        sc, sck = scp
        v("reduce_max", (sc, av.rearrange("p (a b) -> p a b", b=16)),
          rd=avk, wr=sck, tag=f"{tag}_bm", kw={"axis": AX.X})
        v("tensor_scalar", (sc, sc, 1.0 / 6.0, 1e-12, ALU.mult, ALU.max),
          rd=sck, wr=sck, tag=f"{tag}_sc")
        av3 = av.rearrange("p (a b) -> p a b", b=16)
        acc, acck = accp
        acc3 = acc.rearrange("p (a b) -> p a b", b=16)
        thr, thrk = thrp
        ind, indk = indp
        ind3 = ind.rearrange("p (a b) -> p a b", b=16)
        g2a, g2ak = SCR(1024, 1024 + w) if slots is None else indp
        g2a3 = g2a.rearrange("p (a b) -> p a b", b=16)

        rcp, rcpk = thrp if slots is not None else QSC(32, 32 + w // 16)
        v("reciprocal", (rcp, sc), rd=sck, wr=rcpk, tag=f"{tag}_rcp")
        v("tensor_tensor", (av3, av3,
                            rcp[:, :, None].broadcast_to([128, w // 16, 16]),
                            ALU.mult),
          rd=avk + rcpk, wr=avk, tag=f"{tag}_un")

        if slots is not None:
            v("tensor_scalar", (acc, av, GRID_D[0], None, ALU.is_gt),
              rd=avk, wr=acck, tag=f"{tag}_p0")
            for t in (1, 2, 3):
                v("tensor_scalar", (g2a, av, GRID_D[t], None, ALU.is_gt),
                  rd=avk, wr=g2ak, tag=f"{tag}_p{t}")
                v("tensor_tensor", (acc, acc, g2a, ALU.add),
                  rd=acck + g2ak, wr=acck, tag=f"{tag}_pa{t}")
            v("tensor_scalar", (g2a, av, GRID_D[4], None, ALU.is_gt),
              rd=avk, wr=g2ak, tag=f"{tag}_p4")
            v("scalar_tensor_tensor", (acc, acc, 0.5, g2a, ALU.mult, ALU.add),
              rd=acck + g2ak, wr=acck, tag=f"{tag}_pc4")
            v("tensor_scalar", (g2a, av, GRID_D[5], None, ALU.is_gt),
              rd=avk, wr=g2ak, tag=f"{tag}_p5")
            v("tensor_tensor", (acc, acc, g2a, ALU.add),
              rd=acck + g2ak, wr=acck, tag=f"{tag}_pa5")
            v("tensor_scalar", (g2a, av, GRID_D[6], None, ALU.is_gt),
              rd=avk, wr=g2ak, tag=f"{tag}_p6")
            v("scalar_tensor_tensor", (acc, g2a, 2.0, acc, ALU.mult, ALU.add),
              rd=acck + g2ak, wr=acck, tag=f"{tag}_pc6")
        else:
            HW = (0.25, 0.25, 0.25, 0.25, 0.5, 0.5, 1.0)
            for t in range(7):
                st_, stk_ = (g2a, g2ak) if t % 2 == 0 else (ind, indk)
                act(st_, av, AF.Sign, rd=avk + ["EC0"], wr=stk_,
                    bias=negm_col(t), tag=f"{tag}_s{t}")
                if t == 0:
                    v("tensor_scalar", (acc, st_, HW[0], 3.0, ALU.mult,
                                        ALU.add),
                      rd=stk_, wr=acck, tag=f"{tag}_c0")
                else:
                    v("scalar_tensor_tensor", (acc, st_, HW[t], acc,
                                               ALU.mult, ALU.add),
                      rd=stk_ + acck, wr=acck, tag=f"{tag}_c{t}")
        sg, sgk = (ind, indk)
        act(sg, fin, AF.Sign, rd=fink, wr=sgk, tag=f"{tag}_sg")
        v("tensor_tensor", (acc, acc, sg, ALU.mult),
          rd=acck + sgk, wr=acck, tag=f"{tag}_ms")
        scb = sc[:, :, None].broadcast_to([128, nb16, 16])
        v("tensor_tensor", (out_ap.rearrange("p (a b) -> p a b", b=16), acc3, scb,
                            ALU.mult),
          rd=acck + sck, wr=out_k, tag=f"{tag}_msc")

    def q4_free(fin, fink, out_ap, out_k, w, tag, slots=None):
        q4_core(fin, fink, out_ap, out_k, w, tag, OST(0, w), slots=slots)

    def q4_part(fin, fink, out_ap, out_k, w, tag):
        zt, ztk = PKF(0, w)
        v("transpose", (zt, fin), rd=fink, wr=ztk, tag=f"{tag}_zi")
        zq, zqk = SCR(512, 512 + w)
        q4_core(zt, ztk, zq, zqk, w, f"{tag}z", OST(0, w))
        v("transpose", (out_ap, zq), rd=zqk, wr=out_k, tag=f"{tag}_zo")

    def phase_l1a(b, after_qt=None):
        tg = f"b{b}l1a"
        for tt in range(NTT):
            par = tt % 2
            ln_tile(tt, par, f"{tg}t{tt}")
            t0 = tt * TW
            for c in range(NCH):
                ya, yk = A1(c * T + t0, c * T + t0 + TW)
                y_emit(c, t0, t0 + TW, ya, yk, par, f"{tg}_y")

        def load_w1(wap, key_tag):
            ap, keys = WS(0, 8192)
            d("gpsimd", ap.rearrange("p (k n) -> p k n", k=NCH),
              wap.rearrange("(k p) n -> p k n", p=128), key_tag,
              rd=[], wr=keys, tag=f"ld_{key_tag}")
            return keys

        fence([], A4.all()[1] + [f"VT{kt}" for kt in range(NT128)],
              f"{tg}_fv")

        def proj(dst, dkeyf, bias_base, wkeys, tag2, to_vt=False):
            for tt in range(NTT):
                t0 = tt * TW
                for oc in range(NCH):
                    pi = 2 + (oc * NTT + tt) % 2
                    pb, pk = PSB[pi], f"PS{pi}"
                    pairs = []
                    rds = list(wkeys)
                    for k in range(NCH):
                        wa, _ = WS(k * 1024 + oc * 128, k * 1024 + oc * 128 + 128)
                        ya, yk2 = A1(k * T + t0, k * T + t0 + TW)
                        pairs.append((wa, ya))
                        rds.extend(yk2)
                    mm(pb[:, 0:TW], pairs, rd=rds, wr=[pk], tag=f"{tg}_{tag2}mm")
                    if not to_vt:
                        da, dk = dst(oc * T + t0, oc * T + t0 + TW)
                        v("tensor_scalar", (da, pb[:, 0:TW], bcol(bias_base + oc),
                                            None, ALU.add),
                          rd=[pk] + bia_k, wr=dk, tag=f"{tg}_{tag2}ev")
                    else:
                        vt, vtk = SB(1536, 1536 + TW)
                        v("tensor_scalar", (vt, pb[:, 0:TW], bcol(bias_base + oc),
                                            None, ALU.add),
                          rd=[pk] + bia_k, wr=vtk, tag=f"{tg}_{tag2}ev")
                        tp([(PT[:, j * 128:j * 128 + 128], vt[:, j * 128:j * 128 + 128])
                            for j in range(4)], ident, rd=vtk + k_id, wr=["PT"],
                           tag=f"{tg}_{tag2}tp")
                        for j in range(4):
                            kt = tt * 4 + j
                            base = (kt * 8 + oc) * 130
                            ve_ap = A4.h[:][:, base:base + 64]
                            vo_ap = A4.h[:][:, base + 65:base + 129]
                            op2("vector", "tensor_copy", (ve_ap, PT[:, j * 128:j * 128 + 64]),
                                "tensor_copy", (vo_ap, PT[:, j * 128 + 64:j * 128 + 128]),
                                ["PT"], [f"VT{kt}"], f"{tg}_{tag2}vt")

        wk1 = load_w1(w_q1, f"{tg}wq")
        proj(A2, None, 0, wk1, "q")
        wk2 = load_w1(w_k1, f"{tg}wk")
        proj(A3, None, 8, wk2, "k")
        wk3 = load_w1(w_v1, f"{tg}wv")
        proj(None, None, 16, wk3, "v", to_vt=True)
        vt_ones = A4.h[:].rearrange("p (t w) -> p t w", w=130)
        op2("vector", "memset", (vt_ones[:, 0:NT128 * 8, 64:65], 1.0),
            "memset", (vt_ones[:, 0:NT128 * 8, 129:130], 1.0),
            [], [f"VT{kt}" for kt in range(NT128)], f"{tg}_vones")
        wk4 = load_w1(w_o1, f"{tg}wo")

        def att_unit(h, qt, pz):
            ch = h // 2
            p0 = (h % 2) * 64
            q0 = qt * TW
            qb = 2 * pz
            avb, avk = (PSB[4], "PS4") if pz == 0 else (PSB[5], "PS5")
            for g in range(NT128 // 2):
                kc0, kc1 = 2 * g, 2 * g + 1
                qa = A2.h[:][p0:p0 + 64, ch * T + q0:ch * T + q0 + TW]
                qak = A2(ch * T + q0, ch * T + q0 + TW)[1]
                ka0 = A3.h[:][p0:p0 + 64,
                              ch * T + kc0 * 128:ch * T + kc0 * 128 + 128]
                ka1 = A3.h[:][p0:p0 + 64,
                              ch * T + kc1 * 128:ch * T + kc1 * 128 + 128]
                kak = A3(ch * T + kc0 * 128, ch * T + (kc1 + 1) * 128)[1]
                mm2(PSB[qb][:, 0:TW], (ka0, qa),
                    PSB[qb + 1][:, 0:TW], (ka1, qa),
                    kak + qak, [f"PS{qb}", f"PS{qb + 1}"], f"{tg}_lmm")
                yield
                at_ap, at_k = ATB(pz * 1024, pz * 1024 + 1024)
                act(at_ap, PSQ[:][:, qb * 512:qb * 512 + 1024], AF.Exp,
                    rd=[f"PS{qb}", f"PS{qb + 1}"], wr=at_k,
                    scale=0.125, tag=f"{tg}_exp")
                yield
                base0 = (kc0 * 8 + ch) * 130 + (0 if h % 2 == 0 else 65)
                base1 = (kc1 * 8 + ch) * 130 + (0 if h % 2 == 0 else 65)
                av_pairs = [
                    (A4.h[:][:, base0:base0 + 65],
                     ATB.h[:][:, pz * 1024:pz * 1024 + TW]),
                    (A4.h[:][:, base1:base1 + 65],
                     ATB.h[:][:, pz * 1024 + 512:pz * 1024 + 512 + TW])]
                mm(avb[0:65, 0:TW], av_pairs,
                   rd=[f"VT{kc0}", f"VT{kc1}"] + at_k, wr=[avk],
                   accumulate=(g > 0), stop=(g == NT128 // 2 - 1),
                   skip_group_check=True, tag=f"{tg}_avmm")
                yield
            f1 = (SRC.h[:][64:65, 0:TW] if pz == 0 else SRB.h[:][64:65, 0:TW])
            f1k = ["RC0", "RC1", "RC2", "RC3"] if pz == 0 else ["RB0"]
            act(f1, avb[64:65, 0:TW], AF.Ln, rd=[avk], wr=f1k,
                tag=f"{tg}_rc1")
            yield
            act(f1, f1, AF.Exp, rd=f1k, wr=f1k, scale=-1.0, tag=f"{tg}_rc2")
            yield
            fb, fbk = SB(1536, 1536 + TW) if pz == 0 else SB(1024, 1024 + TW)
            mm(PSB[qb + 1][:, 0:TW], [(ON2.h[:][64:65, :], f1)],
               rd=["O20"] + f1k, wr=[f"PS{qb + 1}"], tag=f"{tg}_fbc")
            v("tensor_copy", (fb, PSB[qb + 1][:, 0:TW]), rd=[f"PS{qb + 1}"],
              wr=fbk, tag=f"{tg}_fbe")
            yield
            dst = A2.h[:][p0:p0 + 64, ch * T + q0:ch * T + q0 + TW]
            if h % 2 == 0:
                v("tensor_tensor", (dst, avb[0:64, 0:TW], fb[0:64, :],
                                    ALU.mult),
                  rd=[avk] + fbk, wr=A2(ch * T + q0, ch * T + q0 + TW)[1],
                  tag=f"{tg}_poev")
                yield
            else:
                mv, mvk = SB(0, TW) if pz == 0 else SB(512, 512 + TW)
                v("tensor_tensor", (mv[0:64, :], avb[0:64, 0:TW],
                                    fb[0:64, :], ALU.mult),
                  rd=[avk] + fbk, wr=mvk, tag=f"{tg}_ponm")
                yield
                mm(PSB[qb][64:128, 0:TW], [(ident[0:64, 0:64], mv[0:64, :])],
                   rd=mvk + k_id, wr=[f"PS{qb}"], tag=f"{tg}_pomv")
                v("tensor_copy", (dst, PSB[qb][64:128, 0:TW]),
                  rd=[f"PS{qb}"], wr=A2(ch * T + q0, ch * T + q0 + TW)[1],
                  tag=f"{tg}_poev")
                yield

        def emit_wo(tt):
            t0 = tt * TW
            for oc in range(NCH):
                pairs = []
                rds = list(wk4)
                for k in range(NCH):
                    wa, _ = WS(k * 1024 + oc * 128, k * 1024 + oc * 128 + 128)
                    pa2, pk2 = A2(k * T + t0, k * T + t0 + TW)
                    pairs.append((wa, pa2))
                    rds.extend(pk2)
                mm(PR[:, 0:TW], pairs, rd=rds, wr=["PR"],
                   tag=f"{tg}_womm")
                st, stk = SB(2048, 2048 + TW)
                v("tensor_scalar", (st, PR[:, 0:TW], bcol(24 + oc), None,
                                    ALU.add),
                  rd=["PR"] + bia_k, wr=stk, tag=f"{tg}_wob")
                xa, xk = xap(oc, t0, t0 + TW)
                v("tensor_tensor", (xa, st, xa, ALU.add),
                  rd=stk + xk, wr=xk, tag=f"{tg}_wores")

        units = [(h, qt) for qt in range(NTT) for h in range(H1)]
        ui = 0
        while ui < len(units):
            pump_cb(2)
            pair = units[ui:ui + 2]
            gens = [att_unit(hh, qq, pz) for pz, (hh, qq) in enumerate(pair)]
            while gens:
                for g_ in list(gens):
                    try:
                        next(g_)
                    except StopIteration:
                        gens.remove(g_)
            ui += 2
            if ui % H1 == 0:
                qt_done = units[ui - 1][1]
                emit_wo(qt_done)
                if after_qt is not None:
                    after_qt(qt_done)

    def seg_table():
        segs = []
        pools = [(XB, 0, NCH * T), (A1, 0, NCH * T), (A4, 0, A4C),
                 (A3, ((NKI * TW + T - 1) // T) * T, NCH * T)]
        pi = 0
        off = pools[0][1]
        for i in range(16):
            while off + I_FFN > pools[pi][2]:
                pi += 1
                off = pools[pi][1]
            segs.append((pools[pi][0], off))
            off += I_FFN
        return segs

    SEGS = seg_table() if "l1f" in phases else None

    def l1f_tile(b, tt):
        tg = f"b{b}l1f"
        par = tt % 2
        ln_tile(tt, par, f"{tg}t{tt}")
        t0 = tt * TW
        for c in range(NCH):
            ya, yk = SRF(0, TW)
            y_emit(c, t0, t0 + TW, ya, yk, par, f"{tg}_y")
            qa, qk = A2(c * T + t0, c * T + t0 + TW)
            q4_part(ya, yk, qa, qk, TW, f"{tg}_qy")

    def phase_l1f(b, tiles_emitted=False):
        tg = f"b{b}l1f"
        if not tiles_emitted:
            for tt in range(NTT):
                l1f_tile(b, tt)
        d("gpsimd", ds_x.rearrange("(c p) t -> p c t", p=128),
          XB.h[:].rearrange("p (c t) -> p c t", c=NCH),
          "DSX", rd=XB.all()[1], wr=["DSXM"], tag=f"{tg}_spx")
        fence([], A4.all()[1] + [f"VT{kt}" for kt in range(NT128)],
              f"{tg}_fs")
        segk = {}
        for kc in range(16):
            buf, lo = SEGS[kc]
            ap, ukeys = buf(lo, lo + I_FFN)
            dsrc = ds_wg if kc < 8 else ds_wu
            kk = kc % 8
            d("sync", ap, dsrc[:, kk * 128:(kk + 1) * 128], f"WG{kc}",
              rd=["DSGM" if kc < 8 else "DSUM"], wr=ukeys, tag=f"{tg}_ldg",
              transpose=True)
            segk[kc] = ukeys

        def wg_tile(kc, n):
            buf, lo = SEGS[kc]
            return buf.h[:][:, lo + n * 128:lo + n * 128 + 128], segk[kc]

        def wu_tile(kc, n):
            buf, lo = SEGS[8 + kc]
            return buf.h[:][:, lo + n * 128:lo + n * 128 + 128], segk[8 + kc]

        for tq in range(NTT):
            t0 = tq * TW
            for n in range(NKI):
                gp, gk2 = [], []
                up_, uk2 = [], []
                yqkt = []
                for kc in range(NCH):
                    wt, wk_ = wg_tile(kc, n)
                    yq_ap, yq_k = A2(kc * T + t0, kc * T + t0 + TW)
                    gp.append((wt, yq_ap)); gk2.extend(wk_)
                    yqkt.extend(yq_k)
                    wt2, wk2_ = wu_tile(kc, n)
                    up_.append((wt2, yq_ap)); uk2.extend(wk2_)
                mm(PSB[0][:, 0:TW], gp, rd=gk2 + yqkt, wr=["PS0"], tag=f"{tg}_gmm")
                mm(PSB[1][:, 0:TW], up_, rd=uk2 + yqkt, wr=["PS1"], tag=f"{tg}_umm")
                sgt, sgtk = SRF(0, TW)
                act(sgt, PSB[0][:, 0:TW], AF.Silu, rd=["PS0"], wr=sgtk,
                    tag=f"{tg}_sil")
                v("tensor_tensor", (sgt, sgt, PSB[1][:, 0:TW], ALU.mult),
                  rd=sgtk + ["PS1"], wr=sgtk, tag=f"{tg}_h")
                ha, hk = A3(n * TW, (n + 1) * TW)
                q4_part(sgt, sgtk, ha, hk, TW, f"{tg}_qh")
            for hv in range(2):
                for kcp in range(NKI):
                    wda, wdk = WS((kcp % 2) * 1024, (kcp % 2) * 1024 + 1024)
                    d("sync", wda, ds_wd[:, kcp * 128:(kcp + 1) * 128],
                      f"WD{kcp % 2}", rd=["DSDM"], wr=wdk, tag=f"{tg}_ldd",
                      transpose=True)
                    hq_ap = A3(kcp * TW, (kcp + 1) * TW)[0]
                    hq_k = A3(kcp * TW, (kcp + 1) * TW)[1]
                    for t2 in range(2):
                        ts4 = hv * 2 + t2
                        lhs = hq_ap[:, ts4 * 128:ts4 * 128 + 128]
                        for odh in range(2):
                            ai = t2 * 2 + odh
                            mm(PSB[ai][:, 0:TW],
                               [(lhs, wda[:, odh * 512:odh * 512 + 512])],
                               rd=hq_k + wdk, wr=[f"PS{ai}"],
                               accumulate=(kcp > 0), stop=(kcp == NKI - 1),
                               skip_group_check=True, tag=f"{tg}_dmm")
                for t2 in range(2):
                    ts4 = hv * 2 + t2
                    xr, xrk = SB(0, 1024)
                    d("sync", xr, ds_x[:, t0 + ts4 * 128:t0 + ts4 * 128 + 128],
                      f"XR{ts4 % 2}", rd=["DSXM"], wr=xrk, tag=f"{tg}_ldxr",
                      transpose=True)
                    for odh in range(2):
                        ai = t2 * 2 + odh
                        stg = OST.h[:][:, 0:512]
                        stgk = ["OS0", "OS1"]
                        v("tensor_tensor", (stg, PSB[ai][:, 0:TW],
                                            xr[:, odh * 512:odh * 512 + 512],
                                            ALU.add),
                          rd=[f"PS{ai}"] + xrk,
                          wr=stgk, tag=f"{tg}_oadd")
                        d("sync", out[b, t0 + ts4 * 128:t0 + ts4 * 128 + 128,
                                      odh * 512:odh * 512 + 512],
                          stg, "OUT", rd=stgk, wr=["OUTM"], tag=f"{tg}_st")

    prep_jobs = []

    def make_prep_jobs():
        tg = "prep"
        for (wap, dsap, nrows, dkey) in ((w_gt, ds_wg, I_FFN, "DSG"),
                                         (w_up, ds_wu, I_FFN, "DSU"),
                                         (w_dn, ds_wd, D, "DSD")):
            wcols = 1024 if nrows == I_FFN else I_FFN
            for rc in range(nrows // 128):
                for lo in range(0, wcols, 512):
                    w_ = min(512, wcols - lo)

                    def job(wap=wap, dsap=dsap, dkey=dkey, rc=rc, lo=lo, w_=w_):
                        pka, pkk = PKF(0, w_)
                        d("gpsimd", pka,
                          wap[rc * 128:(rc + 1) * 128, lo:lo + w_],
                          "PKF", rd=[], wr=pkk, tag=f"{tg}_ldw")
                        outw, outwk = SCR(1024, 1024 + w_)
                        q4_free(pka, pkk, outw, outwk, w_, f"{tg}_wq",
                                slots=(SCR(0, w_), SCR(512, 512 + w_),
                                       QSC(0, w_ // 16), QSC(32, 32 + w_ // 16)))
                        d("gpsimd",
                          dsap[rc * 128:(rc + 1) * 128, lo:lo + w_],
                          outw, dkey, rd=outwk, wr=[f"{dkey}M"], tag=f"{tg}_stw")
                    prep_jobs.append(job)

    def prep_pump(n):
        for _ in range(min(n, len(prep_jobs))):
            prep_jobs.pop(0)()

    def prep_flush():
        prep_pump(len(prep_jobs))

    def phase_store(b):
        tg = f"b{b}st"
        for c in range(NCH):
            for tt in range(NTT):
                t0 = tt * TW
                stg = OST.h[:][:, 0:512]
                stgk = ["OS0", "OS1"]
                xa, xk = xap(c, t0, t0 + TW)
                v("tensor_copy", (stg, xa), rd=xk, wr=stgk, tag=f"{tg}_cp")
                d("sync", out[b, c * 128:(c + 1) * 128, t0:t0 + TW], stg, "OUT",
                  rd=stgk, wr=["OUTM"], tag=f"{tg}_st")

    if "l1f" in phases:
        make_prep_jobs()
        pump_holder[0] = prep_pump
        prep_pump(6)
    for b in range(nb):
        xk_all = XB.all()[1]
        d("gpsimd", XB.h[:].rearrange("p (c t) -> p c t", c=NCH),
          xin[b].rearrange("(c p) t -> p c t", p=128), "XIN",
          rd=xk_all, wr=xk_all, tag=f"b{b}_ldx")
        if "l0a" in phases:
            phase_l0a(b)
        if "l0f" in phases:
            phase_l0f(b)
        if "l1a" in phases:
            if "l1f" in phases:
                prep_flush()
                phase_l1a(b, after_qt=lambda qt, b=b: l1f_tile(b, qt))
            else:
                phase_l1a(b)
        if "l1f" in phases:
            if "l1a" not in phases:
                prep_flush()
                phase_l1f(b)
            else:
                phase_l1f(b, tiles_emitted=True)
        else:
            phase_store(b)

    S.op("sync", None, reads=["OUTM"], writes=[], tag="final_wait")

    stats = S.solve()
    S.emit(nc, ctx)
    ctx.close()
    return nc, stats


def _prep_inputs(inputs, T=2048, nb_per_core=2):
    f = np.ascontiguousarray
    x = inputs["L0_attn_hidden_states"]
    B = x.shape[0]
    ncores = B // nb_per_core
    xt = np.transpose(x, (0, 2, 1))
    common = {
        "cw": f(inputs["L0_attn_conv_w"].T.reshape(4, 8, 128).transpose(2, 1, 0)
                .reshape(128, 32)),
        "wq0": f(inputs["L0_attn_Wq"]),
        "wk0": f(inputs["L0_attn_Wk"]),
        "wv0": f(inputs["L0_attn_Wv"]),
        "wb0": f(inputs["L0_attn_Wb"]),
        "wg0": f(inputs["L0_attn_Wg"]),
        "wo0": f(inputs["L0_attn_Wo"]),
        "nw_t": f(np.broadcast_to(inputs["L0_attn_norm_w"], (128, 256))),
        "pk0": f(inputs["L0_ffn_packed_weight"].reshape(256, 1024)),
        "wsc0": f(inputs["L0_ffn_weight_scale"].reshape(1, 1)),
        "c_wsc_col": f(np.broadcast_to(inputs["L0_ffn_weight_scale"]
                                       .reshape(1, 1), (128, 1))),
        "bias0": f(inputs["L0_ffn_bias"].reshape(8, 128).T),
        "wq1": f(inputs["L1_attn_wq_w"].T),
        "bq1": f(inputs["L1_attn_wq_b"].reshape(8, 128).T),
        "wk1": f(inputs["L1_attn_wk_w"].T),
        "bk1": f(inputs["L1_attn_wk_b"].reshape(8, 128).T),
        "wv1": f(inputs["L1_attn_wv_w"].T),
        "bv1": f(inputs["L1_attn_wv_b"].reshape(8, 128).T),
        "wo1": f(inputs["L1_attn_wo_w"].T),
        "bo1": f(inputs["L1_attn_wo_b"].reshape(8, 128).T),
        "wgt": f(inputs["L1_ffn_w_gate"]),
        "wup": f(inputs["L1_ffn_w_up"]),
        "wdn": f(inputs["L1_ffn_w_down"]),
    }
    import ml_dtypes
    bf = ml_dtypes.bfloat16
    consts = {
        "c_id": np.eye(128, dtype=bf),
        "c_mneg_lo": np.where(np.arange(128)[None, :] < np.arange(128)[:, None],
                              -1.0, 0.0).astype(bf),
        "c_mneg_up": np.where(np.arange(128)[None, :] > np.arange(128)[:, None],
                              -1.0, 0.0).astype(bf),
        "c_m_upincl": np.where(np.arange(128)[None, :] >= np.arange(128)[:, None],
                               1.0, 0.0).astype(bf),
        "c_ones": np.ones((128, 1), dtype=bf),
        "c_ones_row": np.ones((1, 128), dtype=bf),
        "c_ones_mat": np.ones((128, 128), dtype=bf),
    }
    in_maps = []
    for core in range(ncores):
        m = {"xin": f(xt[core * nb_per_core:(core + 1) * nb_per_core])}
        m.update(common)
        m.update(consts)
        in_maps.append(m)
    return in_maps, ncores


def kernel(**inputs):
    T = inputs["L0_attn_hidden_states"].shape[1]
    in_maps, ncores = _prep_inputs(inputs, T=T)
    nc, stats = build(T=T, nb=2)
    res = run_bass_kernel_spmd(nc, in_maps, core_ids=list(range(ncores)))
    outs = [np.asarray(r["out"]) for r in res.results]
    full = np.concatenate(outs, axis=0)
    return np.ascontiguousarray(full.astype(np.float32))
